# version 15
# speedup vs baseline: 3.8480x; 1.0038x over previous
import numpy as np
import ml_dtypes
from contextlib import ExitStack

import concourse.bass as bass
import concourse.mybir as mybir
from concourse.bass_utils import run_bass_kernel_spmd

F32 = mybir.dt.float32
BF16 = mybir.dt.bfloat16
AF = mybir.ActivationFunctionType
ALU = mybir.AluOpType
AX = mybir.AxisListType

B, S, H = 2, 8192, 1024
NKH, NVH, DK, DV = 8, 16, 64, 64
VAL_DIM = NVH * DV
CONV_DIM = 2 * NKH * DK + VAL_DIM
KSZ = 4
SEG, CKL = 2048, 128
NCK = SEG // CKL
NTT = SEG // 128
HLO = KSZ - 1
SEGH = SEG + HLO
WIN = 2
RMS_EPS, LN_EPS, NEG_BIG = 1e-6, 1e-5, -1e9

BNC, BCB, BB, BNBG, BKAP, BR8, BG8, BC = (i * 16 for i in range(8))
PT_OFF = {"negc": 0, "cb": 16, "beta": 32, "nbg": 48, "kap": 64, "r8": 80,
          "g8": 96}

DEBUG_TAPS = {}


def svh(vh):
    kh = vh // 2
    r0 = 64 * (kh % 2)
    cb = ((kh // 2) * 2 + (vh % 2)) * 64
    return r0, cb


class Sched:
    ENGINES = ("sync", "tensor", "vector", "scalar", "gpsimd")

    def __init__(self, counts):
        self.ops = {e: [] for e in self.ENGINES}
        self.counts = counts
        for k in ("sd", "sd2"):
            self.counts.setdefault("_mw_" + k, 0)
            self.counts.setdefault("_bar_" + k, 0)

    def cnt(self, sem):
        return self.counts.get(sem, 0)

    def emit(self, eng, fn, wait=(), inc=()):
        for sname, v in wait:
            if sname in ("sd", "sd2"):
                self.counts["_mw_" + sname] = max(
                    self.counts.get("_mw_" + sname, 0), v)
        for sname, n in inc:
            self.counts[sname] = self.counts.get(sname, 0) + n
        self.ops[eng].append((fn, tuple(wait), tuple(inc)))

    def dma(self, eng, out, in_, sem="sd", wait=()):
        wait = list(wait)
        pre = self.counts.get(sem, 0)
        if self.counts.get("_mw_" + sem, 0) > self.counts.get("_bar_" + sem, 0):
            wait.append((sem, pre))
            self.counts["_bar_" + sem] = pre

        def fn(e, out=out, in_=in_):
            return e.dma_start(out=out, in_=in_)
        self.emit(eng, fn, wait=wait, inc=((sem, 16),))

    def realize(self, nc, sems, name=None):
        import contextlib
        scope = nc.named_scope(name) if name else contextlib.nullcontext()
        with scope, nc.Block(name) as blk:
            for eng in self.ENGINES:
                if not self.ops[eng]:
                    continue
                lst = self.ops[eng]

                def body(e, lst=lst):
                    for fn, waits, incs in lst:
                        for sname, v in waits:
                            if v > 0:
                                e.wait_ge(sems[sname], v)
                        inst = fn(e) if fn is not None else None
                        for sname, n in incs:
                            inst.then_inc(sems[sname], n)
                getattr(blk, eng)(body)


class Builder:
    def __init__(self):
        self.nc = bass.Bass("TRN2", target_bir_lowering=False,
                            detect_race_conditions=False)
        self.es = ExitStack()
        self.C = {}
        nc = self.nc
        self.sems = {n: self.es.enter_context(nc.semaphore(n))
                     for n in ("sd", "sd2", "st", "sv", "ss", "scc", "sg")}
        dram = nc.dram_tensor
        self.d_hs = dram("hs_pad", [SEGH, H], F32, kind="ExternalInput")
        self.d_wqkv = dram("wqkvT", [H, CONV_DIM], F32, kind="ExternalInput")
        self.d_convw = dram("convw_p", [128, 64], F32, kind="ExternalInput")
        self.d_wz = dram("wzT", [H, VAL_DIM], F32, kind="ExternalInput")
        self.d_wab = dram("wabT", [H, 32], F32, kind="ExternalInput")
        self.d_wout = dram("woutT", [VAL_DIM, H], F32, kind="ExternalInput")
        self.d_wffn = dram("wffnT", [H, H], F32, kind="ExternalInput")
        self.d_dtb = dram("dtb", [16], F32, kind="ExternalInput")
        self.d_alog = dram("alog", [16], F32, kind="ExternalInput")
        self.d_nwrep = dram("nw_rep", [128], F32, kind="ExternalInput")
        self.d_msel = dram("msel", [128, 32], F32, kind="ExternalInput")
        self.d_mc = dram("mc", [128, 4], F32, kind="ExternalInput")
        self.d_mcn = dram("mcn", [128, 4], F32, kind="ExternalInput")
        self.d_ident = dram("ident_bf", [128, 128], BF16, kind="ExternalInput")
        self.d_identB = dram("identB_bf", [128, 128], BF16, kind="ExternalInput")
        self.d_identF = dram("identF", [128, 128], F32, kind="ExternalInput")
        self.d_maskL = dram("maskL", [128, 128], BF16, kind="ExternalInput")
        self.d_maskU = dram("maskU", [128, 128], BF16, kind="ExternalInput")
        self.d_maskDW = dram("maskDW", [128, WIN * CKL], F32, kind="ExternalInput")
        self.d_ones16 = dram("ones16", [16, CKL], F32, kind="ExternalInput")
        self.d_out = dram("out_x2", [SEG, H], F32, kind="ExternalOutput")
        self.d_send = dram("s_end", [128, 512], F32, kind="ExternalOutput")
        self.d_cb = dram("cb_scr", [16, SEG], F32)
        self.d_gc = dram("gc_scr", [NCK * 16], F32)
        self.d_tst = dram("tst_scr", [128, NCK * NVH * 128], BF16)
        self.d_pst = dram("pst_scr", [128, NCK * NVH * 128], BF16)
        self.d_ktk = dram("ktk_scr", [128, NCK * NVH * 64], BF16)
        self.d_vtb = dram("vtb_scr", [128, NCK * NVH * 64], BF16)
        self.d_agin = dram("ag_in", [128, 512], F32)
        self.d_agout = dram("ag_out", [8 * 128, 512], F32)
        self.taps = {name: dram("tap_" + name, list(shape), F32,
                                kind="ExternalOutput")
                     for name, shape in DEBUG_TAPS.items()}

        sb, ps = self.sbuf, self.psumt
        self.ident = sb("ident", [128, 128], BF16)
        self.identB = sb("identB", [128, 128], BF16)
        self.identF = sb("identF", [128, 128])
        self.maskL = sb("maskL", [128, 128], BF16)
        self.maskU = sb("maskU", [128, 128], BF16)
        self.maskDW = sb("maskDW", [128, WIN * CKL])
        self.ones16 = sb("ones16", [16, CKL])
        self.dtb = sb("dtb", [16, 1])
        self.alog = sb("alog", [16, 1])
        self.negea = sb("negea", [16, 1])
        self.nwrep = sb("nwrep", [128, 1])
        self.convw = sb("convw", [128, 64])
        self.paramT = sb("paramT", [128, NCK * 112])
        self.gcrep = sb("gcrep", [128, NCK * 16])
        self.S_f = sb("S_f", [128, 512])
        self.S_b = sb("S_b", [128, 512], BF16)
        self.S_send = sb("S_send", [128, 512])
        self.msel = sb("msel", [128, 32])
        self.mc = sb("mc", [128, 4])
        self.mcn = sb("mcn", [128, 4])
        self.agbuf = sb("agbuf", [128, 512])
        self.zs = [sb(f"zs{i}", [128, VAL_DIM], BF16) for i in range(NTT)]
        self.ob = [sb(f"ob{i}", [128, VAL_DIM], BF16) for i in range(NTT)]
        self.pb = [ps(f"pb{i}", [128, 512], F32) for i in range(6)]
        self.tbk = [ps(f"tb{i}", [128, 512], F32) for i in range(2)]

    def sbuf(self, n, shape, dt=F32, es=None):
        return (es or self.es).enter_context(
            self.nc.sbuf_tensor("sb_" + n, shape, dt))

    def psumt(self, n, shape, dt=F32, es=None):
        return (es or self.es).enter_context(
            self.nc.psum_tensor("ps_" + n, shape, dt))

    def ptcol(self, ck, band, vh):
        c = 112 * ck + PT_OFF[band] + vh
        return self.paramT[:, c:c + 1]

    def build(self):
        es_y = ExitStack()
        self.yb = [self.sbuf(f"yb{i}", [128, SEG], BF16, es_y) for i in range(16)]
        with ExitStack() as es_fm:
            hsFM_flat = self.sbuf("hsFM", [128, 8 * SEGH], BF16, es_fm)
            self.hsFM = [hsFM_flat[:, i * SEGH:(i + 1) * SEGH] for i in range(8)]
            stage = self.sbuf("stage", [128, 1056], F32, es_fm)
            with ExitStack() as es_tm:
                hsTM = [self.sbuf(f"hsTM{i}", [128, H], F32, es_tm)
                        for i in range(2)]
                self.phase0(stage, hsTM)
            with ExitStack() as es_mx:
                wq = [self.sbuf("wq0", [128, 8 * 128], BF16, es_mx)]
                mixr = [self.sbuf("mixr0", [128, SEGH], BF16, es_mx)]
                yacc = self.sbuf("yacc", [128, 1024], F32, es_mx)
                yacc2 = self.sbuf("yacc2", [128, 1024], F32, es_mx)
                self.phase1_mixed(stage, wq, mixr, yacc, yacc2)
            with ExitStack() as es_z:
                wz = self.sbuf("wz", [128, 8 * 512], BF16, es_z)
                wab = self.sbuf("wab", [128, 8 * 32], BF16, es_z)
                sA = self.sbuf("zsA", [16, 512], F32, es_z)
                sB = self.sbuf("zsB", [16, 512], F32, es_z)
                sC = self.sbuf("zsC", [16, 512], F32, es_z)
                sD = self.sbuf("zsD", [16, 512], F32, es_z)
                sE = self.sbuf("zsE", [16, CKL], F32, es_z)
                sE2 = self.sbuf("zsE2", [16, CKL], F32, es_z)
                self.phase1_z_ab(stage, wz, wab, sA, sB, sC, sD, sE, sE2)
        with ExitStack() as es_scan:
            t = {}
            t["cbb"] = [self.sbuf(f"cbb{i}", [128, WIN * CKL], F32, es_scan)
                        for i in range(2)]
            t["dw"] = [self.sbuf(f"dw{i}", [128, WIN * CKL], F32, es_scan)
                       for i in range(2)]
            t["ep"] = [self.sbuf(f"ep{i}", [128, CKL], F32, es_scan)
                       for i in range(2)]
            t["chS"] = [self.sbuf(f"chS{i}", [128, 128], BF16, es_scan)
                        for i in range(4)]
            t["chN"] = [self.sbuf(f"chN{i}", [128, 128], BF16, es_scan)
                        for i in range(4)]
            t["chH"] = [self.sbuf(f"chH{i}", [128, 128], BF16, es_scan)
                        for i in range(4)]
            t["aqs"] = self.sbuf("aqs", [128, 128], BF16, es_scan)
            t["g1s"] = self.sbuf("g1s", [128, 128], BF16, es_scan)
            t["tst"] = self.sbuf("tst", [128, WIN * NVH * 128], BF16, es_scan)
            t["pst"] = self.sbuf("pst", [128, WIN * NVH * 128], BF16, es_scan)
            t["ktk"] = self.sbuf("ktk", [128, WIN * NVH * 64], BF16, es_scan)
            t["vtb"] = self.sbuf("vtb", [128, WIN * NVH * 64], BF16, es_scan)
            t["rt"] = [self.sbuf(f"rt{i}", [128, 64], BF16, es_scan)
                       for i in range(4)]
            t["wt"] = [self.sbuf(f"wt{i}", [128, 64], BF16, es_scan)
                       for i in range(4)]
            t["ot"] = [self.sbuf(f"ot{i}", [128, 64], F32, es_scan)
                       for i in range(4)]
            self.phase2_scan(t)
        es_y.close()
        with ExitStack() as es_post:
            p = {}
            p["wout"] = self.sbuf("wout", [128, 8 * H], BF16, es_post)
            p["wffn"] = self.sbuf("wffn", [128, 8 * H], BF16, es_post)
            p["stage2"] = self.sbuf("stage2", [128, H], F32, es_post)
            p["x1"] = [self.sbuf(f"x1_{i}", [128, H], F32, es_post)
                       for i in range(NTT)]
            p["htmp"] = self.sbuf("htmp", [128, H], F32, es_post)
            p["hsq"] = self.sbuf("hsq", [128, H], F32, es_post)
            p["hnb"] = self.sbuf("hnb", [128, H], BF16, es_post)
            p["hnF"] = self.sbuf("hnF", [128, 8 * 128], BF16, es_post)
            p["hsr"] = [self.sbuf(f"hsr{i}", [128, H], F32, es_post)
                        for i in range(2)]
            p["sm"] = [self.sbuf(f"sm{i}", [128, 1], F32, es_post)
                       for i in range(4)]
            p["rr"] = self.sbuf("rr", [128, 16], F32, es_post)
            self.phase3_post(p)
        self.es.close()
        return self.nc

    def phase0(self, stage, hsTM):
        sc, C = Sched(self.C), self.C
        for dst, src in [(self.ident, self.d_ident), (self.identB, self.d_identB),
                         (self.identF, self.d_identF),
                         (self.maskL, self.d_maskL), (self.maskU, self.d_maskU),
                         (self.maskDW, self.d_maskDW), (self.ones16, self.d_ones16),
                         (self.convw, self.d_convw), (self.msel, self.d_msel),
                         (self.mc, self.d_mc), (self.mcn, self.d_mcn)]:
            sc.dma("sync", dst[:, :], src.ap())
        sc.dma("sync", self.dtb[:, 0], self.d_dtb.ap())
        sc.dma("sync", self.alog[:, 0], self.d_alog.ap())
        sc.dma("sync", self.nwrep[:, 0], self.d_nwrep.ap())
        cdone = sc.cnt("sd")
        sc.emit("scalar", lambda e: e.activation(self.negea[:, :], self.alog[:, :],
                                                 AF.Exp),
                wait=[("sd", cdone)], inc=[("ss", 1)])
        sc.emit("vector", lambda e: e.tensor_scalar_mul(self.negea[:, :],
                                                        self.negea[:, :], -1.0),
                wait=[("ss", sc.cnt("ss"))], inc=[("sv", 1)])
        sc.emit("vector", lambda e: e.memset(self.S_send[:, :], 0.0),
                inc=[("sv", 1)])
        for i in range(NTT):
            sc.emit("vector", (lambda e, i=i: e.memset(self.ob[i][:, :], 0.0)),
                    inc=[("sv", 1)])

        sc.dma("sync", hsTM[0][0:3, :], self.d_hs.ap()[0:3, :])
        halo_sd = sc.cnt("sd")
        for f in range(8):
            pt = self.tbk[f % 2]
            w = [("sd", halo_sd), ("sd", cdone)] if f < 2 else \
                [("sv", C[f"_hfe{f - 2}"])]
            sc.emit("tensor",
                    (lambda e, f=f, pt=pt: e.transpose(
                        pt[:, 0:3], hsTM[0][0:3, f * 128:(f + 1) * 128],
                        self.identF[0:3, 0:3])),
                    wait=w, inc=[("st", 1)])
            sc.emit("vector",
                    (lambda e, f=f, pt=pt: e.tensor_copy(
                        self.hsFM[f][:, 0:3], pt[:, 0:3])),
                    wait=[("st", sc.cnt("st"))], inc=[("sv", 1)])
            C[f"_hfe{f}"] = sc.cnt("sv")
        halo_st = sc.cnt("st")
        for t in range(NTT):
            buf = hsTM[t % 2]
            w = [("st", C[f"_hst{t - 2}"])] if t >= 2 else \
                ([("st", halo_st)] if t == 0 else ())
            sc.dma("sync", buf[:, :],
                   self.d_hs.ap()[HLO + t * 128: HLO + (t + 1) * 128, :], wait=w)
            sd_here = sc.cnt("sd")
            for f in range(8):
                pt = self.tbk[(t * 8 + f) % 2]
                sc.emit("tensor",
                        (lambda e, f=f, buf=buf, pt=pt: e.transpose(
                            pt[:, 0:128], buf[:, f * 128:(f + 1) * 128],
                            self.identF[:, :])),
                        wait=[("sd", sd_here), ("sv", sc.cnt("sv") - 1)],
                        inc=[("st", 1)])
                sc.emit("vector",
                        (lambda e, f=f, t=t, pt=pt: e.tensor_copy(
                            self.hsFM[f][:, HLO + t * 128: HLO + (t + 1) * 128],
                            pt[:, 0:128])),
                        wait=[("st", sc.cnt("st"))], inc=[("sv", 1)])
            C[f"_hst{t}"] = sc.cnt("st")
        sc.realize(self.nc, self.sems, "p0")

    def phase1_mixed(self, stage, wq, mixr, yacc, yacc2):
        sc, C = Sched(self.C), self.C
        NT = [(0, 512), (512, 512), (1024, 512), (1536, 512), (2048, HLO)]
        for oc in range(16):
            wbuf = wq[0]
            for k in range(8):
                w = []
                if k == 0:
                    w = [("ss", C.get(f"_wqc{oc - 1}", 0)),
                         ("st", C.get(f"_mmq{oc - 1}", 0))]
                sc.dma("sync", stage[:, k * 128:(k + 1) * 128],
                       self.d_wqkv.ap()[k * 128:(k + 1) * 128,
                                        oc * 128:(oc + 1) * 128], wait=w)
                sc.emit("scalar",
                        (lambda e, k=k, wbuf=wbuf: e.copy(
                            wbuf[:, k * 128:(k + 1) * 128],
                            stage[:, k * 128:(k + 1) * 128])),
                        wait=([("sd", sc.cnt("sd"))] +
                              ([("st", C.get(f"_mmq{oc - 1}", 0))] if k == 0 else [])),
                        inc=[("ss", 1)])
            C[f"_wqc{oc}"] = sc.cnt("ss")
            wdone_ss = sc.cnt("ss")
            mbuf = mixr[0]
            for nt in range(5):
                bank = self.pb[nt % 2]
                c0, cn = NT[nt]
                for k in range(8):
                    w = []
                    if k == 0:
                        w = [("ss", wdone_ss),
                             ("sv", C.get(f"_mev{oc}_{nt - 2}", 0)),
                             ("sv", C.get(f"_cv{oc - 1}", 0))]
                    sc.emit("tensor",
                            (lambda e, k=k, wbuf=wbuf, bank=bank, c0=c0, cn=cn:
                             e.matmul(bank[:, 0:cn],
                                      wbuf[:, k * 128:(k + 1) * 128],
                                      self.hsFM[k][:, c0:c0 + cn],
                                      start=(k == 0), stop=(k == 7))),
                            wait=w, inc=([("st", 1)] if k == 7 else ()))
                sc.emit("vector",
                        (lambda e, bank=bank, mbuf=mbuf, c0=c0, cn=cn:
                         e.tensor_copy(mbuf[:, c0:c0 + cn], bank[:, 0:cn])),
                        wait=[("st", sc.cnt("st"))], inc=[("sv", 1)])
                C[f"_mev{oc}_{nt}"] = sc.cnt("sv")
            C[f"_mmq{oc}"] = sc.cnt("st")
            ceng, csem = "vector", "sv"
            yac = yacc if oc % 2 == 0 else yacc2
            for hf in range(2):
                h0 = hf * 1024
                sc.emit(ceng,
                        (lambda e, mbuf=mbuf, oc=oc, h0=h0, yac=yac:
                         e.tensor_scalar(
                            yac[:, :], mbuf[:, h0:h0 + 1024],
                            self.convw[:, 4 * oc:4 * oc + 1], None, ALU.mult)),
                        wait=[("ss", C.get(f"_silu{oc}_{hf - 1}",
                                           C.get(f"_silu{oc - 2}_1", 0)))],
                        inc=[(csem, 1)])
                for j in range(1, 4):
                    sc.emit(ceng,
                            (lambda e, mbuf=mbuf, oc=oc, j=j, h0=h0, yac=yac:
                             e.scalar_tensor_tensor(
                                 yac[:, :], mbuf[:, h0 + j:h0 + j + 1024],
                                 self.convw[:, 4 * oc + j:4 * oc + j + 1],
                                 yac[:, :], ALU.mult, ALU.add)),
                            inc=[(csem, 1)])
                C[f"_cv{oc}"] = sc.cnt(csem)
                sc.emit("scalar",
                        (lambda e, oc=oc, h0=h0, yac=yac: e.activation(
                            self.yb[oc][:, h0:h0 + 1024], yac[:, :], AF.Silu)),
                        wait=[(csem, sc.cnt(csem))], inc=[("ss", 1)])
                C[f"_silu{oc}_{hf}"] = sc.cnt("ss")
        sc.realize(self.nc, self.sems, "p1m")

    def phase1_z_ab(self, stage, wz, wab, sA, sB, sC, sD, sE, sE2):
        sc, C = Sched(self.C), self.C
        for k in range(8):
            sc.dma("sync", stage[:, 1024:1056],
                   self.d_wab.ap()[k * 128:(k + 1) * 128, :],
                   wait=[("sv", sc.cnt("sv"))])
            sc.emit("vector",
                    (lambda e, k=k: e.tensor_copy(
                        wab[:, k * 32:(k + 1) * 32], stage[:, 1024:1056])),
                    wait=[("sd", sc.cnt("sd"))], inc=[("sv", 1)])

        for hf in range(2):
            for k in range(8):
                w = [("sv", sc.cnt("sv"))] if k > 0 else \
                    [("st", C.get("_zmm", 0))]
                sc.dma("sync", stage[:, 0:512],
                       self.d_wz.ap()[k * 128:(k + 1) * 128,
                                      hf * 512:hf * 512 + 512], wait=w)
                sc.emit("vector",
                        (lambda e, k=k: e.tensor_copy(
                            wz[:, k * 512:(k + 1) * 512], stage[:, 0:512])),
                        wait=([("sd", sc.cnt("sd"))] +
                              ([("st", C.get("_zmm", 0))] if k == 0 else [])),
                        inc=[("sv", 1)])
            wdone = sc.cnt("sv")
            for tt in range(NTT):
                bank = self.pb[tt % 2]
                for k in range(8):
                    w = []
                    if k == 0:
                        w = [("sv", wdone),
                             ("ss", C.get(f"_zev{hf}_{tt - 2}", 0))]
                    sc.emit("tensor",
                            (lambda e, k=k, tt=tt, bank=bank:
                             e.matmul(bank[:, 0:512],
                                      self.hsFM[k][:, HLO + tt * 128:
                                                   HLO + (tt + 1) * 128],
                                      wz[:, k * 512:(k + 1) * 512],
                                      start=(k == 0), stop=(k == 7))),
                            wait=w, inc=([("st", 1)] if k == 7 else ()))
                sc.emit("scalar",
                        (lambda e, tt=tt, hf=hf, bank=bank: e.activation(
                            self.zs[tt][:, hf * 512:hf * 512 + 512],
                            bank[:, 0:512], AF.Silu)),
                        wait=[("st", sc.cnt("st"))], inc=[("ss", 1)])
                C[f"_zev{hf}_{tt}"] = sc.cnt("ss")
            C["_zmm"] = sc.cnt("st")

        for nt in range(4):
            c0 = nt * 512
            for k in range(8):
                w = []
                if k == 0:
                    w = [("ss", C.get(f"_abev{nt - 1}", 0))]
                sc.emit("tensor",
                        (lambda e, k=k, c0=c0: e.matmul(
                            self.pb[4][0:16, 0:512],
                            wab[:, k * 32:k * 32 + 16],
                            self.hsFM[k][:, HLO + c0: HLO + c0 + 512],
                            start=(k == 0), stop=(k == 7))),
                        wait=w, inc=([("st", 1)] if k == 7 else ()))
            for k in range(8):
                sc.emit("tensor",
                        (lambda e, k=k, c0=c0: e.matmul(
                            self.pb[5][0:16, 0:512],
                            wab[:, k * 32 + 16:k * 32 + 32],
                            self.hsFM[k][:, HLO + c0: HLO + c0 + 512],
                            start=(k == 0), stop=(k == 7))),
                        inc=([("st", 1)] if k == 7 else ()))
            st_here = sc.cnt("st")
            sc.emit("scalar",
                    (lambda e: e.activation(
                        sA[:, :], self.pb[4][0:16, 0:512], AF.Identity,
                        bias=self.dtb[:, 0:1])),
                    wait=[("st", st_here),
                          ("sv", C.get(f"_pq{nt - 1}", 0)),
                          ("sd2", C.get(f"_pqd{nt - 1}", 0))],
                    inc=[("ss", 1)])
            sc.emit("scalar",
                    (lambda e: e.activation(
                        sB[:, :], self.pb[5][0:16, 0:512], AF.Sigmoid)),
                    inc=[("ss", 1)])
            ab_s = sc.cnt("ss")
            sc.emit("scalar", lambda e: e.activation(sC[:, :], sA[:, :], AF.Abs),
                    inc=[("ss", 1)])
            sc.emit("scalar", lambda e: e.activation(sC[:, :], sC[:, :], AF.Exp,
                                                     scale=-1.0), inc=[("ss", 1)])
            sc.emit("scalar", lambda e: e.activation(sC[:, :], sC[:, :], AF.Ln,
                                                     bias=1.0), inc=[("ss", 1)])
            sp_s = sc.cnt("ss")
            sc.emit("vector", lambda e: e.tensor_scalar_max(sD[:, :], sA[:, :],
                                                            0.0),
                    wait=[("ss", sp_s)], inc=[("sv", 1)])
            sc.emit("vector", lambda e: e.tensor_add(sD[:, :], sD[:, :],
                                                     sC[:, :]),
                    inc=[("sv", 1)])
            sc.emit("vector", lambda e: e.tensor_scalar_mul(
                sD[:, :], sD[:, :], self.negea[:, 0:1]), inc=[("sv", 1)])
            for q in range(4):
                sc.emit("vector",
                        (lambda e, q=q: e.tensor_tensor_scan(
                            sA[:, q * CKL:(q + 1) * CKL], self.ones16[:, :],
                            sD[:, q * CKL:(q + 1) * CKL], 0.0,
                            ALU.mult, ALU.add)),
                        inc=[("sv", 1)])
            c_v = sc.cnt("sv")
            sc.emit("scalar", lambda e: e.activation(sC[:, :], sB[:, :], AF.Ln),
                    wait=[("sv", c_v)], inc=[("ss", 1)])
            sc.emit("vector", lambda e: e.tensor_add(sC[:, :], sA[:, :],
                                                     sC[:, :]),
                    wait=[("ss", sc.cnt("ss"))], inc=[("sv", 1)])
            sc.dma("scalar", self.d_cb.ap()[:, c0:c0 + 512], sC[:, :],
                   sem="sd2", wait=[("sv", sc.cnt("sv"))])
            for q in range(4):
                ck = nt * 4 + q
                ce = q * CKL + CKL - 1
                sc.dma("scalar", self.d_gc.ap()[ck * 16:(ck + 1) * 16],
                       sA[:, ce:ce + 1], sem="sd2")
            cb_sd2 = sc.cnt("sd2")
            sc.emit("scalar", lambda e: e.activation(sD[:, :], sA[:, :], AF.Exp),
                    inc=[("ss", 1)])
            sc.emit("vector", lambda e: e.tensor_mul(sC[:, :], sB[:, :],
                                                     sD[:, :]),
                    wait=[("ss", sc.cnt("ss")), ("sd2", cb_sd2)],
                    inc=[("sv", 1)])
            bands_v, bands_s = sc.cnt("sv"), sc.cnt("ss")
            for q in range(4):
                ck = nt * 4 + q
                cs = slice(q * CKL, (q + 1) * CKL)
                ce = q * CKL + CKL - 1
                sc.emit("vector",
                        (lambda e, cs=cs: e.tensor_scalar_mul(
                            sE[:, :], sB[:, cs], 8.0)),
                        wait=[("sv", bands_v), ("ss", bands_s),
                              ("st", C.get(f"_ptt{ck - 2}", 0))],
                        inc=[("sv", 1)])
                sc.emit("vector",
                        (lambda e: e.reciprocal(sE[:, :], sE[:, :])),
                        wait=[("sv", sc.cnt("sv"))], inc=[("sv", 1)])
                sc.emit("vector",
                        (lambda e, cs=cs, ce=ce: e.tensor_scalar(
                            sE2[:, :], sA[:, cs], sA[:, ce:ce + 1], None,
                            ALU.subtract)),
                        inc=[("sv", 1)])
                sc.emit("scalar",
                        (lambda e: e.activation(sE2[:, :], sE2[:, :], AF.Exp,
                                                scale=-1.0)),
                        wait=[("sv", sc.cnt("sv"))], inc=[("ss", 1)])
                small_v, small_s = sc.cnt("sv"), sc.cnt("ss")
                srcs = [(sA[:, cs], "negc", -1.0), (sB[:, cs], "beta", 1.0),
                        (sC[:, cs], "nbg", -1.0), (sD[:, cs], "g8", 0.125),
                        (sE[:, :], "r8", 1.0), (sE2[:, :], "kap", 1.0)]
                pt = self.tbk[0]
                for i, (src, band, scale) in enumerate(srcs):
                    sc.emit("tensor",
                            (lambda e, src=src, i=i, pt=pt: e.transpose(
                                pt[:, i * 16:(i + 1) * 16], src,
                                self.identF[0:16, 0:16])),
                            wait=([("sv", small_v), ("ss", small_s),
                                   ("sv", C.get(f"_pte{ck - 1}", 0))]
                                  if i == 0 else ()),
                            inc=([("st", 1)] if i == 5 else ()))
                C[f"_ptt{ck}"] = sc.cnt("st")
                for i, (src, band, scale) in enumerate(srcs):
                    col = 112 * ck + PT_OFF[band]
                    sc.emit("vector",
                            (lambda e, col=col, i=i, scale=scale, pt=pt:
                             e.tensor_scalar_mul(
                                 self.paramT[:, col:col + 16],
                                 pt[:, i * 16:(i + 1) * 16], scale)),
                            wait=([("st", C[f"_ptt{ck}"])] if i == 0 else ()),
                            inc=[("sv", 1)])
                C[f"_pte{ck}"] = sc.cnt("sv")
            C[f"_pq{nt}"] = sc.cnt("sv")
            C[f"_pqd{nt}"] = sc.cnt("sd2")
        sc.dma("scalar", self.gcrep[:, :],
               bass.AP(self.d_gc, 0, [[0, 128], [1, NCK * 16]]), sem="sd2",
               wait=[("sd2", sc.cnt("sd2"))])
        sc.emit("scalar", lambda e: e.activation(self.gcrep[:, :],
                                                 self.gcrep[:, :], AF.Exp),
                wait=[("sd2", sc.cnt("sd2"))], inc=[("ss", 1)])
        if "paramT" in self.taps:
            sc.dma("sync", self.taps["paramT"].ap(), self.paramT[:, :],
                   wait=[("sv", sc.cnt("sv")), ("ss", sc.cnt("ss"))])
        if "zraw" in self.taps:
            sc.dma("sync", self.taps["zraw"].ap(),
                   self.zs[0].bitcast(F32)[:, 0:512],
                   wait=[("sv", sc.cnt("sv")), ("ss", sc.cnt("ss"))])
        sc.realize(self.nc, self.sems, "p1z")

    def phase2_scan(self, t):
        sc, C = Sched(self.C), self.C
        pb = self.pb

        def kslice(kh, ck):
            tile, r0 = self.yb[4 + kh // 2], 64 * (kh % 2)
            return tile[r0:r0 + 64, ck * CKL:(ck + 1) * CKL]

        def qslice(kh, ck):
            tile, r0 = self.yb[0 + kh // 2], 64 * (kh % 2)
            return tile[r0:r0 + 64, ck * CKL:(ck + 1) * CKL]

        def vslice(vh, ck):
            tile, r0 = self.yb[8 + vh // 2], 64 * (vh % 2)
            return tile[r0:r0 + 64, ck * CKL:(ck + 1) * CKL]

        def wslot(ck, vh, buf, width):
            i = (ck % WIN) * NVH + vh
            return buf[:, i * width:(i + 1) * width]

        for wk in range(NCK // WIN):
            cks = [wk * WIN + i for i in range(WIN)]
            first_v_gate = [("st", C.get("_p2st", 0)),
                            ("sd", C.get("_spill_sd", 0))]
            for hp in range(NKH):
                kh = hp
                r0 = 64 * (kh % 2)
                for j, vh in enumerate((2 * hp, 2 * hp + 1)):
                    sc.dma("scalar", t["cbb"][j][:, :],
                           bass.AP(self.d_cb, vh * SEG + cks[0] * CKL,
                                   [[0, 128], [1, WIN * CKL]]),
                           sem="sd2",
                           wait=[("sv", C.get(f"_dwv{hp - 1}_{j}", 0))])
                    sc.emit("vector",
                            (lambda e, j=j: e.tensor_add(
                                t["dw"][j][:, :], t["cbb"][j][:, :],
                                self.maskDW[:, :])),
                            wait=(first_v_gate if (hp == 0 and j == 0) else []) +
                                 [("sd2", sc.cnt("sd2")),
                                  ("ss", C.get(f"_eps{hp - 1}_{j}", 0))],
                            inc=[("sv", 1)])
                    C[f"_dwv{hp}_{j}"] = sc.cnt("sv")
                dw_v = sc.cnt("sv")
                for wi, ck in enumerate(cks):
                    aw = [("sv", C.get("_pfree", 0))]
                    sc.emit("tensor",
                            (lambda e, kh=kh, ck=ck: e.matmul(
                                pb[0][:, 0:128], kslice(kh, ck), kslice(kh, ck),
                                start=True, stop=True)),
                            wait=aw, inc=[("st", 1)])
                    sc.emit("tensor",
                            (lambda e, kh=kh, ck=ck: e.matmul(
                                pb[1][:, 0:128], kslice(kh, ck), qslice(kh, ck),
                                start=True, stop=True)),
                            inc=[("st", 1)])
                    sc.emit("tensor",
                            (lambda e, kh=kh, ck=ck, r0=r0: e.transpose(
                                self.tbk[0].bitcast(BF16)[:, 0:64],
                                kslice(kh, ck),
                                self.identB[r0:r0 + 64, 0:64])),
                            inc=[("st", 1)])
                    a_st = sc.cnt("st")
                    sc.emit("vector",
                            (lambda e: e.tensor_tensor(
                                t["g1s"][:, :], pb[0][:, 0:128],
                                self.maskU[:, :], ALU.mult)),
                            wait=[("st", a_st)], inc=[("sv", 1)])
                    sc.emit("vector",
                            (lambda e: e.tensor_copy(t["aqs"][:, :],
                                                     pb[1][:, 0:128])),
                            inc=[("sv", 1)])
                    for j, vh in enumerate((2 * hp, 2 * hp + 1)):
                        sc.emit("tensor",
                                (lambda e, vh=vh, ck=ck, j=j: e.transpose(
                                    self.tbk[1].bitcast(BF16)[:, j * 64:(j + 1) * 64],
                                    vslice(vh, ck),
                                    self.identB[64 * (vh % 2):64 * (vh % 2) + 64,
                                                0:64])),
                                wait=([("sv", C.get("_vtf", 0))] if j == 0 else ()),
                                inc=[("st", 1)])
                    vt_st = sc.cnt("st")
                    ep_s = {}
                    for j, vh in enumerate((2 * hp, 2 * hp + 1)):
                        sc.emit("scalar",
                                (lambda e, wi=wi, ck=ck, vh=vh, j=j:
                                 e.activation(
                                     t["ep"][j][:, :],
                                     t["dw"][j][:, wi * CKL:(wi + 1) * CKL],
                                     AF.Exp, bias=self.ptcol(ck, "negc", vh))),
                                wait=[("sv", dw_v),
                                      ("sv", C.get(f"_epf{j}", 0))],
                                inc=[("ss", 1)])
                        ep_s[j] = sc.cnt("ss")
                        C[f"_eps{hp}_{j}"] = ep_s[j]
                        sc.emit("vector",
                                (lambda e, ck=ck, vh=vh, j=j:
                                 e.scalar_tensor_tensor(
                                    t["chN"][2 * j][:, :], pb[0][:, 0:128],
                                    self.ptcol(ck, "beta", vh),
                                    self.maskL[:, :], ALU.mult, ALU.mult)),
                                wait=[("st", a_st),
                                      ("st", C.get("_chfree_st", 0))],
                                inc=[("sv", 1)])
                        sc.emit("vector",
                                (lambda e, ck=ck, vh=vh: e.tensor_scalar(
                                    wslot(ck, vh, t["ktk"], 64),
                                    self.tbk[0].bitcast(BF16)[:, 0:64],
                                    self.ptcol(ck, "kap", vh), None, ALU.mult)),
                                inc=[("sv", 1)])
                        sc.emit("vector",
                                (lambda e, ck=ck, vh=vh, j=j: e.tensor_scalar(
                                    wslot(ck, vh, t["vtb"], 64),
                                    self.tbk[1].bitcast(BF16)[:, j * 64:(j + 1) * 64],
                                    self.ptcol(ck, "beta", vh), None, ALU.mult)),
                                wait=([("st", vt_st)] if j == 0 else ()),
                                inc=[("sv", 1)])
                    C["_vtf"] = sc.cnt("sv")
                    bld_v = sc.cnt("sv")
                    bld_ss = sc.cnt("ss")

                    CB = {0: (pb[2], pb[3], pb[4]), 1: (pb[1], pb[5], pb[0])}
                    for j in (0, 1):
                        bN, bS, bH = CB[j]
                        sc.emit("tensor",
                                (lambda e, j=j, bN=bN: e.matmul(
                                    bN[:, 0:128], t["g1s"][:, :],
                                    t["chN"][2 * j][:, :],
                                    start=True, stop=True)),
                                wait=[("sv", bld_v), ("ss", bld_ss)],
                                inc=[("st", 1)])
                        sc.emit("tensor",
                                (lambda e, j=j, bS=bS: e.matmul(
                                    bS[:, 0:128], t["chN"][2 * j][:, :],
                                    t["g1s"][:, :], start=True, stop=True)),
                                inc=[("st", 1)])
                        C[f"_cl1{j}"] = sc.cnt("st")
                    for j in (0, 1):
                        bN, bS, bH = CB[j]
                        vh = 2 * hp + j
                        sc.emit("vector",
                                (lambda e, ck=ck, vh=vh, j=j, bN=bN:
                                 e.tensor_scalar(
                                    t["chN"][2 * j + 1][:, :], bN[:, 0:128],
                                    self.ptcol(ck, "beta", vh), None, ALU.mult)),
                                wait=[("st", C[f"_cl1{j}"])], inc=[("sv", 1)])
                        sc.emit("vector",
                                (lambda e, j=j, bS=bS: e.tensor_copy(
                                    t["chS"][2 * j + 1][:, :], bS[:, 0:128])),
                                inc=[("sv", 1)])
                        sc.emit("vector",
                                (lambda e, j=j, bS=bS: e.tensor_sub(
                                    t["chH"][2 * j + 1][:, :], bS[:, 0:128],
                                    t["g1s"][:, :])),
                                inc=[("sv", 1)])
                        C[f"_cle{j}"] = sc.cnt("sv")
                        C[f"_cles{j}"] = sc.cnt("ss")
                    cur = {0: 1, 1: 1}
                    for m in (2, 4, 8, 16, 32):
                        for j in (0, 1):
                            bN, bS, bH = CB[j]
                            cu = cur[j]
                            sc.emit("tensor",
                                    (lambda e, j=j, cu=cu, bN=bN: e.matmul(
                                        bN[:, 0:128], t["chS"][2 * j + cu][:, :],
                                        t["chN"][2 * j + cu][:, :],
                                        start=True, stop=True)),
                                    wait=[("sv", C[f"_cle{j}"])],
                                    inc=[("st", 1)])
                            sc.emit("tensor",
                                    (lambda e, j=j, cu=cu, bS=bS: e.matmul(
                                        bS[:, 0:128], t["chN"][2 * j + cu][:, :],
                                        t["chS"][2 * j + cu][:, :],
                                        start=True, stop=True)),
                                    inc=[("st", 1)])
                            sc.emit("tensor",
                                    (lambda e, j=j, cu=cu, bH=bH: e.matmul(
                                        bH[:, 0:128], t["chN"][2 * j + cu][:, :],
                                        t["chH"][2 * j + cu][:, :],
                                        start=True, stop=True)),
                                    inc=[("st", 1)])
                            C[f"_clm{j}"] = sc.cnt("st")
                        for j in (0, 1):
                            bN, bS, bH = CB[j]
                            cu = cur[j]
                            nx = 1 - cu
                            vh = 2 * hp + j
                            sc.emit("vector",
                                    (lambda e, ck=ck, vh=vh, j=j, nx=nx, bN=bN:
                                     e.tensor_scalar(
                                        t["chN"][2 * j + nx][:, :], bN[:, 0:128],
                                        self.ptcol(ck, "beta", vh), None,
                                        ALU.mult)),
                                    wait=[("st", C[f"_clm{j}"])], inc=[("sv", 1)])
                            sc.emit("vector",
                                    (lambda e, j=j, nx=nx, bS=bS: e.tensor_copy(
                                        t["chS"][2 * j + nx][:, :], bS[:, 0:128])),
                                    inc=[("sv", 1)])
                            sc.emit("vector",
                                    (lambda e, j=j, cu=cu, nx=nx, bH=bH:
                                     e.tensor_add(
                                        t["chH"][2 * j + nx][:, :], bH[:, 0:128],
                                        t["chH"][2 * j + cu][:, :])),
                                    inc=[("sv", 1)])
                            C[f"_cle{j}"] = sc.cnt("sv")
                            cur[j] = nx
                    for j in (0, 1):
                        bN, bS, bH = CB[j]
                        cu = cur[j]
                        sc.emit("tensor",
                                (lambda e, j=j, cu=cu, bH=bH: e.matmul(
                                    bH[:, 0:128], t["chN"][2 * j + cu][:, :],
                                    t["chH"][2 * j + cu][:, :],
                                    start=True, stop=True)),
                                wait=[("sv", C[f"_cle{j}"])],
                                inc=[("st", 1)])
                        C[f"_clf{j}"] = sc.cnt("st")
                    C["_chfree_st"] = sc.cnt("st")
                    for j in (0, 1):
                        bN, bS, bH = CB[j]
                        cu = cur[j]
                        vh = 2 * hp + j
                        sc.emit("vector",
                                (lambda e, j=j, cu=cu, bH=bH: e.tensor_add(
                                    t["chH"][2 * j + (1 - cu)][:, :],
                                    bH[:, 0:128],
                                    t["chH"][2 * j + cu][:, :])),
                                wait=[("st", C[f"_clf{j}"])], inc=[("sv", 1)])
                        sc.emit("vector",
                                (lambda e, ck=ck, vh=vh, j=j, cu=cu:
                                 e.tensor_tensor(
                                     wslot(ck, vh, t["tst"], 128),
                                     t["ep"][j][:, :],
                                     t["chH"][2 * j + (1 - cu)][:, :],
                                     ALU.mult)),
                                wait=[("ss", ep_s[j]), ("sv", sc.cnt("sv"))],
                                inc=[("sv", 1)])
                        sc.emit("vector",
                                (lambda e, ck=ck, vh=vh, j=j: e.tensor_tensor(
                                    wslot(ck, vh, t["pst"], 128),
                                    t["ep"][j][:, :],
                                    t["aqs"][:, :], ALU.mult)),
                                inc=[("sv", 1)])
                        C[f"_epf{j}"] = sc.cnt("sv")
                    C["_pfree"] = sc.cnt("sv")

            spill_gate = [("sv", sc.cnt("sv"))]
            for buf, dscr, w in ((t["tst"], self.d_tst, 128),
                                 (t["pst"], self.d_pst, 128),
                                 (t["ktk"], self.d_ktk, 64),
                                 (t["vtb"], self.d_vtb, 64)):
                sc.dma("sync", dscr.ap()[:, wk * WIN * NVH * w:
                                         (wk + 1) * WIN * NVH * w],
                       buf[:, :], wait=spill_gate)
            C["_spill_sd"] = sc.cnt("sd")
            C["_pfree"] = sc.cnt("sv")
            C["_p2st"] = sc.cnt("st")

        sc.realize(self.nc, self.sems, "p2a")
        sc = Sched(self.C)
        for rnd in range(4):
            if rnd > 0:
                sc.dma("scalar", self.d_agin.ap(), self.S_send[:, :],
                       sem="sd2", wait=[("sv", sc.cnt("sv"))])

                def ccfn(e):
                    return e.collective_compute(
                        "AllGather", ALU.bypass,
                        replica_groups=[list(range(8))],
                        ins=[self.d_agin.ap().opt()],
                        outs=[self.d_agout.ap().opt()],
                    )
                sc.emit("gpsimd", ccfn, wait=[("sd2", sc.cnt("sd2"))],
                        inc=[("scc", 1)])
            sc.emit("vector", lambda e: e.memset(self.S_f[:, :], 0.0),
                    wait=[("sv", sc.cnt("sv")), ("st", sc.cnt("st"))],
                    inc=[("sv", 1)])
            if rnd > 0:
                for i in range(8):
                    sc.dma("scalar", self.agbuf[:, :],
                           self.d_agout.ap()[i * 128:(i + 1) * 128, :],
                           sem="sd2",
                           wait=([("scc", sc.cnt("scc"))] if i == 0 else
                                 [("sv", sc.cnt("sv"))]))
                    sc.emit("vector",
                            (lambda e, i=i, rnd=rnd: e.scalar_tensor_tensor(
                                self.S_f[:, :], self.agbuf[:, :],
                                self.msel[:, rnd * 8 + i:rnd * 8 + i + 1],
                                self.S_f[:, :], ALU.mult, ALU.add)),
                            wait=[("sd2", sc.cnt("sd2")),
                                  ("sv", sc.cnt("sv"))],
                            inc=[("sv", 1)])
            sc.emit("scalar", lambda e: e.copy(self.S_b[:, :], self.S_f[:, :]),
                    wait=[("sv", sc.cnt("sv"))], inc=[("ss", 1)])
            C["_rs_v"] = sc.cnt("sv")
            C["_rs_ss"] = sc.cnt("ss")

            for wk in range(NCK // WIN):
                cks = [wk * WIN + i for i in range(WIN)]
                ld_gate = [("st", sc.cnt("st")), ("sv", sc.cnt("sv"))]
                for buf, dscr, w in ((t["tst"], self.d_tst, 128),
                                     (t["pst"], self.d_pst, 128),
                                     (t["ktk"], self.d_ktk, 64),
                                     (t["vtb"], self.d_vtb, 64)):
                    sc.dma("sync", buf[:, :],
                           dscr.ap()[:, wk * WIN * NVH * w:
                                     (wk + 1) * WIN * NVH * w],
                           wait=ld_gate)
                slab_sd = sc.cnt("sd")
                first = [("sd", slab_sd), ("sv", C["_rs_v"]),
                         ("ss", C.get("_rs_ss", 0))]
                for ck in cks:
                    for g in range(NVH // 4):
                        vhs = [4 * g + i for i in range(4)]
                        banks = [(pb + self.tbk)[(4 * g + i) % 8]
                                 for i in range(4)]
                        for i, vh in enumerate(vhs):
                            kh, (r0, cb) = vh // 2, svh(vh)
                            bk = banks[i]
                            sc.emit("tensor",
                                    (lambda e, kh=kh, ck=ck, r0=r0, cb=cb, bk=bk:
                                     e.matmul(bk[:, 0:64], kslice(kh, ck),
                                              self.S_b[r0:r0 + 64, cb:cb + 64],
                                              start=True, stop=True)),
                                    wait=first +
                                         [("ss", C.get(f"_p2bs{ck}_{vh}", 0)),
                                          ("sv", C.get(f"_p2f{(ck - (0 if g >= 2 else 1)) % NCK}_{(vh + 8) % 16}", 0))],
                                    inc=[("st", 1)])
                            first = []
                            C[f"_p2ks{ck}_{vh}"] = sc.cnt("st")
                        for i, vh in enumerate(vhs):
                            bk = banks[i]
                            sc.emit("vector",
                                    (lambda e, ck=ck, vh=vh, i=i, bk=bk:
                                     e.scalar_tensor_tensor(
                                         t["rt"][i][:, :], bk[:, 0:64],
                                         self.ptcol(ck, "nbg", vh),
                                         wslot(ck, vh, t["vtb"], 64),
                                         ALU.mult, ALU.add)),
                                    wait=[("st", C[f"_p2ks{ck}_{vh}"])],
                                    inc=[("sv", 1)])
                            C[f"_p2r{ck}_{vh}"] = sc.cnt("sv")
                        for i, vh in enumerate(vhs):
                            bk = banks[i]
                            sc.emit("tensor",
                                    (lambda e, ck=ck, vh=vh, i=i, bk=bk: e.matmul(
                                        bk[:, 64:128],
                                        wslot(ck, vh, t["tst"], 128),
                                        t["rt"][i][:, :], start=True, stop=True)),
                                    wait=[("sv", C[f"_p2r{ck}_{vh}"])],
                                    inc=[("st", 1)])
                            C[f"_p2w{ck}_{vh}"] = sc.cnt("st")
                        for i, vh in enumerate(vhs):
                            bk = banks[i]
                            sc.emit("vector",
                                    (lambda e, i=i, bk=bk: e.tensor_add(
                                        t["wt"][i][:, :], bk[:, 64:128],
                                        t["rt"][i][:, :])),
                                    wait=[("st", C[f"_p2w{ck}_{vh}"])],
                                    inc=[("sv", 1)])
                            C[f"_p2we{ck}_{vh}"] = sc.cnt("sv")
                        for i, vh in enumerate(vhs):
                            kh, (r0, cb) = vh // 2, svh(vh)
                            bk = banks[i]
                            sc.emit("tensor",
                                    (lambda e, ck=ck, vh=vh, i=i, bk=bk: e.matmul(
                                        bk[:, 128:192],
                                        wslot(ck, vh, t["pst"], 128),
                                        t["wt"][i][:, :], start=True, stop=True)),
                                    wait=[("sv", C[f"_p2we{ck}_{vh}"])],
                                    inc=[("st", 1)])
                            sc.emit("tensor",
                                    (lambda e, kh=kh, ck=ck, r0=r0, cb=cb, bk=bk:
                                     e.matmul(
                                        bk[:, 192:256], qslice(kh, ck),
                                        self.S_b[r0:r0 + 64, cb:cb + 64],
                                        start=True, stop=True)),
                                    inc=[("st", 1)])
                            C[f"_p2o{ck}_{vh}"] = sc.cnt("st")
                        for i, vh in enumerate(vhs):
                            bk = banks[i]
                            sc.emit("vector",
                                    (lambda e, ck=ck, vh=vh, i=i, bk=bk:
                                     e.tensor_scalar(
                                        t["ot"][i][:, :], bk[:, 192:256],
                                        self.ptcol(ck, "g8", vh), None,
                                        ALU.mult)),
                                    wait=[("st", C[f"_p2o{ck}_{vh}"])],
                                    inc=[("sv", 1)])
                            sc.emit("vector",
                                    (lambda e, ck=ck, vh=vh, i=i, bk=bk:
                                     e.scalar_tensor_tensor(
                                        t["ot"][i][:, :], bk[:, 128:192],
                                        self.ptcol(ck, "r8", vh),
                                        t["ot"][i][:, :], ALU.mult, ALU.add)),
                                    wait=[("sv", sc.cnt("sv"))], inc=[("sv", 1)])
                            sc.emit("vector",
                                    (lambda e, ck=ck, vh=vh, i=i, rnd=rnd:
                                     e.scalar_tensor_tensor(
                                        self.ob[ck][:, vh * 64:(vh + 1) * 64],
                                        t["ot"][i][:, :],
                                        self.mc[:, rnd:rnd + 1],
                                        self.ob[ck][:, vh * 64:(vh + 1) * 64],
                                        ALU.mult, ALU.add)),
                                    wait=[("sv", sc.cnt("sv"))], inc=[("sv", 1)])
                            C[f"_p2oe{ck}_{vh}"] = sc.cnt("sv")
                        for i, vh in enumerate(vhs):
                            r0, cb = svh(vh)
                            bk = banks[i]
                            sc.emit("tensor",
                                    (lambda e, ck=ck, vh=vh, i=i, r0=r0, bk=bk:
                                     e.matmul(
                                        bk[r0:r0 + 64, 256:320],
                                        wslot(ck, vh, t["ktk"], 64),
                                        t["wt"][i][:, :], start=True, stop=True,
                                        tile_position=(0, r0))),
                                    wait=[("sv", C[f"_p2oe{ck}_{vh}"])],
                                    inc=[("st", 1)])
                            C[f"_p2s1{ck}_{vh}"] = sc.cnt("st")
                        for i, vh in enumerate(vhs):
                            r0, cb = svh(vh)
                            bk = banks[i]
                            sc.emit("vector",
                                    (lambda e, ck=ck, vh=vh, r0=r0, cb=cb, bk=bk:
                                     e.scalar_tensor_tensor(
                                        self.S_f[r0:r0 + 64, cb:cb + 64],
                                        self.S_f[r0:r0 + 64, cb:cb + 64],
                                        self.gcrep[r0:r0 + 64,
                                                   16 * ck + vh:16 * ck + vh + 1],
                                        bk[r0:r0 + 64, 256:320], ALU.mult,
                                        ALU.add)),
                                    wait=[("st", C[f"_p2s1{ck}_{vh}"])],
                                    inc=[("sv", 1)])
                            sc.emit("scalar",
                                    (lambda e, r0=r0, cb=cb: e.copy(
                                        self.S_b[r0:r0 + 64, cb:cb + 64],
                                        self.S_f[r0:r0 + 64, cb:cb + 64])),
                                    wait=[("sv", sc.cnt("sv"))], inc=[("ss", 1)])
                            C[f"_p2bs{ck + 1}_{vh}"] = sc.cnt("ss")
                            C[f"_p2f{ck}_{vh}"] = sc.cnt("sv")
            sc.emit("vector",
                    (lambda e, rnd=rnd: e.tensor_scalar(
                        self.agbuf[:, :], self.S_f[:, :],
                        self.mc[:, rnd:rnd + 1], None, ALU.mult)),
                    wait=[("sv", sc.cnt("sv"))], inc=[("sv", 1)])
            sc.emit("vector",
                    (lambda e, rnd=rnd: e.scalar_tensor_tensor(
                        self.S_send[:, :], self.S_send[:, :],
                        self.mcn[:, rnd:rnd + 1], self.agbuf[:, :],
                        ALU.mult, ALU.add)),
                    wait=[("sv", sc.cnt("sv"))], inc=[("sv", 1)])

        sc.dma("sync", self.d_send.ap(), self.S_send[:, :],
               wait=[("sv", sc.cnt("sv"))])
        if "sf" in self.taps:
            sc.dma("sync", self.taps["sf"].ap(), self.S_f[:, :],
                   wait=[("sv", sc.cnt("sv"))])
        sc.realize(self.nc, self.sems, "p2")

    def phase3_post(self, p):
        sc, C = Sched(self.C), self.C
        pb = self.pb
        wout, wffn, stage2 = p["wout"], p["wffn"], p["stage2"]
        self.wout_dbg = wout
        x1, htmp, hsq = p["x1"], p["htmp"], p["hsq"]
        hnb, hnF, hsr, sm, rr = p["hnb"], p["hnF"], p["hsr"], p["sm"], p["rr"]

        for k in range(8):
            w = [("sv", sc.cnt("sv"))] if k > 0 else ()
            sc.dma("sync", stage2[:, :],
                   self.d_wout.ap()[k * 128:(k + 1) * 128, :], wait=w)
            sc.emit("vector",
                    (lambda e, k=k: e.tensor_scalar(
                        wout[:, k * H:(k + 1) * H], stage2[:, :],
                        self.nwrep[:, 0:1], None, ALU.mult)),
                    wait=[("sd", sc.cnt("sd"))], inc=[("sv", 1)])
        wo_v = sc.cnt("sv")
        for k in range(8):
            w = [("sv", wo_v)] if k == 0 else [("ss", sc.cnt("ss"))]
            sc.dma("sync", stage2[:, :],
                   self.d_wffn.ap()[k * 128:(k + 1) * 128, :], wait=w)
            sc.emit("scalar",
                    (lambda e, k=k: e.sign(wffn[:, k * H:(k + 1) * H],
                                           stage2[:, :])),
                    wait=[("sd", sc.cnt("sd"))], inc=[("ss", 1)])
        wf_s = sc.cnt("ss")

        for tt in range(NTT):
            sc.emit("vector",
                    (lambda e, tt=tt: e.tensor_mul(htmp[:, :], self.ob[tt][:, :],
                                                   self.zs[tt][:, :])),
                    wait=[("sv", wo_v), ("ss", C.get(f"_lns{tt - 1}", 0)),
                          ("sd", C.get(f"_osd{tt - 1}", 0))],
                    inc=[("sv", 1)])
            h_v = sc.cnt("sv")
            sc.emit("scalar",
                    (lambda e: e.activation(hsq[:, :], htmp[:, :], AF.Square)),
                    wait=[("sv", h_v)], inc=[("ss", 1)])
            sc.emit("vector",
                    (lambda e: e.tensor_reduce(
                        rr[:, :], hsq[:, :].rearrange("p (h e) -> p h e", h=16),
                        AX.X, ALU.add)),
                    wait=[("ss", sc.cnt("ss"))], inc=[("sv", 1)])
            sc.emit("vector",
                    (lambda e: e.tensor_scalar(rr[:, :], rr[:, :], 1.0 / DV,
                                               RMS_EPS, ALU.mult, ALU.add)),
                    wait=[("sv", sc.cnt("sv"))], inc=[("sv", 1)])
            sc.emit("vector", lambda e: e.reciprocal(rr[:, :], rr[:, :]),
                    wait=[("sv", sc.cnt("sv"))], inc=[("sv", 1)])
            sc.emit("scalar", lambda e: e.activation(rr[:, :], rr[:, :], AF.Sqrt),
                    wait=[("sv", sc.cnt("sv"))], inc=[("ss", 1)])
            sc.emit("vector",
                    (lambda e: e.tensor_tensor(
                        hnb[:, :].rearrange("p (h e) -> p h e", h=16),
                        htmp[:, :].rearrange("p (h e) -> p h e", h=16),
                        rr[:, :].rearrange("p (h o) -> p h o", o=1)
                        .broadcast_to([128, 16, DV]), ALU.mult)),
                    wait=[("ss", sc.cnt("ss"))], inc=[("sv", 1)])
            hn_v = sc.cnt("sv")
            for f in range(8):
                sc.emit("tensor",
                        (lambda e, f=f: e.transpose(
                            self.tbk[0].bitcast(BF16)[:, f * 128:(f + 1) * 128],
                            hnb[:, f * 128:(f + 1) * 128], self.ident[:, :])),
                        wait=([("sv", hn_v)] if f == 0 else ()),
                        inc=([("st", 1)] if f == 7 else ()))
            sc.emit("vector",
                    (lambda e: e.tensor_copy(
                        hnF[:, :], self.tbk[0].bitcast(BF16)[:, 0:1024])),
                    wait=[("st", sc.cnt("st"))], inc=[("sv", 1)])
            hnf_v = sc.cnt("sv")
            sc.dma("sync", hsr[tt % 2][:, :],
                   self.d_hs.ap()[HLO + tt * 128: HLO + (tt + 1) * 128, :],
                   wait=[("sv", C.get(f"_x1v{tt - 2}", 0))])
            hs_sd = sc.cnt("sd")
            for hf in range(2):
                bank = pb[hf]
                for k in range(8):
                    w = [("sv", hnf_v)] if k == 0 else ()
                    sc.emit("tensor",
                            (lambda e, k=k, hf=hf, bank=bank: e.matmul(
                                bank[:, 0:512], hnF[:, k * 128:(k + 1) * 128],
                                wout[:, k * H + hf * 512:k * H + hf * 512 + 512],
                                start=(k == 0), stop=(k == 7))),
                            wait=w, inc=([("st", 1)] if k == 7 else ()))
                sc.emit("vector",
                        (lambda e, tt=tt, hf=hf, bank=bank: e.tensor_add(
                            x1[tt][:, hf * 512:hf * 512 + 512], bank[:, 0:512],
                            hsr[tt % 2][:, hf * 512:hf * 512 + 512])),
                        wait=[("st", sc.cnt("st")), ("sd", hs_sd)],
                        inc=[("sv", 1)])
            if tt == 4 and "hnb4" in self.taps:
                sc.dma("sync", self.taps["hnb4"].ap(),
                       hnb.bitcast(F32)[:, 0:512], wait=[("sv", hn_v)])
                sc.dma("sync", self.taps["hnf4"].ap(),
                       hnF.bitcast(F32)[:, 0:512], wait=[("sv", hnf_v)])
                sc.dma("sync", self.taps["x1p4"].ap(), x1[tt][:, :],
                       wait=[("sv", sc.cnt("sv"))])
                sc.dma("sync", self.taps["wout4"].ap(),
                       self.wout_dbg.bitcast(F32)[:, 0:512],
                       wait=[("sv", sc.cnt("sv"))])
                sc.emit("vector", None, wait=[("sd", sc.cnt("sd"))])
            self._layernorm(sc, x1[tt], hsq, sm)
            C[f"_x1v{tt}"] = sc.cnt("sv")
            C[f"_lns{tt}"] = sc.cnt("ss")
            C[f"_osd{tt}"] = 0
            if "x1" in self.taps and tt == 4:
                sc.dma("sync", self.taps["x1"].ap(), x1[tt][:, :],
                       wait=[("sv", sc.cnt("sv"))])

        for tt in range(NTT):
            x1v = C[f"_x1v{tt}"]
            for f in range(8):
                sc.emit("tensor",
                        (lambda e, tt=tt, f=f: e.transpose(
                            self.tbk[1][:, 0:128],
                            x1[tt][:, f * 128:(f + 1) * 128],
                            self.identF[:, :])),
                        wait=[("sv", max(x1v, C.get("_x1fe", 0)))],
                        inc=[("st", 1)])
                sc.emit("vector",
                        (lambda e, f=f: e.tensor_copy(
                            hnF[:, f * 128:(f + 1) * 128], self.tbk[1][:, 0:128])),
                        wait=[("st", sc.cnt("st"))], inc=[("sv", 1)])
                C["_x1fe"] = sc.cnt("sv")
            x1f_v = sc.cnt("sv")
            for hf in range(2):
                bank = pb[2 + hf]
                for k in range(8):
                    w = [("sv", x1f_v), ("ss", wf_s)] if k == 0 else ()
                    sc.emit("tensor",
                            (lambda e, k=k, hf=hf, bank=bank: e.matmul(
                                bank[:, 0:512], hnF[:, k * 128:(k + 1) * 128],
                                wffn[:, k * H + hf * 512:k * H + hf * 512 + 512],
                                start=(k == 0), stop=(k == 7))),
                            wait=w, inc=([("st", 1)] if k == 7 else ()))
                sc.emit("vector",
                        (lambda e, tt=tt, hf=hf, bank=bank: e.tensor_add(
                            htmp[:, hf * 512:hf * 512 + 512], bank[:, 0:512],
                            x1[tt][:, hf * 512:hf * 512 + 512])),
                        wait=[("st", sc.cnt("st")),
                              ("sd", C.get("_o2sd", 0)),
                              ("ss", C.get(f"_lns{NTT - 1}", 0))],
                        inc=[("sv", 1)])
            self._layernorm(sc, htmp, hsq, sm)
            sc.dma("sync", self.d_out.ap()[tt * 128:(tt + 1) * 128, :],
                   htmp[:, :], wait=[("sv", sc.cnt("sv"))])
            C["_o2sd"] = sc.cnt("sd")
        if "zraw2" in self.taps:
            sc.dma("sync", self.taps["zraw2"].ap(),
                   self.zs[0].bitcast(F32)[:, 0:512],
                   wait=[("sv", sc.cnt("sv"))])
        sc.emit("sync", lambda e: e.wait_ge(self.sems["sd"], sc.cnt("sd")))
        sc.emit("sync", lambda e: e.wait_ge(self.sems["sd2"], sc.cnt("sd2")))
        sc.realize(self.nc, self.sems, "p3")

    def _layernorm(self, sc, xt, scratch, sm):
        xt_v = sc.cnt("sv")
        sc.emit("vector",
                (lambda e: e.tensor_reduce(sm[0][:, :], xt[:, :], AX.X, ALU.add)),
                inc=[("sv", 1)])
        sc.emit("vector",
                (lambda e: e.tensor_scalar_mul(sm[0][:, :], sm[0][:, :], 1.0 / H)),
                wait=[("sv", sc.cnt("sv"))], inc=[("sv", 1)])
        sc.emit("scalar",
                (lambda e: e.activation(scratch[:, :], xt[:, :], AF.Square)),
                wait=[("sv", xt_v)], inc=[("ss", 1)])
        sc.emit("vector",
                (lambda e: e.tensor_reduce(sm[1][:, :], scratch[:, :], AX.X,
                                           ALU.add)),
                wait=[("ss", sc.cnt("ss"))], inc=[("sv", 1)])
        sc.emit("vector",
                (lambda e: e.tensor_scalar_mul(sm[1][:, :], sm[1][:, :], 1.0 / H)),
                wait=[("sv", sc.cnt("sv"))], inc=[("sv", 1)])
        sc.emit("vector",
                (lambda e: e.tensor_mul(sm[2][:, :], sm[0][:, :], sm[0][:, :])),
                wait=[("sv", sc.cnt("sv"))], inc=[("sv", 1)])
        sc.emit("vector",
                (lambda e: e.tensor_sub(sm[1][:, :], sm[1][:, :], sm[2][:, :])),
                wait=[("sv", sc.cnt("sv"))], inc=[("sv", 1)])
        sc.emit("vector",
                (lambda e: e.tensor_scalar_add(sm[1][:, :], sm[1][:, :], LN_EPS)),
                wait=[("sv", sc.cnt("sv"))], inc=[("sv", 1)])
        sc.emit("vector", (lambda e: e.reciprocal(sm[2][:, :], sm[1][:, :])),
                wait=[("sv", sc.cnt("sv"))], inc=[("sv", 1)])
        sc.emit("scalar", (lambda e: e.activation(sm[3][:, :], sm[2][:, :],
                                                  AF.Sqrt)),
                wait=[("sv", sc.cnt("sv"))], inc=[("ss", 1)])
        sc.emit("vector",
                (lambda e: e.tensor_scalar(xt[:, :], xt[:, :], sm[0][:, 0:1],
                                           sm[3][:, 0:1], ALU.subtract,
                                           ALU.mult)),
                wait=[("ss", sc.cnt("ss"))], inc=[("sv", 1)])



_NC_CACHE = {}


def _get_nc():
    if "main" not in _NC_CACHE:
        _NC_CACHE["main"] = Builder().build()
    return _NC_CACHE["main"]


def _consts_maps():
    i0, i1 = np.indices((128, 128))
    ident = np.eye(128, dtype=np.float32).astype(ml_dtypes.bfloat16)
    identB = (np.eye(128) + np.eye(128, k=-64)).astype(np.float32) \
        .astype(ml_dtypes.bfloat16)
    maskL = (i0 > i1).astype(np.float32).astype(ml_dtypes.bfloat16)
    maskU = (i0 < i1).astype(np.float32).astype(ml_dtypes.bfloat16)
    maskD = np.where(i0 <= i1, 0.0, NEG_BIG).astype(np.float32)
    maskDW = np.ascontiguousarray(np.tile(maskD, (1, WIN)))
    ones16 = np.ones((16, CKL), dtype=np.float32)
    identF = np.eye(128, dtype=np.float32)
    return {"ident_bf": ident, "identB_bf": identB, "identF": identF,
            "maskL": maskL, "maskU": maskU, "maskDW": maskDW,
            "ones16": ones16}


def kernel(L0_attn_hidden_states, L0_attn_W_qkv, L0_attn_conv_w, L0_attn_W_z,
           L0_attn_W_b, L0_attn_W_a, L0_attn_dt_bias, L0_attn_A_log,
           L0_attn_norm_weight, L0_attn_W_out, L0_ffn_weight):
    hs = np.ascontiguousarray(L0_attn_hidden_states, dtype=np.float32)
    wqkvT = np.ascontiguousarray(np.asarray(L0_attn_W_qkv, np.float32).T)
    convw = np.asarray(L0_attn_conv_w, np.float32)[:, 0, :]
    convw_p = np.zeros((128, 64), np.float32)
    for oc in range(16):
        convw_p[:, 4 * oc:4 * oc + 4] = convw[oc * 128:(oc + 1) * 128, :]
    wzT = np.ascontiguousarray(np.asarray(L0_attn_W_z, np.float32).T)
    wab = np.concatenate([np.asarray(L0_attn_W_a, np.float32),
                          np.asarray(L0_attn_W_b, np.float32)], axis=0)
    wabT = np.ascontiguousarray(wab.T)
    woutT = np.ascontiguousarray(np.asarray(L0_attn_W_out, np.float32).T)
    wffnT = np.ascontiguousarray(np.asarray(L0_ffn_weight, np.float32).T)
    dtb = np.ascontiguousarray(L0_attn_dt_bias, np.float32)
    alog = np.ascontiguousarray(L0_attn_A_log, np.float32)
    nw_rep = np.ascontiguousarray(
        np.tile(np.asarray(L0_attn_norm_weight, np.float32), 2))
    consts = _consts_maps()

    def shard_hs(core):
        b, q = core // 4, core % 4
        st = q * SEG
        pad = np.zeros((SEGH, H), np.float32)
        lo = max(0, st - HLO)
        pad[HLO - (st - lo):] = hs[b, lo:st + SEG]
        return pad

    base = [dict(hs_pad=shard_hs(c), wqkvT=wqkvT, convw_p=convw_p, wzT=wzT,
                 wabT=wabT, woutT=woutT, wffnT=wffnT, dtb=dtb, alog=alog,
                 nw_rep=nw_rep, **consts) for c in range(8)]

    nc = _get_nc()

    def masks(c):
        msel = np.zeros((128, 32), np.float32)
        r = c % 4
        if r > 0:
            msel[:, r * 8 + (c - 1)] = 1.0
        mc = np.zeros((128, 4), np.float32)
        mc[:, r] = 1.0
        mcn = 1.0 - mc
        return {"msel": msel, "mc": mc, "mcn": mcn}

    in_maps = [dict(base[c], **masks(c)) for c in range(8)]
    res = run_bass_kernel_spmd(nc, in_maps, core_ids=list(range(8)))
    out = np.empty((B, S, H), np.float32)
    for c in range(8):
        b, q = c // 4, c % 4
        out[b, q * SEG:(q + 1) * SEG] = res.results[c]["out_x2"]
    return out


# version 19
# speedup vs baseline: 3.8880x; 1.0104x over previous
import numpy as np
import ml_dtypes
from contextlib import ExitStack

import concourse.bass as bass
import concourse.mybir as mybir
from concourse.bass_utils import run_bass_kernel_spmd

F32 = mybir.dt.float32
BF16 = mybir.dt.bfloat16
AF = mybir.ActivationFunctionType
ALU = mybir.AluOpType
AX = mybir.AxisListType

B, S, H = 2, 8192, 1024
NKH, NVH, DK, DV = 8, 16, 64, 64
VAL_DIM = NVH * DV
CONV_DIM = 2 * NKH * DK + VAL_DIM
KSZ = 4
SEG, CKL = 2048, 128
NCK = SEG // CKL
NTT = SEG // 128
HLO = KSZ - 1
SEGH = SEG + HLO
WIN = 2
RMS_EPS, LN_EPS, NEG_BIG = 1e-6, 1e-5, -1e9

BNC, BCB, BB, BNBG, BKAP, BR8, BG8, BC = (i * 16 for i in range(8))
PT_OFF = {"negc": 0, "cb": 16, "beta": 32, "nbg": 48, "kap": 64, "r8": 80,
          "g8": 96}

DEBUG_TAPS = {}


def svh(vh):
    kh = vh // 2
    r0 = 64 * (kh % 2)
    cb = ((kh // 2) * 2 + (vh % 2)) * 64
    return r0, cb


class Sched:
    ENGINES = ("sync", "tensor", "vector", "scalar", "gpsimd")

    def __init__(self, counts):
        self.ops = {e: [] for e in self.ENGINES}
        self.counts = counts
        for k in ("sd", "sd2"):
            self.counts.setdefault("_mw_" + k, 0)
            self.counts.setdefault("_bar_" + k, 0)

    def cnt(self, sem):
        return self.counts.get(sem, 0)

    def emit(self, eng, fn, wait=(), inc=()):
        for sname, v in wait:
            if sname in ("sd", "sd2"):
                self.counts["_mw_" + sname] = max(
                    self.counts.get("_mw_" + sname, 0), v)
        for sname, n in inc:
            self.counts[sname] = self.counts.get(sname, 0) + n
        self.ops[eng].append((fn, tuple(wait), tuple(inc)))

    def dma(self, eng, out, in_, sem="sd", wait=()):
        wait = list(wait)
        pre = self.counts.get(sem, 0)
        if self.counts.get("_mw_" + sem, 0) > self.counts.get("_bar_" + sem, 0):
            wait.append((sem, pre))
            self.counts["_bar_" + sem] = pre

        def fn(e, out=out, in_=in_):
            return e.dma_start(out=out, in_=in_)
        self.emit(eng, fn, wait=wait, inc=((sem, 16),))

    def realize(self, nc, sems, name=None):
        import contextlib
        scope = nc.named_scope(name) if name else contextlib.nullcontext()
        with scope, nc.Block(name) as blk:
            for eng in self.ENGINES:
                if not self.ops[eng]:
                    continue
                lst = self.ops[eng]

                def body(e, lst=lst):
                    for fn, waits, incs in lst:
                        for sname, v in waits:
                            if v > 0:
                                e.wait_ge(sems[sname], v)
                        inst = fn(e) if fn is not None else None
                        for sname, n in incs:
                            inst.then_inc(sems[sname], n)
                getattr(blk, eng)(body)


class Builder:
    def __init__(self):
        self.nc = bass.Bass("TRN2", target_bir_lowering=False,
                            detect_race_conditions=False)
        self.es = ExitStack()
        self.C = {}
        nc = self.nc
        self.sems = {n: self.es.enter_context(nc.semaphore(n))
                     for n in ("sd", "sd2", "st", "sv", "ss", "scc", "sg")}
        dram = nc.dram_tensor
        self.d_hs = dram("hs_pad", [SEGH, H], F32, kind="ExternalInput")
        self.d_wqkv = dram("wqkvT", [H, CONV_DIM], F32, kind="ExternalInput")
        self.d_convw = dram("convw_p", [128, 64], F32, kind="ExternalInput")
        self.d_wz = dram("wzT", [H, VAL_DIM], F32, kind="ExternalInput")
        self.d_wab = dram("wabT", [H, 32], F32, kind="ExternalInput")
        self.d_wout = dram("woutT", [VAL_DIM, H], F32, kind="ExternalInput")
        self.d_wffn = dram("wffnT", [H, H], F32, kind="ExternalInput")
        self.d_dtb = dram("dtb", [16], F32, kind="ExternalInput")
        self.d_alog = dram("alog", [16], F32, kind="ExternalInput")
        self.d_nwrep = dram("nw_rep", [128], F32, kind="ExternalInput")
        self.d_msel = dram("msel", [128, 32], F32, kind="ExternalInput")
        self.d_mc = dram("mc", [128, 4], F32, kind="ExternalInput")
        self.d_mcn = dram("mcn", [128, 4], F32, kind="ExternalInput")
        self.d_ident = dram("ident_bf", [128, 128], BF16, kind="ExternalInput")
        self.d_identB = dram("identB_bf", [128, 128], BF16, kind="ExternalInput")
        self.d_identF = dram("identF", [128, 128], F32, kind="ExternalInput")
        self.d_maskL = dram("maskL", [128, 128], BF16, kind="ExternalInput")
        self.d_maskU = dram("maskU", [128, 128], BF16, kind="ExternalInput")
        self.d_maskDW = dram("maskDW", [128, WIN * CKL], F32, kind="ExternalInput")
        self.d_ones16 = dram("ones16", [16, CKL], F32, kind="ExternalInput")
        self.d_out = dram("out_x2", [SEG, H], F32, kind="ExternalOutput")
        self.d_send = dram("s_end", [128, 512], F32, kind="ExternalOutput")
        self.d_cb = dram("cb_scr", [16, SEG], F32)
        self.d_gc = dram("gc_scr", [NCK * 16], F32)
        self.d_tst = dram("tst_scr", [128, NCK * NVH * 128], BF16)
        self.d_pst = dram("pst_scr", [128, NCK * NVH * 128], BF16)
        self.d_ktk = dram("ktk_scr", [128, NCK * NVH * 64], BF16)
        self.d_vtb = dram("vtb_scr", [128, NCK * NVH * 64], BF16)
        self.d_agin = dram("ag_in", [128, 512], F32)
        self.d_agout = dram("ag_out", [8 * 128, 512], F32)
        self.taps = {name: dram("tap_" + name, list(shape), F32,
                                kind="ExternalOutput")
                     for name, shape in DEBUG_TAPS.items()}

        sb, ps = self.sbuf, self.psumt
        self.ident = sb("ident", [128, 128], BF16)
        self.identB = sb("identB", [128, 128], BF16)
        self.identF = sb("identF", [128, 128])
        self.maskL = sb("maskL", [128, 128], BF16)
        self.maskU = sb("maskU", [128, 128], BF16)
        self.maskDW = sb("maskDW", [128, WIN * CKL])
        self.ones16 = sb("ones16", [16, CKL])
        self.dtb = sb("dtb", [16, 1])
        self.alog = sb("alog", [16, 1])
        self.negea = sb("negea", [16, 1])
        self.nwrep = sb("nwrep", [128, 1])
        self.convw = sb("convw", [128, 64])
        self.paramT = sb("paramT", [128, NCK * 112])
        self.gcrep = sb("gcrep", [128, NCK * 16])
        self.S_f = sb("S_f", [128, 512])
        self.S_b = sb("S_b", [128, 512], BF16)
        self.S_send = sb("S_send", [128, 512])
        self.msel = sb("msel", [128, 32])
        self.mc = sb("mc", [128, 4])
        self.mcn = sb("mcn", [128, 4])
        self.agbuf = sb("agbuf", [128, 512])
        self.zs = [sb(f"zs{i}", [128, VAL_DIM], BF16) for i in range(NTT)]
        self.ob = [sb(f"ob{i}", [128, VAL_DIM], BF16) for i in range(NTT)]
        self.pb = [ps(f"pb{i}", [128, 512], F32) for i in range(6)]
        self.tbk = [ps(f"tb{i}", [128, 512], F32) for i in range(2)]

    def sbuf(self, n, shape, dt=F32, es=None):
        return (es or self.es).enter_context(
            self.nc.sbuf_tensor("sb_" + n, shape, dt))

    def psumt(self, n, shape, dt=F32, es=None):
        return (es or self.es).enter_context(
            self.nc.psum_tensor("ps_" + n, shape, dt))

    def ptcol(self, ck, band, vh):
        c = 112 * ck + PT_OFF[band] + vh
        return self.paramT[:, c:c + 1]

    def build(self):
        es_y = ExitStack()
        self.yb = [self.sbuf(f"yb{i}", [128, SEG], BF16, es_y) for i in range(16)]
        with ExitStack() as es_fm:
            hsFM_flat = self.sbuf("hsFM", [128, 8 * SEGH], BF16, es_fm)
            self.hsFM = [hsFM_flat[:, i * SEGH:(i + 1) * SEGH] for i in range(8)]
            stage = self.sbuf("stage", [128, 1056], F32, es_fm)
            with ExitStack() as es_tm:
                hsTM = [self.sbuf(f"hsTM{i}", [128, H], F32, es_tm)
                        for i in range(2)]
                self.phase0(stage, hsTM)
            with ExitStack() as es_mx:
                wq = [self.sbuf("wq0", [128, 8 * 128], BF16, es_mx)]
                mixr = [self.sbuf("mixr0", [128, SEGH], BF16, es_mx)]
                yacc = self.sbuf("yacc", [128, 1024], F32, es_mx)
                yacc2 = self.sbuf("yacc2", [128, 1024], F32, es_mx)
                self.phase1_mixed(stage, wq, mixr, yacc, yacc2)
            with ExitStack() as es_z:
                wz = self.sbuf("wz", [128, 8 * 512], BF16, es_z)
                wab = self.sbuf("wab", [128, 8 * 32], BF16, es_z)
                sA = self.sbuf("zsA", [16, 512], F32, es_z)
                sB = self.sbuf("zsB", [16, 512], F32, es_z)
                sC = self.sbuf("zsC", [16, 512], F32, es_z)
                sD = self.sbuf("zsD", [16, 512], F32, es_z)
                sE = self.sbuf("zsE", [16, CKL], F32, es_z)
                sE2 = self.sbuf("zsE2", [16, CKL], F32, es_z)
                self.phase1_z_ab(stage, wz, wab, sA, sB, sC, sD, sE, sE2)
        with ExitStack() as es_scan:
            t = {}
            t["cbb"] = [self.sbuf(f"cbb{i}", [128, WIN * CKL], F32, es_scan)
                        for i in range(2)]
            t["dw"] = [self.sbuf(f"dw{i}", [128, WIN * CKL], F32, es_scan)
                       for i in range(2)]
            t["ep"] = [self.sbuf(f"ep{i}", [128, CKL], F32, es_scan)
                       for i in range(2)]
            t["chS"] = [self.sbuf(f"chS{i}", [128, 128], BF16, es_scan)
                        for i in range(4)]
            t["chN"] = [self.sbuf(f"chN{i}", [128, 128], BF16, es_scan)
                        for i in range(4)]
            t["chH"] = [self.sbuf(f"chH{i}", [128, 128], BF16, es_scan)
                        for i in range(4)]
            t["aqs"] = self.sbuf("aqs", [128, 128], BF16, es_scan)
            t["g1s"] = self.sbuf("g1s", [128, 128], BF16, es_scan)
            t["tst"] = self.sbuf("tst", [128, WIN * NVH * 128], BF16, es_scan)
            t["pst"] = self.sbuf("pst", [128, WIN * NVH * 128], BF16, es_scan)
            t["ktk"] = self.sbuf("ktk", [128, WIN * NVH * 64], BF16, es_scan)
            t["vtb"] = self.sbuf("vtb", [128, WIN * NVH * 64], BF16, es_scan)
            t["rt"] = [self.sbuf(f"rt{i}", [128, 64], BF16, es_scan)
                       for i in range(4)]
            t["wt"] = [self.sbuf(f"wt{i}", [128, 64], BF16, es_scan)
                       for i in range(4)]
            t["ot"] = [self.sbuf(f"ot{i}", [128, 64], F32, es_scan)
                       for i in range(4)]
            self.phase2_scan(t)
        es_y.close()
        with ExitStack() as es_post:
            p = {}
            p["wout"] = self.sbuf("wout", [128, 8 * H], BF16, es_post)
            p["wffn"] = self.sbuf("wffn", [128, 8 * H], BF16, es_post)
            p["stage2"] = self.sbuf("stage2", [128, H], F32, es_post)
            p["x1"] = [self.sbuf(f"x1_{i}", [128, H], F32, es_post)
                       for i in range(NTT)]
            p["htmp"] = self.sbuf("htmp", [128, H], F32, es_post)
            p["hsq"] = self.sbuf("hsq", [128, H], F32, es_post)
            p["hnb"] = self.sbuf("hnb", [128, H], BF16, es_post)
            p["hnF"] = self.sbuf("hnF", [128, 8 * 128], BF16, es_post)
            p["hsr"] = [self.sbuf(f"hsr{i}", [128, H], F32, es_post)
                        for i in range(2)]
            p["sm"] = [self.sbuf(f"sm{i}", [128, 1], F32, es_post)
                       for i in range(4)]
            p["rr"] = self.sbuf("rr", [128, 16], F32, es_post)
            self.phase3_post(p)
        self.es.close()
        return self.nc

    def phase0(self, stage, hsTM):
        sc, C = Sched(self.C), self.C
        for dst, src in [(self.ident, self.d_ident), (self.identB, self.d_identB),
                         (self.identF, self.d_identF),
                         (self.maskL, self.d_maskL), (self.maskU, self.d_maskU),
                         (self.maskDW, self.d_maskDW), (self.ones16, self.d_ones16),
                         (self.convw, self.d_convw), (self.msel, self.d_msel),
                         (self.mc, self.d_mc), (self.mcn, self.d_mcn)]:
            sc.dma("sync", dst[:, :], src.ap())
        sc.dma("sync", self.dtb[:, 0], self.d_dtb.ap())
        sc.dma("sync", self.alog[:, 0], self.d_alog.ap())
        sc.dma("sync", self.nwrep[:, 0], self.d_nwrep.ap())
        cdone = sc.cnt("sd")
        sc.emit("scalar", lambda e: e.activation(self.negea[:, :], self.alog[:, :],
                                                 AF.Exp),
                wait=[("sd", cdone)], inc=[("ss", 1)])
        sc.emit("vector", lambda e: e.tensor_scalar_mul(self.negea[:, :],
                                                        self.negea[:, :], -1.0),
                wait=[("ss", sc.cnt("ss"))], inc=[("sv", 1)])
        sc.emit("vector", lambda e: e.memset(self.S_send[:, :], 0.0),
                inc=[("sv", 1)])
        for i in range(NTT):
            sc.emit("vector", (lambda e, i=i: e.memset(self.ob[i][:, :], 0.0)),
                    inc=[("sv", 1)])

        sc.dma("sync", hsTM[0][0:3, :], self.d_hs.ap()[0:3, :])
        halo_sd = sc.cnt("sd")
        for f in range(8):
            pt = self.tbk[f % 2]
            w = [("sd", halo_sd), ("sd", cdone)] if f < 2 else \
                [("sv", C[f"_hfe{f - 2}"])]
            sc.emit("tensor",
                    (lambda e, f=f, pt=pt: e.transpose(
                        pt[:, 0:3], hsTM[0][0:3, f * 128:(f + 1) * 128],
                        self.identF[0:3, 0:3])),
                    wait=w, inc=[("st", 1)])
            sc.emit("vector",
                    (lambda e, f=f, pt=pt: e.tensor_copy(
                        self.hsFM[f][:, 0:3], pt[:, 0:3])),
                    wait=[("st", sc.cnt("st"))], inc=[("sv", 1)])
            C[f"_hfe{f}"] = sc.cnt("sv")
        halo_st = sc.cnt("st")
        for t in range(NTT):
            buf = hsTM[t % 2]
            w = [("st", C[f"_hst{t - 2}"])] if t >= 2 else \
                ([("st", halo_st)] if t == 0 else ())
            sc.dma("sync", buf[:, :],
                   self.d_hs.ap()[HLO + t * 128: HLO + (t + 1) * 128, :], wait=w)
            sd_here = sc.cnt("sd")
            for f in range(8):
                pt = self.tbk[(t * 8 + f) % 2]
                sc.emit("tensor",
                        (lambda e, f=f, buf=buf, pt=pt: e.transpose(
                            pt[:, 0:128], buf[:, f * 128:(f + 1) * 128],
                            self.identF[:, :])),
                        wait=[("sd", sd_here), ("sv", sc.cnt("sv") - 1)],
                        inc=[("st", 1)])
                sc.emit("vector",
                        (lambda e, f=f, t=t, pt=pt: e.tensor_copy(
                            self.hsFM[f][:, HLO + t * 128: HLO + (t + 1) * 128],
                            pt[:, 0:128])),
                        wait=[("st", sc.cnt("st"))], inc=[("sv", 1)])
            C[f"_hst{t}"] = sc.cnt("st")
        sc.realize(self.nc, self.sems, "p0")

    def phase1_mixed(self, stage, wq, mixr, yacc, yacc2):
        sc, C = Sched(self.C), self.C
        NT = [(0, 512), (512, 512), (1024, 512), (1536, 512), (2048, HLO)]
        for oc in range(16):
            wbuf = wq[0]
            for k in range(8):
                w = []
                if k == 0:
                    w = [("ss", C.get(f"_wqc{oc - 1}", 0)),
                         ("st", C.get(f"_mmq{oc - 1}", 0))]
                sc.dma("sync", stage[:, k * 128:(k + 1) * 128],
                       self.d_wqkv.ap()[k * 128:(k + 1) * 128,
                                        oc * 128:(oc + 1) * 128], wait=w)
                sc.emit("scalar",
                        (lambda e, k=k, wbuf=wbuf: e.copy(
                            wbuf[:, k * 128:(k + 1) * 128],
                            stage[:, k * 128:(k + 1) * 128])),
                        wait=([("sd", sc.cnt("sd"))] +
                              ([("st", C.get(f"_mmq{oc - 1}", 0))] if k == 0 else [])),
                        inc=[("ss", 1)])
            C[f"_wqc{oc}"] = sc.cnt("ss")
            wdone_ss = sc.cnt("ss")
            mbuf = mixr[0]
            for nt in range(5):
                bank = self.pb[nt % 2]
                c0, cn = NT[nt]
                for k in range(8):
                    w = []
                    if k == 0:
                        w = [("ss", wdone_ss),
                             ("sv", C.get(f"_mev{oc}_{nt - 2}", 0)),
                             ("sv", C.get(f"_cv{oc - 1}", 0))]
                    sc.emit("tensor",
                            (lambda e, k=k, wbuf=wbuf, bank=bank, c0=c0, cn=cn:
                             e.matmul(bank[:, 0:cn],
                                      wbuf[:, k * 128:(k + 1) * 128],
                                      self.hsFM[k][:, c0:c0 + cn],
                                      start=(k == 0), stop=(k == 7))),
                            wait=w, inc=([("st", 1)] if k == 7 else ()))
                sc.emit("vector",
                        (lambda e, bank=bank, mbuf=mbuf, c0=c0, cn=cn:
                         e.tensor_copy(mbuf[:, c0:c0 + cn], bank[:, 0:cn])),
                        wait=[("st", sc.cnt("st"))], inc=[("sv", 1)])
                C[f"_mev{oc}_{nt}"] = sc.cnt("sv")
            C[f"_mmq{oc}"] = sc.cnt("st")
            ceng, csem = "vector", "sv"
            yac = yacc if oc % 2 == 0 else yacc2
            for hf in range(2):
                h0 = hf * 1024
                sc.emit(ceng,
                        (lambda e, mbuf=mbuf, oc=oc, h0=h0, yac=yac:
                         e.tensor_scalar(
                            yac[:, :], mbuf[:, h0:h0 + 1024],
                            self.convw[:, 4 * oc:4 * oc + 1], None, ALU.mult)),
                        wait=[("ss", C.get(f"_silu{oc}_{hf - 1}",
                                           C.get(f"_silu{oc - 2}_1", 0)))],
                        inc=[(csem, 1)])
                for j in range(1, 4):
                    sc.emit(ceng,
                            (lambda e, mbuf=mbuf, oc=oc, j=j, h0=h0, yac=yac:
                             e.scalar_tensor_tensor(
                                 yac[:, :], mbuf[:, h0 + j:h0 + j + 1024],
                                 self.convw[:, 4 * oc + j:4 * oc + j + 1],
                                 yac[:, :], ALU.mult, ALU.add)),
                            inc=[(csem, 1)])
                C[f"_cv{oc}"] = sc.cnt(csem)
                sc.emit("scalar",
                        (lambda e, oc=oc, h0=h0, yac=yac: e.activation(
                            self.yb[oc][:, h0:h0 + 1024], yac[:, :], AF.Silu)),
                        wait=[(csem, sc.cnt(csem))], inc=[("ss", 1)])
                C[f"_silu{oc}_{hf}"] = sc.cnt("ss")
        sc.realize(self.nc, self.sems, "p1m")

    def phase1_z_ab(self, stage, wz, wab, sA, sB, sC, sD, sE, sE2):
        sc, C = Sched(self.C), self.C
        for k in range(8):
            sc.dma("sync", stage[:, 1024:1056],
                   self.d_wab.ap()[k * 128:(k + 1) * 128, :],
                   wait=[("sv", sc.cnt("sv"))])
            sc.emit("vector",
                    (lambda e, k=k: e.tensor_copy(
                        wab[:, k * 32:(k + 1) * 32], stage[:, 1024:1056])),
                    wait=[("sd", sc.cnt("sd"))], inc=[("sv", 1)])

        for hf in range(2):
            for k in range(8):
                w = [("sv", sc.cnt("sv"))] if k > 0 else \
                    [("st", C.get("_zmm", 0))]
                sc.dma("sync", stage[:, 0:512],
                       self.d_wz.ap()[k * 128:(k + 1) * 128,
                                      hf * 512:hf * 512 + 512], wait=w)
                sc.emit("vector",
                        (lambda e, k=k: e.tensor_copy(
                            wz[:, k * 512:(k + 1) * 512], stage[:, 0:512])),
                        wait=([("sd", sc.cnt("sd"))] +
                              ([("st", C.get("_zmm", 0))] if k == 0 else [])),
                        inc=[("sv", 1)])
            wdone = sc.cnt("sv")
            for tt in range(NTT):
                bank = self.pb[tt % 2]
                for k in range(8):
                    w = []
                    if k == 0:
                        w = [("sv", wdone),
                             ("ss", C.get(f"_zev{hf}_{tt - 2}", 0))]
                    sc.emit("tensor",
                            (lambda e, k=k, tt=tt, bank=bank:
                             e.matmul(bank[:, 0:512],
                                      self.hsFM[k][:, HLO + tt * 128:
                                                   HLO + (tt + 1) * 128],
                                      wz[:, k * 512:(k + 1) * 512],
                                      start=(k == 0), stop=(k == 7))),
                            wait=w, inc=([("st", 1)] if k == 7 else ()))
                sc.emit("scalar",
                        (lambda e, tt=tt, hf=hf, bank=bank: e.activation(
                            self.zs[tt][:, hf * 512:hf * 512 + 512],
                            bank[:, 0:512], AF.Silu)),
                        wait=[("st", sc.cnt("st"))], inc=[("ss", 1)])
                C[f"_zev{hf}_{tt}"] = sc.cnt("ss")
            C["_zmm"] = sc.cnt("st")

        for nt in range(4):
            c0 = nt * 512
            for k in range(8):
                w = []
                if k == 0:
                    w = [("ss", C.get(f"_abev{nt - 1}", 0))]
                sc.emit("tensor",
                        (lambda e, k=k, c0=c0: e.matmul(
                            self.pb[4][0:16, 0:512],
                            wab[:, k * 32:k * 32 + 16],
                            self.hsFM[k][:, HLO + c0: HLO + c0 + 512],
                            start=(k == 0), stop=(k == 7))),
                        wait=w, inc=([("st", 1)] if k == 7 else ()))
            for k in range(8):
                sc.emit("tensor",
                        (lambda e, k=k, c0=c0: e.matmul(
                            self.pb[5][0:16, 0:512],
                            wab[:, k * 32 + 16:k * 32 + 32],
                            self.hsFM[k][:, HLO + c0: HLO + c0 + 512],
                            start=(k == 0), stop=(k == 7))),
                        inc=([("st", 1)] if k == 7 else ()))
            st_here = sc.cnt("st")
            sc.emit("scalar",
                    (lambda e: e.activation(
                        sA[:, :], self.pb[4][0:16, 0:512], AF.Identity,
                        bias=self.dtb[:, 0:1])),
                    wait=[("st", st_here),
                          ("sv", C.get(f"_pq{nt - 1}", 0)),
                          ("sd2", C.get(f"_pqd{nt - 1}", 0))],
                    inc=[("ss", 1)])
            sc.emit("scalar",
                    (lambda e: e.activation(
                        sB[:, :], self.pb[5][0:16, 0:512], AF.Sigmoid)),
                    inc=[("ss", 1)])
            ab_s = sc.cnt("ss")
            sc.emit("scalar", lambda e: e.activation(sC[:, :], sA[:, :], AF.Abs),
                    inc=[("ss", 1)])
            sc.emit("scalar", lambda e: e.activation(sC[:, :], sC[:, :], AF.Exp,
                                                     scale=-1.0), inc=[("ss", 1)])
            sc.emit("scalar", lambda e: e.activation(sC[:, :], sC[:, :], AF.Ln,
                                                     bias=1.0), inc=[("ss", 1)])
            sp_s = sc.cnt("ss")
            sc.emit("vector", lambda e: e.tensor_scalar_max(sD[:, :], sA[:, :],
                                                            0.0),
                    wait=[("ss", sp_s)], inc=[("sv", 1)])
            sc.emit("vector", lambda e: e.tensor_add(sD[:, :], sD[:, :],
                                                     sC[:, :]),
                    inc=[("sv", 1)])
            sc.emit("vector", lambda e: e.tensor_scalar_mul(
                sD[:, :], sD[:, :], self.negea[:, 0:1]), inc=[("sv", 1)])
            for q in range(4):
                sc.emit("vector",
                        (lambda e, q=q: e.tensor_tensor_scan(
                            sA[:, q * CKL:(q + 1) * CKL], self.ones16[:, :],
                            sD[:, q * CKL:(q + 1) * CKL], 0.0,
                            ALU.mult, ALU.add)),
                        inc=[("sv", 1)])
            c_v = sc.cnt("sv")
            sc.emit("scalar", lambda e: e.activation(sC[:, :], sB[:, :], AF.Ln),
                    wait=[("sv", c_v)], inc=[("ss", 1)])
            sc.emit("vector", lambda e: e.tensor_add(sC[:, :], sA[:, :],
                                                     sC[:, :]),
                    wait=[("ss", sc.cnt("ss"))], inc=[("sv", 1)])
            sc.dma("scalar", self.d_cb.ap()[:, c0:c0 + 512], sC[:, :],
                   sem="sd2", wait=[("sv", sc.cnt("sv"))])
            for q in range(4):
                ck = nt * 4 + q
                ce = q * CKL + CKL - 1
                sc.dma("scalar", self.d_gc.ap()[ck * 16:(ck + 1) * 16],
                       sA[:, ce:ce + 1], sem="sd2")
            cb_sd2 = sc.cnt("sd2")
            sc.emit("scalar", lambda e: e.activation(sD[:, :], sA[:, :], AF.Exp),
                    inc=[("ss", 1)])
            sc.emit("vector", lambda e: e.tensor_mul(sC[:, :], sB[:, :],
                                                     sD[:, :]),
                    wait=[("ss", sc.cnt("ss")), ("sd2", cb_sd2)],
                    inc=[("sv", 1)])
            bands_v, bands_s = sc.cnt("sv"), sc.cnt("ss")
            for q in range(4):
                ck = nt * 4 + q
                cs = slice(q * CKL, (q + 1) * CKL)
                ce = q * CKL + CKL - 1
                sc.emit("vector",
                        (lambda e, cs=cs: e.tensor_scalar_mul(
                            sE[:, :], sB[:, cs], 8.0)),
                        wait=[("sv", bands_v), ("ss", bands_s),
                              ("st", C.get(f"_ptt{ck - 2}", 0))],
                        inc=[("sv", 1)])
                sc.emit("vector",
                        (lambda e: e.reciprocal(sE[:, :], sE[:, :])),
                        wait=[("sv", sc.cnt("sv"))], inc=[("sv", 1)])
                sc.emit("vector",
                        (lambda e, cs=cs, ce=ce: e.tensor_scalar(
                            sE2[:, :], sA[:, cs], sA[:, ce:ce + 1], None,
                            ALU.subtract)),
                        inc=[("sv", 1)])
                sc.emit("scalar",
                        (lambda e: e.activation(sE2[:, :], sE2[:, :], AF.Exp,
                                                scale=-1.0)),
                        wait=[("sv", sc.cnt("sv"))], inc=[("ss", 1)])
                small_v, small_s = sc.cnt("sv"), sc.cnt("ss")
                srcs = [(sA[:, cs], "negc", -1.0), (sB[:, cs], "beta", 1.0),
                        (sC[:, cs], "nbg", -1.0), (sD[:, cs], "g8", 0.125),
                        (sE[:, :], "r8", 1.0), (sE2[:, :], "kap", 1.0)]
                pt = self.tbk[0]
                for i, (src, band, scale) in enumerate(srcs):
                    sc.emit("tensor",
                            (lambda e, src=src, i=i, pt=pt: e.transpose(
                                pt[:, i * 16:(i + 1) * 16], src,
                                self.identF[0:16, 0:16])),
                            wait=([("sv", small_v), ("ss", small_s),
                                   ("sv", C.get(f"_pte{ck - 1}", 0))]
                                  if i == 0 else ()),
                            inc=([("st", 1)] if i == 5 else ()))
                C[f"_ptt{ck}"] = sc.cnt("st")
                for i, (src, band, scale) in enumerate(srcs):
                    col = 112 * ck + PT_OFF[band]
                    sc.emit("vector",
                            (lambda e, col=col, i=i, scale=scale, pt=pt:
                             e.tensor_scalar_mul(
                                 self.paramT[:, col:col + 16],
                                 pt[:, i * 16:(i + 1) * 16], scale)),
                            wait=([("st", C[f"_ptt{ck}"])] if i == 0 else ()),
                            inc=[("sv", 1)])
                C[f"_pte{ck}"] = sc.cnt("sv")
            C[f"_pq{nt}"] = sc.cnt("sv")
            C[f"_pqd{nt}"] = sc.cnt("sd2")
        sc.dma("scalar", self.gcrep[:, :],
               bass.AP(self.d_gc, 0, [[0, 128], [1, NCK * 16]]), sem="sd2",
               wait=[("sd2", sc.cnt("sd2"))])
        sc.emit("scalar", lambda e: e.activation(self.gcrep[:, :],
                                                 self.gcrep[:, :], AF.Exp),
                wait=[("sd2", sc.cnt("sd2"))], inc=[("ss", 1)])
        if "paramT" in self.taps:
            sc.dma("sync", self.taps["paramT"].ap(), self.paramT[:, :],
                   wait=[("sv", sc.cnt("sv")), ("ss", sc.cnt("ss"))])
        if "zraw" in self.taps:
            sc.dma("sync", self.taps["zraw"].ap(),
                   self.zs[0].bitcast(F32)[:, 0:512],
                   wait=[("sv", sc.cnt("sv")), ("ss", sc.cnt("ss"))])
        sc.realize(self.nc, self.sems, "p1z")

    def phase2_scan(self, t):
        sc, C = Sched(self.C), self.C
        pb = self.pb

        def kslice(kh, ck):
            tile, r0 = self.yb[4 + kh // 2], 64 * (kh % 2)
            return tile[r0:r0 + 64, ck * CKL:(ck + 1) * CKL]

        def qslice(kh, ck):
            tile, r0 = self.yb[0 + kh // 2], 64 * (kh % 2)
            return tile[r0:r0 + 64, ck * CKL:(ck + 1) * CKL]

        def vslice(vh, ck):
            tile, r0 = self.yb[8 + vh // 2], 64 * (vh % 2)
            return tile[r0:r0 + 64, ck * CKL:(ck + 1) * CKL]

        def wslot(ck, vh, buf, width):
            i = (ck % WIN) * NVH + vh
            return buf[:, i * width:(i + 1) * width]

        for wk in range(NCK // WIN):
            cks = [wk * WIN + i for i in range(WIN)]
            first_v_gate = [("st", C.get("_p2st", 0)),
                            ("sd", C.get("_spill_sd", 0))]
            for hp in range(NKH):
                kh = hp
                r0 = 64 * (kh % 2)
                for j, vh in enumerate((2 * hp, 2 * hp + 1)):
                    sc.dma("scalar", t["cbb"][j][:, :],
                           bass.AP(self.d_cb, vh * SEG + cks[0] * CKL,
                                   [[0, 128], [1, WIN * CKL]]),
                           sem="sd2",
                           wait=[("sv", C.get(f"_dwv{hp - 1}_{j}", 0))])
                    sc.emit("vector",
                            (lambda e, j=j: e.tensor_add(
                                t["dw"][j][:, :], t["cbb"][j][:, :],
                                self.maskDW[:, :])),
                            wait=(first_v_gate if (hp == 0 and j == 0) else []) +
                                 [("sd2", sc.cnt("sd2")),
                                  ("ss", C.get(f"_eps{hp - 1}_{j}", 0))],
                            inc=[("sv", 1)])
                    C[f"_dwv{hp}_{j}"] = sc.cnt("sv")
                dw_v = sc.cnt("sv")
                for wi, ck in enumerate(cks):
                    aw = [("sv", C.get("_pfree", 0))]
                    sc.emit("tensor",
                            (lambda e, kh=kh, ck=ck: e.matmul(
                                pb[0][:, 0:128], kslice(kh, ck), kslice(kh, ck),
                                start=True, stop=True)),
                            wait=aw, inc=[("st", 1)])
                    sc.emit("tensor",
                            (lambda e, kh=kh, ck=ck: e.matmul(
                                pb[1][:, 0:128], kslice(kh, ck), qslice(kh, ck),
                                start=True, stop=True)),
                            inc=[("st", 1)])
                    sc.emit("tensor",
                            (lambda e, kh=kh, ck=ck, r0=r0: e.transpose(
                                self.tbk[0].bitcast(BF16)[:, 0:64],
                                kslice(kh, ck),
                                self.identB[r0:r0 + 64, 0:64])),
                            inc=[("st", 1)])
                    a_st = sc.cnt("st")
                    sc.emit("vector",
                            (lambda e: e.tensor_tensor(
                                t["g1s"][:, :], pb[0][:, 0:128],
                                self.maskU[:, :], ALU.mult)),
                            wait=[("st", a_st)], inc=[("sv", 1)])
                    sc.emit("vector",
                            (lambda e: e.tensor_copy(t["aqs"][:, :],
                                                     pb[1][:, 0:128])),
                            inc=[("sv", 1)])
                    for j, vh in enumerate((2 * hp, 2 * hp + 1)):
                        sc.emit("tensor",
                                (lambda e, vh=vh, ck=ck, j=j: e.transpose(
                                    self.tbk[1].bitcast(BF16)[:, j * 64:(j + 1) * 64],
                                    vslice(vh, ck),
                                    self.identB[64 * (vh % 2):64 * (vh % 2) + 64,
                                                0:64])),
                                wait=([("sv", C.get("_vtf", 0))] if j == 0 else ()),
                                inc=[("st", 1)])
                    vt_st = sc.cnt("st")
                    ep_s = {}
                    for j, vh in enumerate((2 * hp, 2 * hp + 1)):
                        sc.emit("scalar",
                                (lambda e, wi=wi, ck=ck, vh=vh, j=j:
                                 e.activation(
                                     t["ep"][j][:, :],
                                     t["dw"][j][:, wi * CKL:(wi + 1) * CKL],
                                     AF.Exp, bias=self.ptcol(ck, "negc", vh))),
                                wait=[("sv", dw_v),
                                      ("sv", C.get(f"_epf{j}", 0))],
                                inc=[("ss", 1)])
                        ep_s[j] = sc.cnt("ss")
                        C[f"_eps{hp}_{j}"] = ep_s[j]
                        sc.emit("vector",
                                (lambda e, ck=ck, vh=vh, j=j:
                                 e.scalar_tensor_tensor(
                                    t["chN"][2 * j][:, :], pb[0][:, 0:128],
                                    self.ptcol(ck, "beta", vh),
                                    self.maskL[:, :], ALU.mult, ALU.mult)),
                                wait=[("st", a_st),
                                      ("st", C.get("_chfree_st", 0))],
                                inc=[("sv", 1)])
                        sc.emit("vector",
                                (lambda e, ck=ck, vh=vh: e.tensor_scalar(
                                    wslot(ck, vh, t["ktk"], 64),
                                    self.tbk[0].bitcast(BF16)[:, 0:64],
                                    self.ptcol(ck, "kap", vh), None, ALU.mult)),
                                inc=[("sv", 1)])
                        sc.emit("vector",
                                (lambda e, ck=ck, vh=vh, j=j: e.tensor_scalar(
                                    wslot(ck, vh, t["vtb"], 64),
                                    self.tbk[1].bitcast(BF16)[:, j * 64:(j + 1) * 64],
                                    self.ptcol(ck, "beta", vh), None, ALU.mult)),
                                wait=([("st", vt_st)] if j == 0 else ()),
                                inc=[("sv", 1)])
                    C["_vtf"] = sc.cnt("sv")
                    bld_v = sc.cnt("sv")
                    bld_ss = sc.cnt("ss")

                    CB = {0: (pb[2], pb[3], pb[4]), 1: (pb[1], pb[5], pb[0])}
                    for j in (0, 1):
                        bN, bS, bH = CB[j]
                        sc.emit("tensor",
                                (lambda e, j=j, bN=bN: e.matmul(
                                    bN[:, 0:128], t["g1s"][:, :],
                                    t["chN"][2 * j][:, :],
                                    start=True, stop=True)),
                                wait=[("sv", bld_v), ("ss", bld_ss)],
                                inc=[("st", 1)])
                        sc.emit("tensor",
                                (lambda e, j=j, bS=bS: e.matmul(
                                    bS[:, 0:128], t["chN"][2 * j][:, :],
                                    t["g1s"][:, :], start=True, stop=True)),
                                inc=[("st", 1)])
                        C[f"_cl1{j}"] = sc.cnt("st")
                    for j in (0, 1):
                        bN, bS, bH = CB[j]
                        vh = 2 * hp + j
                        sc.emit("vector",
                                (lambda e, ck=ck, vh=vh, j=j, bN=bN:
                                 e.tensor_scalar(
                                    t["chN"][2 * j + 1][:, :], bN[:, 0:128],
                                    self.ptcol(ck, "beta", vh), None, ALU.mult)),
                                wait=[("st", C[f"_cl1{j}"])], inc=[("sv", 1)])
                        sc.emit("vector",
                                (lambda e, j=j, bS=bS: e.tensor_copy(
                                    t["chS"][2 * j + 1][:, :], bS[:, 0:128])),
                                inc=[("sv", 1)])
                        sc.emit("vector",
                                (lambda e, j=j, bS=bS: e.tensor_sub(
                                    t["chH"][2 * j + 1][:, :], bS[:, 0:128],
                                    t["g1s"][:, :])),
                                inc=[("sv", 1)])
                        C[f"_cle{j}"] = sc.cnt("sv")
                        C[f"_cles{j}"] = sc.cnt("ss")
                    cur = {0: 1, 1: 1}
                    for m in (2, 4, 8, 16, 32):
                        for j in (0, 1):
                            bN, bS, bH = CB[j]
                            cu = cur[j]
                            sc.emit("tensor",
                                    (lambda e, j=j, cu=cu, bN=bN: e.matmul(
                                        bN[:, 0:128], t["chS"][2 * j + cu][:, :],
                                        t["chN"][2 * j + cu][:, :],
                                        start=True, stop=True)),
                                    wait=[("sv", C[f"_cle{j}"])],
                                    inc=[("st", 1)])
                            sc.emit("tensor",
                                    (lambda e, j=j, cu=cu, bS=bS: e.matmul(
                                        bS[:, 0:128], t["chN"][2 * j + cu][:, :],
                                        t["chS"][2 * j + cu][:, :],
                                        start=True, stop=True)),
                                    inc=[("st", 1)])
                            sc.emit("tensor",
                                    (lambda e, j=j, cu=cu, bH=bH: e.matmul(
                                        bH[:, 0:128], t["chN"][2 * j + cu][:, :],
                                        t["chH"][2 * j + cu][:, :],
                                        start=True, stop=True)),
                                    inc=[("st", 1)])
                            C[f"_clm{j}"] = sc.cnt("st")
                        for j in (0, 1):
                            bN, bS, bH = CB[j]
                            cu = cur[j]
                            nx = 1 - cu
                            vh = 2 * hp + j
                            sc.emit("vector",
                                    (lambda e, ck=ck, vh=vh, j=j, nx=nx, bN=bN:
                                     e.tensor_scalar(
                                        t["chN"][2 * j + nx][:, :], bN[:, 0:128],
                                        self.ptcol(ck, "beta", vh), None,
                                        ALU.mult)),
                                    wait=[("st", C[f"_clm{j}"])], inc=[("sv", 1)])
                            sc.emit("vector",
                                    (lambda e, j=j, nx=nx, bS=bS: e.tensor_copy(
                                        t["chS"][2 * j + nx][:, :], bS[:, 0:128])),
                                    inc=[("sv", 1)])
                            sc.emit("vector",
                                    (lambda e, j=j, cu=cu, nx=nx, bH=bH:
                                     e.tensor_add(
                                        t["chH"][2 * j + nx][:, :], bH[:, 0:128],
                                        t["chH"][2 * j + cu][:, :])),
                                    inc=[("sv", 1)])
                            C[f"_cle{j}"] = sc.cnt("sv")
                            cur[j] = nx
                    for j in (0, 1):
                        bN, bS, bH = CB[j]
                        cu = cur[j]
                        sc.emit("tensor",
                                (lambda e, j=j, cu=cu, bH=bH: e.matmul(
                                    bH[:, 0:128], t["chN"][2 * j + cu][:, :],
                                    t["chH"][2 * j + cu][:, :],
                                    start=True, stop=True)),
                                wait=[("sv", C[f"_cle{j}"])],
                                inc=[("st", 1)])
                        C[f"_clf{j}"] = sc.cnt("st")
                    C["_chfree_st"] = sc.cnt("st")
                    for j in (0, 1):
                        bN, bS, bH = CB[j]
                        cu = cur[j]
                        vh = 2 * hp + j
                        sc.emit("vector",
                                (lambda e, j=j, cu=cu, bH=bH: e.tensor_add(
                                    t["chH"][2 * j + (1 - cu)][:, :],
                                    bH[:, 0:128],
                                    t["chH"][2 * j + cu][:, :])),
                                wait=[("st", C[f"_clf{j}"])], inc=[("sv", 1)])
                        sc.emit("vector",
                                (lambda e, ck=ck, vh=vh, j=j, cu=cu:
                                 e.tensor_tensor(
                                     wslot(ck, vh, t["tst"], 128),
                                     t["ep"][j][:, :],
                                     t["chH"][2 * j + (1 - cu)][:, :],
                                     ALU.mult)),
                                wait=[("ss", ep_s[j]), ("sv", sc.cnt("sv"))],
                                inc=[("sv", 1)])
                        sc.emit("vector",
                                (lambda e, ck=ck, vh=vh, j=j: e.tensor_tensor(
                                    wslot(ck, vh, t["pst"], 128),
                                    t["ep"][j][:, :],
                                    t["aqs"][:, :], ALU.mult)),
                                inc=[("sv", 1)])
                        C[f"_epf{j}"] = sc.cnt("sv")
                    C["_pfree"] = sc.cnt("sv")

            spill_gate = [("sv", sc.cnt("sv"))]
            for buf, dscr, w in ((t["tst"], self.d_tst, 128),
                                 (t["pst"], self.d_pst, 128),
                                 (t["ktk"], self.d_ktk, 64),
                                 (t["vtb"], self.d_vtb, 64)):
                sc.dma("sync", dscr.ap()[:, wk * WIN * NVH * w:
                                         (wk + 1) * WIN * NVH * w],
                       buf[:, :], wait=spill_gate)
            C["_spill_sd"] = sc.cnt("sd")
            C["_pfree"] = sc.cnt("sv")
            C["_p2st"] = sc.cnt("st")

        sc.realize(self.nc, self.sems, "p2a")
        sc = Sched(self.C)
        for rnd in range(4):
            if rnd > 0:
                sc.dma("scalar", self.d_agin.ap(), self.S_send[:, :],
                       sem="sd2", wait=[("sv", sc.cnt("sv"))])

                def ccfn(e):
                    return e.collective_compute(
                        "AllGather", ALU.bypass,
                        replica_groups=[list(range(8))],
                        ins=[self.d_agin.ap().opt()],
                        outs=[self.d_agout.ap().opt()],
                    )
                sc.emit("gpsimd", ccfn, wait=[("sd2", sc.cnt("sd2"))],
                        inc=[("scc", 1)])
            sc.emit("vector", lambda e: e.memset(self.S_f[:, :], 0.0),
                    wait=[("sv", sc.cnt("sv")), ("st", sc.cnt("st"))],
                    inc=[("sv", 1)])
            if rnd > 0:
                for i in range(8):
                    sc.dma("scalar", self.agbuf[:, :],
                           self.d_agout.ap()[i * 128:(i + 1) * 128, :],
                           sem="sd2",
                           wait=([("scc", sc.cnt("scc"))] if i == 0 else
                                 [("sv", sc.cnt("sv"))]))
                    sc.emit("vector",
                            (lambda e, i=i, rnd=rnd: e.scalar_tensor_tensor(
                                self.S_f[:, :], self.agbuf[:, :],
                                self.msel[:, rnd * 8 + i:rnd * 8 + i + 1],
                                self.S_f[:, :], ALU.mult, ALU.add)),
                            wait=[("sd2", sc.cnt("sd2")),
                                  ("sv", sc.cnt("sv"))],
                            inc=[("sv", 1)])
            sc.emit("scalar", lambda e: e.copy(self.S_b[:, :], self.S_f[:, :]),
                    wait=[("sv", sc.cnt("sv"))], inc=[("ss", 1)])
            C["_rs_v"] = sc.cnt("sv")
            C["_rs_ss"] = sc.cnt("ss")

            for wk in range(NCK // WIN):
                cks = [wk * WIN + i for i in range(WIN)]
                ld_gate = [("st", sc.cnt("st")), ("sv", sc.cnt("sv"))]
                for buf, dscr, w in ((t["tst"], self.d_tst, 128),
                                     (t["pst"], self.d_pst, 128),
                                     (t["ktk"], self.d_ktk, 64),
                                     (t["vtb"], self.d_vtb, 64)):
                    sc.dma("sync", buf[:, :],
                           dscr.ap()[:, wk * WIN * NVH * w:
                                     (wk + 1) * WIN * NVH * w],
                           wait=ld_gate)
                slab_sd = sc.cnt("sd")
                first = [("sd", slab_sd), ("sv", C["_rs_v"]),
                         ("ss", C.get("_rs_ss", 0))]
                for ck in cks:
                    for g in range(NVH // 4):
                        vhs = [4 * g + i for i in range(4)]
                        banks = [(pb + self.tbk)[(4 * g + i) % 8]
                                 for i in range(4)]
                        for i, vh in enumerate(vhs):
                            kh, (r0, cb) = vh // 2, svh(vh)
                            bk = banks[i]
                            sc.emit("tensor",
                                    (lambda e, kh=kh, ck=ck, r0=r0, cb=cb, bk=bk:
                                     e.matmul(bk[:, 0:64], kslice(kh, ck),
                                              self.S_b[r0:r0 + 64, cb:cb + 64],
                                              start=True, stop=True)),
                                    wait=first +
                                         [("ss", C.get(f"_p2bs{ck}_{vh}", 0)),
                                          ("sv", C.get(f"_p2f{(ck - (0 if g >= 2 else 1)) % NCK}_{(vh + 8) % 16}", 0))],
                                    inc=[("st", 1)])
                            first = []
                            C[f"_p2ks{ck}_{vh}"] = sc.cnt("st")
                        for i, vh in enumerate(vhs):
                            bk = banks[i]
                            sc.emit("vector",
                                    (lambda e, ck=ck, vh=vh, i=i, bk=bk:
                                     e.scalar_tensor_tensor(
                                         t["rt"][i][:, :], bk[:, 0:64],
                                         self.ptcol(ck, "nbg", vh),
                                         wslot(ck, vh, t["vtb"], 64),
                                         ALU.mult, ALU.add)),
                                    wait=[("st", C[f"_p2ks{ck}_{vh}"])],
                                    inc=[("sv", 1)])
                            C[f"_p2r{ck}_{vh}"] = sc.cnt("sv")
                        for i, vh in enumerate(vhs):
                            bk = banks[i]
                            sc.emit("tensor",
                                    (lambda e, ck=ck, vh=vh, i=i, bk=bk: e.matmul(
                                        bk[:, 64:128],
                                        wslot(ck, vh, t["tst"], 128),
                                        t["rt"][i][:, :], start=True, stop=True)),
                                    wait=[("sv", C[f"_p2r{ck}_{vh}"])],
                                    inc=[("st", 1)])
                            C[f"_p2w{ck}_{vh}"] = sc.cnt("st")
                        for i, vh in enumerate(vhs):
                            bk = banks[i]
                            sc.emit("vector",
                                    (lambda e, i=i, bk=bk: e.tensor_add(
                                        t["wt"][i][:, :], bk[:, 64:128],
                                        t["rt"][i][:, :])),
                                    wait=[("st", C[f"_p2w{ck}_{vh}"])],
                                    inc=[("sv", 1)])
                            C[f"_p2we{ck}_{vh}"] = sc.cnt("sv")
                        for i, vh in enumerate(vhs):
                            kh, (r0, cb) = vh // 2, svh(vh)
                            bk = banks[i]
                            sc.emit("tensor",
                                    (lambda e, ck=ck, vh=vh, i=i, bk=bk: e.matmul(
                                        bk[:, 128:192],
                                        wslot(ck, vh, t["pst"], 128),
                                        t["wt"][i][:, :], start=True, stop=True)),
                                    wait=[("sv", C[f"_p2we{ck}_{vh}"])],
                                    inc=[("st", 1)])
                            sc.emit("tensor",
                                    (lambda e, kh=kh, ck=ck, r0=r0, cb=cb, bk=bk:
                                     e.matmul(
                                        bk[:, 192:256], qslice(kh, ck),
                                        self.S_b[r0:r0 + 64, cb:cb + 64],
                                        start=True, stop=True)),
                                    inc=[("st", 1)])
                            C[f"_p2o{ck}_{vh}"] = sc.cnt("st")
                        for i, vh in enumerate(vhs):
                            bk = banks[i]
                            sc.emit("vector",
                                    (lambda e, ck=ck, vh=vh, i=i, bk=bk:
                                     e.tensor_scalar(
                                        t["ot"][i][:, :], bk[:, 192:256],
                                        self.ptcol(ck, "g8", vh), None,
                                        ALU.mult)),
                                    wait=[("st", C[f"_p2o{ck}_{vh}"])],
                                    inc=[("sv", 1)])
                            sc.emit("vector",
                                    (lambda e, ck=ck, vh=vh, i=i, bk=bk:
                                     e.scalar_tensor_tensor(
                                        t["ot"][i][:, :], bk[:, 128:192],
                                        self.ptcol(ck, "r8", vh),
                                        t["ot"][i][:, :], ALU.mult, ALU.add)),
                                    wait=[("sv", sc.cnt("sv"))], inc=[("sv", 1)])
                            sc.emit("vector",
                                    (lambda e, ck=ck, vh=vh, i=i, rnd=rnd:
                                     e.scalar_tensor_tensor(
                                        self.ob[ck][:, vh * 64:(vh + 1) * 64],
                                        t["ot"][i][:, :],
                                        self.mc[:, rnd:rnd + 1],
                                        self.ob[ck][:, vh * 64:(vh + 1) * 64],
                                        ALU.mult, ALU.add)),
                                    wait=[("sv", sc.cnt("sv"))], inc=[("sv", 1)])
                            C[f"_p2oe{ck}_{vh}"] = sc.cnt("sv")
                        for i, vh in enumerate(vhs):
                            r0, cb = svh(vh)
                            bk = banks[i]
                            sc.emit("tensor",
                                    (lambda e, ck=ck, vh=vh, i=i, r0=r0, bk=bk:
                                     e.matmul(
                                        bk[r0:r0 + 64, 256:320],
                                        wslot(ck, vh, t["ktk"], 64),
                                        t["wt"][i][:, :], start=True, stop=True,
                                        tile_position=(0, r0))),
                                    wait=[("sv", C[f"_p2oe{ck}_{vh}"])],
                                    inc=[("st", 1)])
                            C[f"_p2s1{ck}_{vh}"] = sc.cnt("st")
                        for i, vh in enumerate(vhs):
                            r0, cb = svh(vh)
                            bk = banks[i]
                            sc.emit("vector",
                                    (lambda e, ck=ck, vh=vh, r0=r0, cb=cb, bk=bk:
                                     e.scalar_tensor_tensor(
                                        self.S_f[r0:r0 + 64, cb:cb + 64],
                                        self.S_f[r0:r0 + 64, cb:cb + 64],
                                        self.gcrep[r0:r0 + 64,
                                                   16 * ck + vh:16 * ck + vh + 1],
                                        bk[r0:r0 + 64, 256:320], ALU.mult,
                                        ALU.add)),
                                    wait=[("st", C[f"_p2s1{ck}_{vh}"])],
                                    inc=[("sv", 1)])
                            sc.emit("scalar",
                                    (lambda e, r0=r0, cb=cb: e.copy(
                                        self.S_b[r0:r0 + 64, cb:cb + 64],
                                        self.S_f[r0:r0 + 64, cb:cb + 64])),
                                    wait=[("sv", sc.cnt("sv"))], inc=[("ss", 1)])
                            C[f"_p2bs{ck + 1}_{vh}"] = sc.cnt("ss")
                            C[f"_p2f{ck}_{vh}"] = sc.cnt("sv")
            sc.emit("vector",
                    (lambda e, rnd=rnd: e.tensor_scalar(
                        self.agbuf[:, :], self.S_f[:, :],
                        self.mc[:, rnd:rnd + 1], None, ALU.mult)),
                    wait=[("sv", sc.cnt("sv"))], inc=[("sv", 1)])
            sc.emit("vector",
                    (lambda e, rnd=rnd: e.scalar_tensor_tensor(
                        self.S_send[:, :], self.S_send[:, :],
                        self.mcn[:, rnd:rnd + 1], self.agbuf[:, :],
                        ALU.mult, ALU.add)),
                    wait=[("sv", sc.cnt("sv"))], inc=[("sv", 1)])

        sc.dma("sync", self.d_send.ap(), self.S_send[:, :],
               wait=[("sv", sc.cnt("sv"))])
        if "sf" in self.taps:
            sc.dma("sync", self.taps["sf"].ap(), self.S_f[:, :],
                   wait=[("sv", sc.cnt("sv"))])
        sc.realize(self.nc, self.sems, "p2")

    def phase3_post(self, p):
        sc, C = Sched(self.C), self.C
        pb = self.pb
        wout, wffn, stage2 = p["wout"], p["wffn"], p["stage2"]
        self.wout_dbg = wout
        x1, htmp, hsq = p["x1"], p["htmp"], p["hsq"]
        hnb, hnF, hsr, sm, rr = p["hnb"], p["hnF"], p["hsr"], p["sm"], p["rr"]

        for k in range(8):
            w = [("sv", sc.cnt("sv"))] if k > 0 else ()
            sc.dma("sync", stage2[:, :],
                   self.d_wout.ap()[k * 128:(k + 1) * 128, :], wait=w)
            sc.emit("vector",
                    (lambda e, k=k: e.tensor_scalar(
                        wout[:, k * H:(k + 1) * H], stage2[:, :],
                        self.nwrep[:, 0:1], None, ALU.mult)),
                    wait=[("sd", sc.cnt("sd"))], inc=[("sv", 1)])
        wo_v = sc.cnt("sv")
        for k in range(8):
            w = [("sv", wo_v)] if k == 0 else [("ss", sc.cnt("ss"))]
            sc.dma("sync", stage2[:, :],
                   self.d_wffn.ap()[k * 128:(k + 1) * 128, :], wait=w)
            sc.emit("scalar",
                    (lambda e, k=k: e.sign(wffn[:, k * H:(k + 1) * H],
                                           stage2[:, :])),
                    wait=[("sd", sc.cnt("sd"))], inc=[("ss", 1)])
        wf_s = sc.cnt("ss")

        for tt in range(NTT):
            sc.emit("vector",
                    (lambda e, tt=tt: e.tensor_mul(htmp[:, :], self.ob[tt][:, :],
                                                   self.zs[tt][:, :])),
                    wait=[("sv", wo_v), ("ss", C.get(f"_lns{tt - 1}", 0)),
                          ("sd", C.get(f"_osd{tt - 1}", 0))],
                    inc=[("sv", 1)])
            h_v = sc.cnt("sv")
            sc.emit("scalar",
                    (lambda e: e.activation(hsq[:, :], htmp[:, :], AF.Square)),
                    wait=[("sv", h_v)], inc=[("ss", 1)])
            sc.emit("vector",
                    (lambda e: e.tensor_reduce(
                        rr[:, :], hsq[:, :].rearrange("p (h e) -> p h e", h=16),
                        AX.X, ALU.add)),
                    wait=[("ss", sc.cnt("ss"))], inc=[("sv", 1)])
            sc.emit("vector",
                    (lambda e: e.tensor_scalar(rr[:, :], rr[:, :], 1.0 / DV,
                                               RMS_EPS, ALU.mult, ALU.add)),
                    wait=[("sv", sc.cnt("sv"))], inc=[("sv", 1)])
            sc.emit("vector", lambda e: e.reciprocal(rr[:, :], rr[:, :]),
                    wait=[("sv", sc.cnt("sv"))], inc=[("sv", 1)])
            sc.emit("scalar", lambda e: e.activation(rr[:, :], rr[:, :], AF.Sqrt),
                    wait=[("sv", sc.cnt("sv"))], inc=[("ss", 1)])
            sc.emit("vector",
                    (lambda e: e.tensor_tensor(
                        hnb[:, :].rearrange("p (h e) -> p h e", h=16),
                        htmp[:, :].rearrange("p (h e) -> p h e", h=16),
                        rr[:, :].rearrange("p (h o) -> p h o", o=1)
                        .broadcast_to([128, 16, DV]), ALU.mult)),
                    wait=[("ss", sc.cnt("ss"))], inc=[("sv", 1)])
            hn_v = sc.cnt("sv")
            for f in range(8):
                sc.emit("tensor",
                        (lambda e, f=f: e.transpose(
                            self.tbk[0].bitcast(BF16)[:, f * 128:(f + 1) * 128],
                            hnb[:, f * 128:(f + 1) * 128], self.ident[:, :])),
                        wait=([("sv", hn_v)] if f == 0 else ()),
                        inc=([("st", 1)] if f == 7 else ()))
            sc.emit("vector",
                    (lambda e: e.tensor_copy(
                        hnF[:, :], self.tbk[0].bitcast(BF16)[:, 0:1024])),
                    wait=[("st", sc.cnt("st"))], inc=[("sv", 1)])
            hnf_v = sc.cnt("sv")
            sc.dma("sync", hsr[tt % 2][:, :],
                   self.d_hs.ap()[HLO + tt * 128: HLO + (tt + 1) * 128, :],
                   wait=[("sv", C.get(f"_x1v{tt - 2}", 0))])
            hs_sd = sc.cnt("sd")
            for hf in range(2):
                bank = pb[hf]
                for k in range(8):
                    w = [("sv", hnf_v)] if k == 0 else ()
                    sc.emit("tensor",
                            (lambda e, k=k, hf=hf, bank=bank: e.matmul(
                                bank[:, 0:512], hnF[:, k * 128:(k + 1) * 128],
                                wout[:, k * H + hf * 512:k * H + hf * 512 + 512],
                                start=(k == 0), stop=(k == 7))),
                            wait=w, inc=([("st", 1)] if k == 7 else ()))
                sc.emit("vector",
                        (lambda e, tt=tt, hf=hf, bank=bank: e.tensor_add(
                            x1[tt][:, hf * 512:hf * 512 + 512], bank[:, 0:512],
                            hsr[tt % 2][:, hf * 512:hf * 512 + 512])),
                        wait=[("st", sc.cnt("st")), ("sd", hs_sd)],
                        inc=[("sv", 1)])
            if tt == 4 and "hnb4" in self.taps:
                sc.dma("sync", self.taps["hnb4"].ap(),
                       hnb.bitcast(F32)[:, 0:512], wait=[("sv", hn_v)])
                sc.dma("sync", self.taps["hnf4"].ap(),
                       hnF.bitcast(F32)[:, 0:512], wait=[("sv", hnf_v)])
                sc.dma("sync", self.taps["x1p4"].ap(), x1[tt][:, :],
                       wait=[("sv", sc.cnt("sv"))])
                sc.dma("sync", self.taps["wout4"].ap(),
                       self.wout_dbg.bitcast(F32)[:, 0:512],
                       wait=[("sv", sc.cnt("sv"))])
                sc.emit("vector", None, wait=[("sd", sc.cnt("sd"))])
            self._layernorm(sc, x1[tt], hsq, sm)
            C[f"_x1v{tt}"] = sc.cnt("sv")
            C[f"_lns{tt}"] = sc.cnt("ss")
            C[f"_osd{tt}"] = 0
            if "x1" in self.taps and tt == 4:
                sc.dma("sync", self.taps["x1"].ap(), x1[tt][:, :],
                       wait=[("sv", sc.cnt("sv"))])

        for tt in range(NTT):
            x1v = C[f"_x1v{tt}"]
            for h4 in range(2):
                for f4 in range(4):
                    f = h4 * 4 + f4
                    sc.emit("tensor",
                            (lambda e, tt=tt, f=f, f4=f4: e.transpose(
                                self.tbk[1][:, f4 * 128:(f4 + 1) * 128],
                                x1[tt][:, f * 128:(f + 1) * 128],
                                self.identF[:, :])),
                            wait=([("sv", max(x1v, C.get("_x1fe", 0)))]
                                  if f4 == 0 else ()),
                            inc=([("st", 1)] if f4 == 3 else ()))
                sc.emit("vector",
                        (lambda e, h4=h4: e.tensor_copy(
                            hnF[:, h4 * 512:(h4 + 1) * 512],
                            self.tbk[1][:, 0:512])),
                        wait=[("st", sc.cnt("st"))], inc=[("sv", 1)])
                C["_x1fe"] = sc.cnt("sv")
            x1f_v = sc.cnt("sv")
            for hf in range(2):
                bank = pb[2 + hf]
                for k in range(8):
                    w = [("sv", x1f_v), ("ss", wf_s)] if k == 0 else ()
                    sc.emit("tensor",
                            (lambda e, k=k, hf=hf, bank=bank: e.matmul(
                                bank[:, 0:512], hnF[:, k * 128:(k + 1) * 128],
                                wffn[:, k * H + hf * 512:k * H + hf * 512 + 512],
                                start=(k == 0), stop=(k == 7))),
                            wait=w, inc=([("st", 1)] if k == 7 else ()))
                sc.emit("vector",
                        (lambda e, tt=tt, hf=hf, bank=bank: e.tensor_add(
                            htmp[:, hf * 512:hf * 512 + 512], bank[:, 0:512],
                            x1[tt][:, hf * 512:hf * 512 + 512])),
                        wait=[("st", sc.cnt("st")),
                              ("sd", C.get("_o2sd", 0)),
                              ("ss", C.get(f"_lns{NTT - 1}", 0))],
                        inc=[("sv", 1)])
            self._layernorm(sc, htmp, hsq, sm)
            sc.dma("sync", self.d_out.ap()[tt * 128:(tt + 1) * 128, :],
                   htmp[:, :], wait=[("sv", sc.cnt("sv"))])
            C["_o2sd"] = sc.cnt("sd")
        if "zraw2" in self.taps:
            sc.dma("sync", self.taps["zraw2"].ap(),
                   self.zs[0].bitcast(F32)[:, 0:512],
                   wait=[("sv", sc.cnt("sv"))])
        sc.emit("sync", lambda e: e.wait_ge(self.sems["sd"], sc.cnt("sd")))
        sc.emit("sync", lambda e: e.wait_ge(self.sems["sd2"], sc.cnt("sd2")))
        sc.realize(self.nc, self.sems, "p3")

    def _layernorm(self, sc, xt, scratch, sm):
        xt_v = sc.cnt("sv")
        sc.emit("vector",
                (lambda e: e.tensor_reduce(sm[0][:, :], xt[:, :], AX.X, ALU.add)),
                inc=[("sv", 1)])
        sc.emit("vector",
                (lambda e: e.tensor_scalar_mul(sm[0][:, :], sm[0][:, :], 1.0 / H)),
                wait=[("sv", sc.cnt("sv"))], inc=[("sv", 1)])
        sc.emit("scalar",
                (lambda e: e.activation(scratch[:, :], xt[:, :], AF.Square)),
                wait=[("sv", xt_v)], inc=[("ss", 1)])
        sc.emit("vector",
                (lambda e: e.tensor_reduce(sm[1][:, :], scratch[:, :], AX.X,
                                           ALU.add)),
                wait=[("ss", sc.cnt("ss"))], inc=[("sv", 1)])
        sc.emit("vector",
                (lambda e: e.tensor_scalar_mul(sm[1][:, :], sm[1][:, :], 1.0 / H)),
                wait=[("sv", sc.cnt("sv"))], inc=[("sv", 1)])
        sc.emit("vector",
                (lambda e: e.tensor_mul(sm[2][:, :], sm[0][:, :], sm[0][:, :])),
                wait=[("sv", sc.cnt("sv"))], inc=[("sv", 1)])
        sc.emit("vector",
                (lambda e: e.tensor_sub(sm[1][:, :], sm[1][:, :], sm[2][:, :])),
                wait=[("sv", sc.cnt("sv"))], inc=[("sv", 1)])
        sc.emit("vector",
                (lambda e: e.tensor_scalar_add(sm[1][:, :], sm[1][:, :], LN_EPS)),
                wait=[("sv", sc.cnt("sv"))], inc=[("sv", 1)])
        sc.emit("vector", (lambda e: e.reciprocal(sm[2][:, :], sm[1][:, :])),
                wait=[("sv", sc.cnt("sv"))], inc=[("sv", 1)])
        sc.emit("scalar", (lambda e: e.activation(sm[3][:, :], sm[2][:, :],
                                                  AF.Sqrt)),
                wait=[("sv", sc.cnt("sv"))], inc=[("ss", 1)])
        sc.emit("vector",
                (lambda e: e.tensor_scalar(xt[:, :], xt[:, :], sm[0][:, 0:1],
                                           sm[3][:, 0:1], ALU.subtract,
                                           ALU.mult)),
                wait=[("ss", sc.cnt("ss"))], inc=[("sv", 1)])



_NC_CACHE = {}


def _get_nc():
    if "main" not in _NC_CACHE:
        _NC_CACHE["main"] = Builder().build()
    return _NC_CACHE["main"]


def _consts_maps():
    i0, i1 = np.indices((128, 128))
    ident = np.eye(128, dtype=np.float32).astype(ml_dtypes.bfloat16)
    identB = (np.eye(128) + np.eye(128, k=-64)).astype(np.float32) \
        .astype(ml_dtypes.bfloat16)
    maskL = (i0 > i1).astype(np.float32).astype(ml_dtypes.bfloat16)
    maskU = (i0 < i1).astype(np.float32).astype(ml_dtypes.bfloat16)
    maskD = np.where(i0 <= i1, 0.0, NEG_BIG).astype(np.float32)
    maskDW = np.ascontiguousarray(np.tile(maskD, (1, WIN)))
    ones16 = np.ones((16, CKL), dtype=np.float32)
    identF = np.eye(128, dtype=np.float32)
    return {"ident_bf": ident, "identB_bf": identB, "identF": identF,
            "maskL": maskL, "maskU": maskU, "maskDW": maskDW,
            "ones16": ones16}


def kernel(L0_attn_hidden_states, L0_attn_W_qkv, L0_attn_conv_w, L0_attn_W_z,
           L0_attn_W_b, L0_attn_W_a, L0_attn_dt_bias, L0_attn_A_log,
           L0_attn_norm_weight, L0_attn_W_out, L0_ffn_weight):
    hs = np.ascontiguousarray(L0_attn_hidden_states, dtype=np.float32)
    wqkvT = np.ascontiguousarray(np.asarray(L0_attn_W_qkv, np.float32).T)
    convw = np.asarray(L0_attn_conv_w, np.float32)[:, 0, :]
    convw_p = np.zeros((128, 64), np.float32)
    for oc in range(16):
        convw_p[:, 4 * oc:4 * oc + 4] = convw[oc * 128:(oc + 1) * 128, :]
    wzT = np.ascontiguousarray(np.asarray(L0_attn_W_z, np.float32).T)
    wab = np.concatenate([np.asarray(L0_attn_W_a, np.float32),
                          np.asarray(L0_attn_W_b, np.float32)], axis=0)
    wabT = np.ascontiguousarray(wab.T)
    woutT = np.ascontiguousarray(np.asarray(L0_attn_W_out, np.float32).T)
    wffnT = np.ascontiguousarray(np.asarray(L0_ffn_weight, np.float32).T)
    dtb = np.ascontiguousarray(L0_attn_dt_bias, np.float32)
    alog = np.ascontiguousarray(L0_attn_A_log, np.float32)
    nw_rep = np.ascontiguousarray(
        np.tile(np.asarray(L0_attn_norm_weight, np.float32), 2))
    consts = _consts_maps()

    def shard_hs(core):
        b, q = core // 4, core % 4
        st = q * SEG
        pad = np.zeros((SEGH, H), np.float32)
        lo = max(0, st - HLO)
        pad[HLO - (st - lo):] = hs[b, lo:st + SEG]
        return pad

    base = [dict(hs_pad=shard_hs(c), wqkvT=wqkvT, convw_p=convw_p, wzT=wzT,
                 wabT=wabT, woutT=woutT, wffnT=wffnT, dtb=dtb, alog=alog,
                 nw_rep=nw_rep, **consts) for c in range(8)]

    nc = _get_nc()

    def masks(c):
        msel = np.zeros((128, 32), np.float32)
        r = c % 4
        if r > 0:
            msel[:, r * 8 + (c - 1)] = 1.0
        mc = np.zeros((128, 4), np.float32)
        mc[:, r] = 1.0
        mcn = 1.0 - mc
        return {"msel": msel, "mc": mc, "mcn": mcn}

    in_maps = [dict(base[c], **masks(c)) for c in range(8)]
    res = run_bass_kernel_spmd(nc, in_maps, core_ids=list(range(8)))
    out = np.empty((B, S, H), np.float32)
    for c in range(8):
        b, q = c // 4, c % 4
        out[b, q * SEG:(q + 1) * SEG] = res.results[c]["out_x2"]
    return out


# version 20 (final)
# speedup vs baseline: 3.9953x; 1.0276x over previous
import numpy as np
import ml_dtypes
from contextlib import ExitStack

import concourse.bass as bass
import concourse.mybir as mybir
from concourse.bass_utils import run_bass_kernel_spmd

F32 = mybir.dt.float32
BF16 = mybir.dt.bfloat16
AF = mybir.ActivationFunctionType
ALU = mybir.AluOpType
AX = mybir.AxisListType

B, S, H = 2, 8192, 1024
NKH, NVH, DK, DV = 8, 16, 64, 64
VAL_DIM = NVH * DV
CONV_DIM = 2 * NKH * DK + VAL_DIM
KSZ = 4
SEG, CKL = 2048, 128
NCK = SEG // CKL
NTT = SEG // 128
HLO = KSZ - 1
SEGH = SEG + HLO
WIN = 2
RMS_EPS, LN_EPS, NEG_BIG = 1e-6, 1e-5, -1e9

BNC, BCB, BB, BNBG, BKAP, BR8, BG8, BC = (i * 16 for i in range(8))
PT_OFF = {"negc": 0, "cb": 16, "beta": 32, "nbg": 48, "kap": 64, "r8": 80,
          "g8": 96}

DEBUG_TAPS = {}


def svh(vh):
    kh = vh // 2
    r0 = 64 * (kh % 2)
    cb = ((kh // 2) * 2 + (vh % 2)) * 64
    return r0, cb


class Sched:
    ENGINES = ("sync", "tensor", "vector", "scalar", "gpsimd")

    def __init__(self, counts):
        self.ops = {e: [] for e in self.ENGINES}
        self.counts = counts
        for k in ("sd", "sd2"):
            self.counts.setdefault("_mw_" + k, 0)
            self.counts.setdefault("_bar_" + k, 0)

    def cnt(self, sem):
        return self.counts.get(sem, 0)

    def emit(self, eng, fn, wait=(), inc=()):
        for sname, v in wait:
            if sname in ("sd", "sd2"):
                self.counts["_mw_" + sname] = max(
                    self.counts.get("_mw_" + sname, 0), v)
        for sname, n in inc:
            self.counts[sname] = self.counts.get(sname, 0) + n
        self.ops[eng].append((fn, tuple(wait), tuple(inc)))

    def dma(self, eng, out, in_, sem="sd", wait=()):
        wait = list(wait)
        pre = self.counts.get(sem, 0)
        if self.counts.get("_mw_" + sem, 0) > self.counts.get("_bar_" + sem, 0):
            wait.append((sem, pre))
            self.counts["_bar_" + sem] = pre

        def fn(e, out=out, in_=in_):
            return e.dma_start(out=out, in_=in_)
        self.emit(eng, fn, wait=wait, inc=((sem, 16),))

    def realize(self, nc, sems, name=None):
        import contextlib
        scope = nc.named_scope(name) if name else contextlib.nullcontext()
        with scope, nc.Block(name) as blk:
            for eng in self.ENGINES:
                if not self.ops[eng]:
                    continue
                lst = self.ops[eng]

                def body(e, lst=lst):
                    for fn, waits, incs in lst:
                        for sname, v in waits:
                            if v > 0:
                                e.wait_ge(sems[sname], v)
                        inst = fn(e) if fn is not None else None
                        for sname, n in incs:
                            inst.then_inc(sems[sname], n)
                getattr(blk, eng)(body)


class Builder:
    def __init__(self):
        self.nc = bass.Bass("TRN2", target_bir_lowering=False,
                            detect_race_conditions=False)
        self.es = ExitStack()
        self.C = {}
        nc = self.nc
        self.sems = {n: self.es.enter_context(nc.semaphore(n))
                     for n in ("sd", "sd2", "st", "sv", "ss", "scc", "sg")}
        dram = nc.dram_tensor
        self.d_hs = dram("hs_pad", [SEGH, H], F32, kind="ExternalInput")
        self.d_wqkv = dram("wqkvT", [H, CONV_DIM], F32, kind="ExternalInput")
        self.d_convw = dram("convw_p", [128, 64], F32, kind="ExternalInput")
        self.d_wz = dram("wzT", [H, VAL_DIM], F32, kind="ExternalInput")
        self.d_wab = dram("wabT", [H, 32], F32, kind="ExternalInput")
        self.d_wout = dram("woutT", [VAL_DIM, H], F32, kind="ExternalInput")
        self.d_wffn = dram("wffnT", [H, H], F32, kind="ExternalInput")
        self.d_dtb = dram("dtb", [16], F32, kind="ExternalInput")
        self.d_alog = dram("alog", [16], F32, kind="ExternalInput")
        self.d_nwrep = dram("nw_rep", [128], F32, kind="ExternalInput")
        self.d_msel = dram("msel", [128, 32], F32, kind="ExternalInput")
        self.d_mc = dram("mc", [128, 4], F32, kind="ExternalInput")
        self.d_mcn = dram("mcn", [128, 4], F32, kind="ExternalInput")
        self.d_ident = dram("ident_bf", [128, 128], BF16, kind="ExternalInput")
        self.d_identB = dram("identB_bf", [128, 128], BF16, kind="ExternalInput")
        self.d_identF = dram("identF", [128, 128], F32, kind="ExternalInput")
        self.d_maskL = dram("maskL", [128, 128], BF16, kind="ExternalInput")
        self.d_maskU = dram("maskU", [128, 128], BF16, kind="ExternalInput")
        self.d_maskDW = dram("maskDW", [128, WIN * CKL], F32, kind="ExternalInput")
        self.d_ones16 = dram("ones16", [16, CKL], F32, kind="ExternalInput")
        self.d_out = dram("out_x2", [SEG, H], F32, kind="ExternalOutput")
        self.d_send = dram("s_end", [128, 512], F32, kind="ExternalOutput")
        self.d_cb = dram("cb_scr", [16, SEG], F32)
        self.d_gc = dram("gc_scr", [NCK * 16], F32)
        self.d_tst = dram("tst_scr", [128, NCK * NVH * 128], BF16)
        self.d_pst = dram("pst_scr", [128, NCK * NVH * 128], BF16)
        self.d_ktk = dram("ktk_scr", [128, NCK * NVH * 64], BF16)
        self.d_vtb = dram("vtb_scr", [128, NCK * NVH * 64], BF16)
        self.d_agin = dram("ag_in", [128, 512], F32)
        self.d_agout = dram("ag_out", [8 * 128, 512], F32)
        self.taps = {name: dram("tap_" + name, list(shape), F32,
                                kind="ExternalOutput")
                     for name, shape in DEBUG_TAPS.items()}

        sb, ps = self.sbuf, self.psumt
        self.ident = sb("ident", [128, 128], BF16)
        self.identB = sb("identB", [128, 128], BF16)
        self.identF = sb("identF", [128, 128])
        self.maskL = sb("maskL", [128, 128], BF16)
        self.maskU = sb("maskU", [128, 128], BF16)
        self.maskDW = sb("maskDW", [128, WIN * CKL])
        self.ones16 = sb("ones16", [16, CKL])
        self.dtb = sb("dtb", [16, 1])
        self.alog = sb("alog", [16, 1])
        self.negea = sb("negea", [16, 1])
        self.nwrep = sb("nwrep", [128, 1])
        self.convw = sb("convw", [128, 64])
        self.paramT = sb("paramT", [128, NCK * 112])
        self.gcrep = sb("gcrep", [128, NCK * 16])
        self.S_f = sb("S_f", [128, 512])
        self.S_b = sb("S_b", [128, 512], BF16)
        self.S_send = sb("S_send", [128, 512])
        self.msel = sb("msel", [128, 32])
        self.mc = sb("mc", [128, 4])
        self.mcn = sb("mcn", [128, 4])
        self.agbuf = sb("agbuf", [128, 512])
        self.zs = [sb(f"zs{i}", [128, VAL_DIM], BF16) for i in range(NTT)]
        self.ob = [sb(f"ob{i}", [128, VAL_DIM], BF16) for i in range(NTT)]
        self.pb = [ps(f"pb{i}", [128, 512], F32) for i in range(6)]
        self.tbk = [ps(f"tb{i}", [128, 512], F32) for i in range(2)]

    def sbuf(self, n, shape, dt=F32, es=None):
        return (es or self.es).enter_context(
            self.nc.sbuf_tensor("sb_" + n, shape, dt))

    def psumt(self, n, shape, dt=F32, es=None):
        return (es or self.es).enter_context(
            self.nc.psum_tensor("ps_" + n, shape, dt))

    def ptcol(self, ck, band, vh):
        c = 112 * ck + PT_OFF[band] + vh
        return self.paramT[:, c:c + 1]

    def build(self):
        es_y = ExitStack()
        self.yb = [self.sbuf(f"yb{i}", [128, SEG], BF16, es_y) for i in range(16)]
        with ExitStack() as es_fm:
            hsFM_flat = self.sbuf("hsFM", [128, 8 * SEGH], BF16, es_fm)
            self.hsFM = [hsFM_flat[:, i * SEGH:(i + 1) * SEGH] for i in range(8)]
            stage = self.sbuf("stage", [128, 1056], F32, es_fm)
            with ExitStack() as es_tm:
                hsTM = [self.sbuf(f"hsTM{i}", [128, H], F32, es_tm)
                        for i in range(2)]
                self.phase0(stage, hsTM)
            with ExitStack() as es_mx:
                wq = [self.sbuf("wq0", [128, 8 * 128], BF16, es_mx)]
                mixr = [self.sbuf("mixr0", [128, SEGH], BF16, es_mx)]
                yacc = self.sbuf("yacc", [128, 1024], F32, es_mx)
                yacc2 = self.sbuf("yacc2", [128, 1024], F32, es_mx)
                self.phase1_mixed(stage, wq, mixr, yacc, yacc2)
            with ExitStack() as es_z:
                wz = self.sbuf("wz", [128, 8 * 512], BF16, es_z)
                wab = self.sbuf("wab", [128, 8 * 32], BF16, es_z)
                sA = self.sbuf("zsA", [16, 512], F32, es_z)
                sB = self.sbuf("zsB", [16, 512], F32, es_z)
                sC = self.sbuf("zsC", [16, 512], F32, es_z)
                sD = self.sbuf("zsD", [16, 512], F32, es_z)
                sE = self.sbuf("zsE", [16, CKL], F32, es_z)
                sE2 = self.sbuf("zsE2", [16, CKL], F32, es_z)
                self.phase1_z_ab(stage, wz, wab, sA, sB, sC, sD, sE, sE2)
        with ExitStack() as es_scan:
            t = {}
            t["cbb"] = [self.sbuf(f"cbb{i}", [128, WIN * CKL], F32, es_scan)
                        for i in range(2)]
            t["dw"] = [self.sbuf(f"dw{i}", [128, WIN * CKL], F32, es_scan)
                       for i in range(2)]
            t["ep"] = [self.sbuf(f"ep{i}", [128, CKL], F32, es_scan)
                       for i in range(2)]
            t["chS"] = [self.sbuf(f"chS{i}", [128, 128], BF16, es_scan)
                        for i in range(4)]
            t["chN"] = [self.sbuf(f"chN{i}", [128, 128], BF16, es_scan)
                        for i in range(4)]
            t["chH"] = [self.sbuf(f"chH{i}", [128, 128], BF16, es_scan)
                        for i in range(4)]
            t["aqs"] = self.sbuf("aqs", [128, 128], BF16, es_scan)
            t["g1s"] = self.sbuf("g1s", [128, 128], BF16, es_scan)
            t["tst"] = self.sbuf("tst", [128, WIN * NVH * 128], BF16, es_scan)
            t["pst"] = self.sbuf("pst", [128, WIN * NVH * 128], BF16, es_scan)
            t["ktk"] = self.sbuf("ktk", [128, WIN * NVH * 64], BF16, es_scan)
            t["vtb"] = self.sbuf("vtb", [128, WIN * NVH * 64], BF16, es_scan)
            t["rt"] = [self.sbuf(f"rt{i}", [128, 64], BF16, es_scan)
                       for i in range(4)]
            t["wt"] = [self.sbuf(f"wt{i}", [128, 64], BF16, es_scan)
                       for i in range(4)]
            t["ot"] = [self.sbuf(f"ot{i}", [128, 64], F32, es_scan)
                       for i in range(4)]
            self.phase2_scan(t)
        es_y.close()
        with ExitStack() as es_post:
            p = {}
            p["wout"] = self.sbuf("wout", [128, 8 * H], BF16, es_post)
            p["wffn"] = self.sbuf("wffn", [128, 8 * H], BF16, es_post)
            p["stage2"] = self.sbuf("stage2", [128, H], F32, es_post)
            p["x1"] = [self.sbuf(f"x1_{i}", [128, H], F32, es_post)
                       for i in range(4)]
            p["htmp"] = self.sbuf("htmp", [128, H], F32, es_post)
            p["hsq"] = self.sbuf("hsq", [128, H], F32, es_post)
            p["htmp2"] = self.sbuf("htmp2", [128, H], F32, es_post)
            p["hsq2"] = self.sbuf("hsq2", [128, H], F32, es_post)
            p["hnb"] = self.sbuf("hnb", [128, H], BF16, es_post)
            p["hnF"] = self.sbuf("hnF", [128, 8 * 128], BF16, es_post)
            p["hnF2"] = self.sbuf("hnF2", [128, 8 * 128], BF16, es_post)
            p["hsr"] = [self.sbuf(f"hsr{i}", [128, H], F32, es_post)
                        for i in range(2)]
            p["sm"] = [self.sbuf(f"sm{i}", [128, 1], F32, es_post)
                       for i in range(4)]
            p["sm2"] = [self.sbuf(f"sm2_{i}", [128, 1], F32, es_post)
                        for i in range(4)]
            p["rr"] = self.sbuf("rr", [128, 16], F32, es_post)
            self.phase3_post(p)
        self.es.close()
        return self.nc

    def phase0(self, stage, hsTM):
        sc, C = Sched(self.C), self.C
        for dst, src in [(self.ident, self.d_ident), (self.identB, self.d_identB),
                         (self.identF, self.d_identF),
                         (self.maskL, self.d_maskL), (self.maskU, self.d_maskU),
                         (self.maskDW, self.d_maskDW), (self.ones16, self.d_ones16),
                         (self.convw, self.d_convw), (self.msel, self.d_msel),
                         (self.mc, self.d_mc), (self.mcn, self.d_mcn)]:
            sc.dma("sync", dst[:, :], src.ap())
        sc.dma("sync", self.dtb[:, 0], self.d_dtb.ap())
        sc.dma("sync", self.alog[:, 0], self.d_alog.ap())
        sc.dma("sync", self.nwrep[:, 0], self.d_nwrep.ap())
        cdone = sc.cnt("sd")
        sc.emit("scalar", lambda e: e.activation(self.negea[:, :], self.alog[:, :],
                                                 AF.Exp),
                wait=[("sd", cdone)], inc=[("ss", 1)])
        sc.emit("vector", lambda e: e.tensor_scalar_mul(self.negea[:, :],
                                                        self.negea[:, :], -1.0),
                wait=[("ss", sc.cnt("ss"))], inc=[("sv", 1)])
        sc.emit("vector", lambda e: e.memset(self.S_send[:, :], 0.0),
                inc=[("sv", 1)])
        for i in range(NTT):
            sc.emit("vector", (lambda e, i=i: e.memset(self.ob[i][:, :], 0.0)),
                    inc=[("sv", 1)])

        sc.dma("sync", hsTM[0][0:3, :], self.d_hs.ap()[0:3, :])
        halo_sd = sc.cnt("sd")
        for f in range(8):
            pt = self.tbk[f % 2]
            w = [("sd", halo_sd), ("sd", cdone)] if f < 2 else \
                [("sv", C[f"_hfe{f - 2}"])]
            sc.emit("tensor",
                    (lambda e, f=f, pt=pt: e.transpose(
                        pt[:, 0:3], hsTM[0][0:3, f * 128:(f + 1) * 128],
                        self.identF[0:3, 0:3])),
                    wait=w, inc=[("st", 1)])
            sc.emit("vector",
                    (lambda e, f=f, pt=pt: e.tensor_copy(
                        self.hsFM[f][:, 0:3], pt[:, 0:3])),
                    wait=[("st", sc.cnt("st"))], inc=[("sv", 1)])
            C[f"_hfe{f}"] = sc.cnt("sv")
        halo_st = sc.cnt("st")
        for t in range(NTT):
            buf = hsTM[t % 2]
            w = [("st", C[f"_hst{t - 2}"])] if t >= 2 else \
                ([("st", halo_st)] if t == 0 else ())
            sc.dma("sync", buf[:, :],
                   self.d_hs.ap()[HLO + t * 128: HLO + (t + 1) * 128, :], wait=w)
            sd_here = sc.cnt("sd")
            for f in range(8):
                pt = self.tbk[(t * 8 + f) % 2]
                sc.emit("tensor",
                        (lambda e, f=f, buf=buf, pt=pt: e.transpose(
                            pt[:, 0:128], buf[:, f * 128:(f + 1) * 128],
                            self.identF[:, :])),
                        wait=[("sd", sd_here), ("sv", sc.cnt("sv") - 1)],
                        inc=[("st", 1)])
                sc.emit("vector",
                        (lambda e, f=f, t=t, pt=pt: e.tensor_copy(
                            self.hsFM[f][:, HLO + t * 128: HLO + (t + 1) * 128],
                            pt[:, 0:128])),
                        wait=[("st", sc.cnt("st"))], inc=[("sv", 1)])
            C[f"_hst{t}"] = sc.cnt("st")
        sc.realize(self.nc, self.sems, "p0")

    def phase1_mixed(self, stage, wq, mixr, yacc, yacc2):
        sc, C = Sched(self.C), self.C
        NT = [(0, 512), (512, 512), (1024, 512), (1536, 512), (2048, HLO)]
        for oc in range(16):
            wbuf = wq[0]
            for k in range(8):
                w = []
                if k == 0:
                    w = [("ss", C.get(f"_wqc{oc - 1}", 0)),
                         ("st", C.get(f"_mmq{oc - 1}", 0))]
                sc.dma("sync", stage[:, k * 128:(k + 1) * 128],
                       self.d_wqkv.ap()[k * 128:(k + 1) * 128,
                                        oc * 128:(oc + 1) * 128], wait=w)
                sc.emit("scalar",
                        (lambda e, k=k, wbuf=wbuf: e.copy(
                            wbuf[:, k * 128:(k + 1) * 128],
                            stage[:, k * 128:(k + 1) * 128])),
                        wait=([("sd", sc.cnt("sd"))] +
                              ([("st", C.get(f"_mmq{oc - 1}", 0))] if k == 0 else [])),
                        inc=[("ss", 1)])
            C[f"_wqc{oc}"] = sc.cnt("ss")
            wdone_ss = sc.cnt("ss")
            mbuf = mixr[0]
            for nt in range(5):
                bank = self.pb[nt % 2]
                c0, cn = NT[nt]
                for k in range(8):
                    w = []
                    if k == 0:
                        w = [("ss", wdone_ss),
                             ("sv", C.get(f"_mev{oc}_{nt - 2}", 0)),
                             ("sv", C.get(f"_cv{oc - 1}", 0))]
                    sc.emit("tensor",
                            (lambda e, k=k, wbuf=wbuf, bank=bank, c0=c0, cn=cn:
                             e.matmul(bank[:, 0:cn],
                                      wbuf[:, k * 128:(k + 1) * 128],
                                      self.hsFM[k][:, c0:c0 + cn],
                                      start=(k == 0), stop=(k == 7))),
                            wait=w, inc=([("st", 1)] if k == 7 else ()))
                sc.emit("vector",
                        (lambda e, bank=bank, mbuf=mbuf, c0=c0, cn=cn:
                         e.tensor_copy(mbuf[:, c0:c0 + cn], bank[:, 0:cn])),
                        wait=[("st", sc.cnt("st"))], inc=[("sv", 1)])
                C[f"_mev{oc}_{nt}"] = sc.cnt("sv")
            C[f"_mmq{oc}"] = sc.cnt("st")
            ceng, csem = "vector", "sv"
            yac = yacc if oc % 2 == 0 else yacc2
            for hf in range(2):
                h0 = hf * 1024
                sc.emit(ceng,
                        (lambda e, mbuf=mbuf, oc=oc, h0=h0, yac=yac:
                         e.tensor_scalar(
                            yac[:, :], mbuf[:, h0:h0 + 1024],
                            self.convw[:, 4 * oc:4 * oc + 1], None, ALU.mult)),
                        wait=[("ss", C.get(f"_silu{oc}_{hf - 1}",
                                           C.get(f"_silu{oc - 2}_1", 0)))],
                        inc=[(csem, 1)])
                for j in range(1, 4):
                    sc.emit(ceng,
                            (lambda e, mbuf=mbuf, oc=oc, j=j, h0=h0, yac=yac:
                             e.scalar_tensor_tensor(
                                 yac[:, :], mbuf[:, h0 + j:h0 + j + 1024],
                                 self.convw[:, 4 * oc + j:4 * oc + j + 1],
                                 yac[:, :], ALU.mult, ALU.add)),
                            inc=[(csem, 1)])
                C[f"_cv{oc}"] = sc.cnt(csem)
                sc.emit("scalar",
                        (lambda e, oc=oc, h0=h0, yac=yac: e.activation(
                            self.yb[oc][:, h0:h0 + 1024], yac[:, :], AF.Silu)),
                        wait=[(csem, sc.cnt(csem))], inc=[("ss", 1)])
                C[f"_silu{oc}_{hf}"] = sc.cnt("ss")
        sc.realize(self.nc, self.sems, "p1m")

    def phase1_z_ab(self, stage, wz, wab, sA, sB, sC, sD, sE, sE2):
        sc, C = Sched(self.C), self.C
        for k in range(8):
            sc.dma("sync", stage[:, 1024:1056],
                   self.d_wab.ap()[k * 128:(k + 1) * 128, :],
                   wait=[("sv", sc.cnt("sv"))])
            sc.emit("vector",
                    (lambda e, k=k: e.tensor_copy(
                        wab[:, k * 32:(k + 1) * 32], stage[:, 1024:1056])),
                    wait=[("sd", sc.cnt("sd"))], inc=[("sv", 1)])

        for hf in range(2):
            for k in range(8):
                w = [("sv", sc.cnt("sv"))] if k > 0 else \
                    [("st", C.get("_zmm", 0))]
                sc.dma("sync", stage[:, 0:512],
                       self.d_wz.ap()[k * 128:(k + 1) * 128,
                                      hf * 512:hf * 512 + 512], wait=w)
                sc.emit("vector",
                        (lambda e, k=k: e.tensor_copy(
                            wz[:, k * 512:(k + 1) * 512], stage[:, 0:512])),
                        wait=([("sd", sc.cnt("sd"))] +
                              ([("st", C.get("_zmm", 0))] if k == 0 else [])),
                        inc=[("sv", 1)])
            wdone = sc.cnt("sv")
            for tt in range(NTT):
                bank = self.pb[tt % 2]
                for k in range(8):
                    w = []
                    if k == 0:
                        w = [("sv", wdone),
                             ("ss", C.get(f"_zev{hf}_{tt - 2}", 0))]
                    sc.emit("tensor",
                            (lambda e, k=k, tt=tt, bank=bank:
                             e.matmul(bank[:, 0:512],
                                      self.hsFM[k][:, HLO + tt * 128:
                                                   HLO + (tt + 1) * 128],
                                      wz[:, k * 512:(k + 1) * 512],
                                      start=(k == 0), stop=(k == 7))),
                            wait=w, inc=([("st", 1)] if k == 7 else ()))
                sc.emit("scalar",
                        (lambda e, tt=tt, hf=hf, bank=bank: e.activation(
                            self.zs[tt][:, hf * 512:hf * 512 + 512],
                            bank[:, 0:512], AF.Silu)),
                        wait=[("st", sc.cnt("st"))], inc=[("ss", 1)])
                C[f"_zev{hf}_{tt}"] = sc.cnt("ss")
            C["_zmm"] = sc.cnt("st")

        for nt in range(4):
            c0 = nt * 512
            for k in range(8):
                w = []
                if k == 0:
                    w = [("ss", C.get(f"_abev{nt - 1}", 0))]
                sc.emit("tensor",
                        (lambda e, k=k, c0=c0: e.matmul(
                            self.pb[4][0:16, 0:512],
                            wab[:, k * 32:k * 32 + 16],
                            self.hsFM[k][:, HLO + c0: HLO + c0 + 512],
                            start=(k == 0), stop=(k == 7))),
                        wait=w, inc=([("st", 1)] if k == 7 else ()))
            for k in range(8):
                sc.emit("tensor",
                        (lambda e, k=k, c0=c0: e.matmul(
                            self.pb[5][0:16, 0:512],
                            wab[:, k * 32 + 16:k * 32 + 32],
                            self.hsFM[k][:, HLO + c0: HLO + c0 + 512],
                            start=(k == 0), stop=(k == 7))),
                        inc=([("st", 1)] if k == 7 else ()))
            st_here = sc.cnt("st")
            sc.emit("scalar",
                    (lambda e: e.activation(
                        sA[:, :], self.pb[4][0:16, 0:512], AF.Identity,
                        bias=self.dtb[:, 0:1])),
                    wait=[("st", st_here),
                          ("sv", C.get(f"_pq{nt - 1}", 0)),
                          ("sd2", C.get(f"_pqd{nt - 1}", 0))],
                    inc=[("ss", 1)])
            sc.emit("scalar",
                    (lambda e: e.activation(
                        sB[:, :], self.pb[5][0:16, 0:512], AF.Sigmoid)),
                    inc=[("ss", 1)])
            ab_s = sc.cnt("ss")
            sc.emit("scalar", lambda e: e.activation(sC[:, :], sA[:, :], AF.Abs),
                    inc=[("ss", 1)])
            sc.emit("scalar", lambda e: e.activation(sC[:, :], sC[:, :], AF.Exp,
                                                     scale=-1.0), inc=[("ss", 1)])
            sc.emit("scalar", lambda e: e.activation(sC[:, :], sC[:, :], AF.Ln,
                                                     bias=1.0), inc=[("ss", 1)])
            sp_s = sc.cnt("ss")
            sc.emit("vector", lambda e: e.tensor_scalar_max(sD[:, :], sA[:, :],
                                                            0.0),
                    wait=[("ss", sp_s)], inc=[("sv", 1)])
            sc.emit("vector", lambda e: e.tensor_add(sD[:, :], sD[:, :],
                                                     sC[:, :]),
                    inc=[("sv", 1)])
            sc.emit("vector", lambda e: e.tensor_scalar_mul(
                sD[:, :], sD[:, :], self.negea[:, 0:1]), inc=[("sv", 1)])
            for q in range(4):
                sc.emit("vector",
                        (lambda e, q=q: e.tensor_tensor_scan(
                            sA[:, q * CKL:(q + 1) * CKL], self.ones16[:, :],
                            sD[:, q * CKL:(q + 1) * CKL], 0.0,
                            ALU.mult, ALU.add)),
                        inc=[("sv", 1)])
            c_v = sc.cnt("sv")
            sc.emit("scalar", lambda e: e.activation(sC[:, :], sB[:, :], AF.Ln),
                    wait=[("sv", c_v)], inc=[("ss", 1)])
            sc.emit("vector", lambda e: e.tensor_add(sC[:, :], sA[:, :],
                                                     sC[:, :]),
                    wait=[("ss", sc.cnt("ss"))], inc=[("sv", 1)])
            sc.dma("scalar", self.d_cb.ap()[:, c0:c0 + 512], sC[:, :],
                   sem="sd2", wait=[("sv", sc.cnt("sv"))])
            for q in range(4):
                ck = nt * 4 + q
                ce = q * CKL + CKL - 1
                sc.dma("scalar", self.d_gc.ap()[ck * 16:(ck + 1) * 16],
                       sA[:, ce:ce + 1], sem="sd2")
            cb_sd2 = sc.cnt("sd2")
            sc.emit("scalar", lambda e: e.activation(sD[:, :], sA[:, :], AF.Exp),
                    inc=[("ss", 1)])
            sc.emit("vector", lambda e: e.tensor_mul(sC[:, :], sB[:, :],
                                                     sD[:, :]),
                    wait=[("ss", sc.cnt("ss")), ("sd2", cb_sd2)],
                    inc=[("sv", 1)])
            bands_v, bands_s = sc.cnt("sv"), sc.cnt("ss")
            for q in range(4):
                ck = nt * 4 + q
                cs = slice(q * CKL, (q + 1) * CKL)
                ce = q * CKL + CKL - 1
                sc.emit("vector",
                        (lambda e, cs=cs: e.tensor_scalar_mul(
                            sE[:, :], sB[:, cs], 8.0)),
                        wait=[("sv", bands_v), ("ss", bands_s),
                              ("st", C.get(f"_ptt{ck - 2}", 0))],
                        inc=[("sv", 1)])
                sc.emit("vector",
                        (lambda e: e.reciprocal(sE[:, :], sE[:, :])),
                        wait=[("sv", sc.cnt("sv"))], inc=[("sv", 1)])
                sc.emit("vector",
                        (lambda e, cs=cs, ce=ce: e.tensor_scalar(
                            sE2[:, :], sA[:, cs], sA[:, ce:ce + 1], None,
                            ALU.subtract)),
                        inc=[("sv", 1)])
                sc.emit("scalar",
                        (lambda e: e.activation(sE2[:, :], sE2[:, :], AF.Exp,
                                                scale=-1.0)),
                        wait=[("sv", sc.cnt("sv"))], inc=[("ss", 1)])
                small_v, small_s = sc.cnt("sv"), sc.cnt("ss")
                srcs = [(sA[:, cs], "negc", -1.0), (sB[:, cs], "beta", 1.0),
                        (sC[:, cs], "nbg", -1.0), (sD[:, cs], "g8", 0.125),
                        (sE[:, :], "r8", 1.0), (sE2[:, :], "kap", 1.0)]
                pt = self.tbk[0]
                for i, (src, band, scale) in enumerate(srcs):
                    sc.emit("tensor",
                            (lambda e, src=src, i=i, pt=pt: e.transpose(
                                pt[:, i * 16:(i + 1) * 16], src,
                                self.identF[0:16, 0:16])),
                            wait=([("sv", small_v), ("ss", small_s),
                                   ("sv", C.get(f"_pte{ck - 1}", 0))]
                                  if i == 0 else ()),
                            inc=([("st", 1)] if i == 5 else ()))
                C[f"_ptt{ck}"] = sc.cnt("st")
                for i, (src, band, scale) in enumerate(srcs):
                    col = 112 * ck + PT_OFF[band]
                    sc.emit("vector",
                            (lambda e, col=col, i=i, scale=scale, pt=pt:
                             e.tensor_scalar_mul(
                                 self.paramT[:, col:col + 16],
                                 pt[:, i * 16:(i + 1) * 16], scale)),
                            wait=([("st", C[f"_ptt{ck}"])] if i == 0 else ()),
                            inc=[("sv", 1)])
                C[f"_pte{ck}"] = sc.cnt("sv")
            C[f"_pq{nt}"] = sc.cnt("sv")
            C[f"_pqd{nt}"] = sc.cnt("sd2")
        sc.dma("scalar", self.gcrep[:, :],
               bass.AP(self.d_gc, 0, [[0, 128], [1, NCK * 16]]), sem="sd2",
               wait=[("sd2", sc.cnt("sd2"))])
        sc.emit("scalar", lambda e: e.activation(self.gcrep[:, :],
                                                 self.gcrep[:, :], AF.Exp),
                wait=[("sd2", sc.cnt("sd2"))], inc=[("ss", 1)])
        if "paramT" in self.taps:
            sc.dma("sync", self.taps["paramT"].ap(), self.paramT[:, :],
                   wait=[("sv", sc.cnt("sv")), ("ss", sc.cnt("ss"))])
        if "zraw" in self.taps:
            sc.dma("sync", self.taps["zraw"].ap(),
                   self.zs[0].bitcast(F32)[:, 0:512],
                   wait=[("sv", sc.cnt("sv")), ("ss", sc.cnt("ss"))])
        sc.realize(self.nc, self.sems, "p1z")

    def phase2_scan(self, t):
        sc, C = Sched(self.C), self.C
        pb = self.pb

        def kslice(kh, ck):
            tile, r0 = self.yb[4 + kh // 2], 64 * (kh % 2)
            return tile[r0:r0 + 64, ck * CKL:(ck + 1) * CKL]

        def qslice(kh, ck):
            tile, r0 = self.yb[0 + kh // 2], 64 * (kh % 2)
            return tile[r0:r0 + 64, ck * CKL:(ck + 1) * CKL]

        def vslice(vh, ck):
            tile, r0 = self.yb[8 + vh // 2], 64 * (vh % 2)
            return tile[r0:r0 + 64, ck * CKL:(ck + 1) * CKL]

        def wslot(ck, vh, buf, width):
            i = (ck % WIN) * NVH + vh
            return buf[:, i * width:(i + 1) * width]

        for wk in range(NCK // WIN):
            cks = [wk * WIN + i for i in range(WIN)]
            first_v_gate = [("st", C.get("_p2st", 0)),
                            ("sd", C.get("_spill_sd", 0))]
            for hp in range(NKH):
                kh = hp
                r0 = 64 * (kh % 2)
                for j, vh in enumerate((2 * hp, 2 * hp + 1)):
                    sc.dma("scalar", t["cbb"][j][:, :],
                           bass.AP(self.d_cb, vh * SEG + cks[0] * CKL,
                                   [[0, 128], [1, WIN * CKL]]),
                           sem="sd2",
                           wait=[("sv", C.get(f"_dwv{hp - 1}_{j}", 0))])
                    sc.emit("vector",
                            (lambda e, j=j: e.tensor_add(
                                t["dw"][j][:, :], t["cbb"][j][:, :],
                                self.maskDW[:, :])),
                            wait=(first_v_gate if (hp == 0 and j == 0) else []) +
                                 [("sd2", sc.cnt("sd2")),
                                  ("ss", C.get(f"_eps{hp - 1}_{j}", 0))],
                            inc=[("sv", 1)])
                    C[f"_dwv{hp}_{j}"] = sc.cnt("sv")
                dw_v = sc.cnt("sv")
                for wi, ck in enumerate(cks):
                    aw = [("sv", C.get("_pfree", 0))]
                    sc.emit("tensor",
                            (lambda e, kh=kh, ck=ck: e.matmul(
                                pb[0][:, 0:128], kslice(kh, ck), kslice(kh, ck),
                                start=True, stop=True)),
                            wait=aw, inc=[("st", 1)])
                    sc.emit("tensor",
                            (lambda e, kh=kh, ck=ck: e.matmul(
                                pb[1][:, 0:128], kslice(kh, ck), qslice(kh, ck),
                                start=True, stop=True)),
                            inc=[("st", 1)])
                    sc.emit("tensor",
                            (lambda e, kh=kh, ck=ck, r0=r0: e.transpose(
                                self.tbk[0].bitcast(BF16)[:, 0:64],
                                kslice(kh, ck),
                                self.identB[r0:r0 + 64, 0:64])),
                            inc=[("st", 1)])
                    a_st = sc.cnt("st")
                    sc.emit("vector",
                            (lambda e: e.tensor_tensor(
                                t["g1s"][:, :], pb[0][:, 0:128],
                                self.maskU[:, :], ALU.mult)),
                            wait=[("st", a_st)], inc=[("sv", 1)])
                    sc.emit("vector",
                            (lambda e: e.tensor_copy(t["aqs"][:, :],
                                                     pb[1][:, 0:128])),
                            inc=[("sv", 1)])
                    for j, vh in enumerate((2 * hp, 2 * hp + 1)):
                        sc.emit("tensor",
                                (lambda e, vh=vh, ck=ck, j=j: e.transpose(
                                    self.tbk[1].bitcast(BF16)[:, j * 64:(j + 1) * 64],
                                    vslice(vh, ck),
                                    self.identB[64 * (vh % 2):64 * (vh % 2) + 64,
                                                0:64])),
                                wait=([("sv", C.get("_vtf", 0))] if j == 0 else ()),
                                inc=[("st", 1)])
                    vt_st = sc.cnt("st")
                    ep_s = {}
                    for j, vh in enumerate((2 * hp, 2 * hp + 1)):
                        sc.emit("scalar",
                                (lambda e, wi=wi, ck=ck, vh=vh, j=j:
                                 e.activation(
                                     t["ep"][j][:, :],
                                     t["dw"][j][:, wi * CKL:(wi + 1) * CKL],
                                     AF.Exp, bias=self.ptcol(ck, "negc", vh))),
                                wait=[("sv", dw_v),
                                      ("sv", C.get(f"_epf{j}", 0))],
                                inc=[("ss", 1)])
                        ep_s[j] = sc.cnt("ss")
                        C[f"_eps{hp}_{j}"] = ep_s[j]
                        sc.emit("vector",
                                (lambda e, ck=ck, vh=vh, j=j:
                                 e.scalar_tensor_tensor(
                                    t["chN"][2 * j][:, :], pb[0][:, 0:128],
                                    self.ptcol(ck, "beta", vh),
                                    self.maskL[:, :], ALU.mult, ALU.mult)),
                                wait=[("st", a_st),
                                      ("st", C.get("_chfree_st", 0))],
                                inc=[("sv", 1)])
                        sc.emit("vector",
                                (lambda e, ck=ck, vh=vh: e.tensor_scalar(
                                    wslot(ck, vh, t["ktk"], 64),
                                    self.tbk[0].bitcast(BF16)[:, 0:64],
                                    self.ptcol(ck, "kap", vh), None, ALU.mult)),
                                inc=[("sv", 1)])
                        sc.emit("vector",
                                (lambda e, ck=ck, vh=vh, j=j: e.tensor_scalar(
                                    wslot(ck, vh, t["vtb"], 64),
                                    self.tbk[1].bitcast(BF16)[:, j * 64:(j + 1) * 64],
                                    self.ptcol(ck, "beta", vh), None, ALU.mult)),
                                wait=([("st", vt_st)] if j == 0 else ()),
                                inc=[("sv", 1)])
                    C["_vtf"] = sc.cnt("sv")
                    bld_v = sc.cnt("sv")
                    bld_ss = sc.cnt("ss")

                    CB = {0: (pb[2], pb[3], pb[4]), 1: (pb[1], pb[5], pb[0])}
                    for j in (0, 1):
                        bN, bS, bH = CB[j]
                        sc.emit("tensor",
                                (lambda e, j=j, bN=bN: e.matmul(
                                    bN[:, 0:128], t["g1s"][:, :],
                                    t["chN"][2 * j][:, :],
                                    start=True, stop=True)),
                                wait=[("sv", bld_v), ("ss", bld_ss)],
                                inc=[("st", 1)])
                        sc.emit("tensor",
                                (lambda e, j=j, bS=bS: e.matmul(
                                    bS[:, 0:128], t["chN"][2 * j][:, :],
                                    t["g1s"][:, :], start=True, stop=True)),
                                inc=[("st", 1)])
                        C[f"_cl1{j}"] = sc.cnt("st")
                    for j in (0, 1):
                        bN, bS, bH = CB[j]
                        vh = 2 * hp + j
                        sc.emit("vector",
                                (lambda e, ck=ck, vh=vh, j=j, bN=bN:
                                 e.tensor_scalar(
                                    t["chN"][2 * j + 1][:, :], bN[:, 0:128],
                                    self.ptcol(ck, "beta", vh), None, ALU.mult)),
                                wait=[("st", C[f"_cl1{j}"])], inc=[("sv", 1)])
                        sc.emit("vector",
                                (lambda e, j=j, bS=bS: e.tensor_copy(
                                    t["chS"][2 * j + 1][:, :], bS[:, 0:128])),
                                inc=[("sv", 1)])
                        sc.emit("vector",
                                (lambda e, j=j, bS=bS: e.tensor_sub(
                                    t["chH"][2 * j + 1][:, :], bS[:, 0:128],
                                    t["g1s"][:, :])),
                                inc=[("sv", 1)])
                        C[f"_cle{j}"] = sc.cnt("sv")
                        C[f"_cles{j}"] = sc.cnt("ss")
                    cur = {0: 1, 1: 1}
                    for m in (2, 4, 8, 16, 32):
                        for j in (0, 1):
                            bN, bS, bH = CB[j]
                            cu = cur[j]
                            sc.emit("tensor",
                                    (lambda e, j=j, cu=cu, bN=bN: e.matmul(
                                        bN[:, 0:128], t["chS"][2 * j + cu][:, :],
                                        t["chN"][2 * j + cu][:, :],
                                        start=True, stop=True)),
                                    wait=[("sv", C[f"_cle{j}"])],
                                    inc=[("st", 1)])
                            sc.emit("tensor",
                                    (lambda e, j=j, cu=cu, bS=bS: e.matmul(
                                        bS[:, 0:128], t["chN"][2 * j + cu][:, :],
                                        t["chS"][2 * j + cu][:, :],
                                        start=True, stop=True)),
                                    inc=[("st", 1)])
                            sc.emit("tensor",
                                    (lambda e, j=j, cu=cu, bH=bH: e.matmul(
                                        bH[:, 0:128], t["chN"][2 * j + cu][:, :],
                                        t["chH"][2 * j + cu][:, :],
                                        start=True, stop=True)),
                                    inc=[("st", 1)])
                            C[f"_clm{j}"] = sc.cnt("st")
                        for j in (0, 1):
                            bN, bS, bH = CB[j]
                            cu = cur[j]
                            nx = 1 - cu
                            vh = 2 * hp + j
                            sc.emit("vector",
                                    (lambda e, ck=ck, vh=vh, j=j, nx=nx, bN=bN:
                                     e.tensor_scalar(
                                        t["chN"][2 * j + nx][:, :], bN[:, 0:128],
                                        self.ptcol(ck, "beta", vh), None,
                                        ALU.mult)),
                                    wait=[("st", C[f"_clm{j}"])], inc=[("sv", 1)])
                            sc.emit("vector",
                                    (lambda e, j=j, nx=nx, bS=bS: e.tensor_copy(
                                        t["chS"][2 * j + nx][:, :], bS[:, 0:128])),
                                    inc=[("sv", 1)])
                            sc.emit("vector",
                                    (lambda e, j=j, cu=cu, nx=nx, bH=bH:
                                     e.tensor_add(
                                        t["chH"][2 * j + nx][:, :], bH[:, 0:128],
                                        t["chH"][2 * j + cu][:, :])),
                                    inc=[("sv", 1)])
                            C[f"_cle{j}"] = sc.cnt("sv")
                            cur[j] = nx
                    for j in (0, 1):
                        bN, bS, bH = CB[j]
                        cu = cur[j]
                        sc.emit("tensor",
                                (lambda e, j=j, cu=cu, bH=bH: e.matmul(
                                    bH[:, 0:128], t["chN"][2 * j + cu][:, :],
                                    t["chH"][2 * j + cu][:, :],
                                    start=True, stop=True)),
                                wait=[("sv", C[f"_cle{j}"])],
                                inc=[("st", 1)])
                        C[f"_clf{j}"] = sc.cnt("st")
                    C["_chfree_st"] = sc.cnt("st")
                    for j in (0, 1):
                        bN, bS, bH = CB[j]
                        cu = cur[j]
                        vh = 2 * hp + j
                        sc.emit("vector",
                                (lambda e, j=j, cu=cu, bH=bH: e.tensor_add(
                                    t["chH"][2 * j + (1 - cu)][:, :],
                                    bH[:, 0:128],
                                    t["chH"][2 * j + cu][:, :])),
                                wait=[("st", C[f"_clf{j}"])], inc=[("sv", 1)])
                        sc.emit("vector",
                                (lambda e, ck=ck, vh=vh, j=j, cu=cu:
                                 e.tensor_tensor(
                                     wslot(ck, vh, t["tst"], 128),
                                     t["ep"][j][:, :],
                                     t["chH"][2 * j + (1 - cu)][:, :],
                                     ALU.mult)),
                                wait=[("ss", ep_s[j]), ("sv", sc.cnt("sv"))],
                                inc=[("sv", 1)])
                        sc.emit("vector",
                                (lambda e, ck=ck, vh=vh, j=j: e.tensor_tensor(
                                    wslot(ck, vh, t["pst"], 128),
                                    t["ep"][j][:, :],
                                    t["aqs"][:, :], ALU.mult)),
                                inc=[("sv", 1)])
                        C[f"_epf{j}"] = sc.cnt("sv")
                    C["_pfree"] = sc.cnt("sv")

            spill_gate = [("sv", sc.cnt("sv"))]
            for buf, dscr, w in ((t["tst"], self.d_tst, 128),
                                 (t["pst"], self.d_pst, 128),
                                 (t["ktk"], self.d_ktk, 64),
                                 (t["vtb"], self.d_vtb, 64)):
                sc.dma("sync", dscr.ap()[:, wk * WIN * NVH * w:
                                         (wk + 1) * WIN * NVH * w],
                       buf[:, :], wait=spill_gate)
            C["_spill_sd"] = sc.cnt("sd")
            C["_pfree"] = sc.cnt("sv")
            C["_p2st"] = sc.cnt("st")

        sc.realize(self.nc, self.sems, "p2a")
        sc = Sched(self.C)
        for rnd in range(4):
            if rnd > 0:
                sc.dma("scalar", self.d_agin.ap(), self.S_send[:, :],
                       sem="sd2", wait=[("sv", sc.cnt("sv"))])

                def ccfn(e):
                    return e.collective_compute(
                        "AllGather", ALU.bypass,
                        replica_groups=[list(range(8))],
                        ins=[self.d_agin.ap().opt()],
                        outs=[self.d_agout.ap().opt()],
                    )
                sc.emit("gpsimd", ccfn, wait=[("sd2", sc.cnt("sd2"))],
                        inc=[("scc", 1)])
            sc.emit("vector", lambda e: e.memset(self.S_f[:, :], 0.0),
                    wait=[("sv", sc.cnt("sv")), ("st", sc.cnt("st"))],
                    inc=[("sv", 1)])
            if rnd > 0:
                for i in range(8):
                    sc.dma("scalar", self.agbuf[:, :],
                           self.d_agout.ap()[i * 128:(i + 1) * 128, :],
                           sem="sd2",
                           wait=([("scc", sc.cnt("scc"))] if i == 0 else
                                 [("sv", sc.cnt("sv"))]))
                    sc.emit("vector",
                            (lambda e, i=i, rnd=rnd: e.scalar_tensor_tensor(
                                self.S_f[:, :], self.agbuf[:, :],
                                self.msel[:, rnd * 8 + i:rnd * 8 + i + 1],
                                self.S_f[:, :], ALU.mult, ALU.add)),
                            wait=[("sd2", sc.cnt("sd2")),
                                  ("sv", sc.cnt("sv"))],
                            inc=[("sv", 1)])
            sc.emit("scalar", lambda e: e.copy(self.S_b[:, :], self.S_f[:, :]),
                    wait=[("sv", sc.cnt("sv"))], inc=[("ss", 1)])
            C["_rs_v"] = sc.cnt("sv")
            C["_rs_ss"] = sc.cnt("ss")

            for wk in range(NCK // WIN):
                cks = [wk * WIN + i for i in range(WIN)]
                ld_gate = [("st", sc.cnt("st")), ("sv", sc.cnt("sv"))]
                for buf, dscr, w in ((t["tst"], self.d_tst, 128),
                                     (t["pst"], self.d_pst, 128),
                                     (t["ktk"], self.d_ktk, 64),
                                     (t["vtb"], self.d_vtb, 64)):
                    sc.dma("sync", buf[:, :],
                           dscr.ap()[:, wk * WIN * NVH * w:
                                     (wk + 1) * WIN * NVH * w],
                           wait=ld_gate)
                slab_sd = sc.cnt("sd")
                first = [("sd", slab_sd), ("sv", C["_rs_v"]),
                         ("ss", C.get("_rs_ss", 0))]
                for ck in cks:
                    for g in range(NVH // 4):
                        vhs = [4 * g + i for i in range(4)]
                        banks = [(pb + self.tbk)[(4 * g + i) % 8]
                                 for i in range(4)]
                        for i, vh in enumerate(vhs):
                            kh, (r0, cb) = vh // 2, svh(vh)
                            bk = banks[i]
                            sc.emit("tensor",
                                    (lambda e, kh=kh, ck=ck, r0=r0, cb=cb, bk=bk:
                                     e.matmul(bk[:, 0:64], kslice(kh, ck),
                                              self.S_b[r0:r0 + 64, cb:cb + 64],
                                              start=True, stop=True)),
                                    wait=first +
                                         [("ss", C.get(f"_p2bs{ck}_{vh}", 0)),
                                          ("sv", C.get(f"_p2f{(ck - (0 if g >= 2 else 1)) % NCK}_{(vh + 8) % 16}", 0))],
                                    inc=[("st", 1)])
                            first = []
                            C[f"_p2ks{ck}_{vh}"] = sc.cnt("st")
                        for i, vh in enumerate(vhs):
                            bk = banks[i]
                            sc.emit("vector",
                                    (lambda e, ck=ck, vh=vh, i=i, bk=bk:
                                     e.scalar_tensor_tensor(
                                         t["rt"][i][:, :], bk[:, 0:64],
                                         self.ptcol(ck, "nbg", vh),
                                         wslot(ck, vh, t["vtb"], 64),
                                         ALU.mult, ALU.add)),
                                    wait=[("st", C[f"_p2ks{ck}_{vh}"])],
                                    inc=[("sv", 1)])
                            C[f"_p2r{ck}_{vh}"] = sc.cnt("sv")
                        for i, vh in enumerate(vhs):
                            bk = banks[i]
                            sc.emit("tensor",
                                    (lambda e, ck=ck, vh=vh, i=i, bk=bk: e.matmul(
                                        bk[:, 64:128],
                                        wslot(ck, vh, t["tst"], 128),
                                        t["rt"][i][:, :], start=True, stop=True)),
                                    wait=[("sv", C[f"_p2r{ck}_{vh}"])],
                                    inc=[("st", 1)])
                            C[f"_p2w{ck}_{vh}"] = sc.cnt("st")
                        for i, vh in enumerate(vhs):
                            bk = banks[i]
                            sc.emit("vector",
                                    (lambda e, i=i, bk=bk: e.tensor_add(
                                        t["wt"][i][:, :], bk[:, 64:128],
                                        t["rt"][i][:, :])),
                                    wait=[("st", C[f"_p2w{ck}_{vh}"])],
                                    inc=[("sv", 1)])
                            C[f"_p2we{ck}_{vh}"] = sc.cnt("sv")
                        for i, vh in enumerate(vhs):
                            kh, (r0, cb) = vh // 2, svh(vh)
                            bk = banks[i]
                            sc.emit("tensor",
                                    (lambda e, ck=ck, vh=vh, i=i, bk=bk: e.matmul(
                                        bk[:, 128:192],
                                        wslot(ck, vh, t["pst"], 128),
                                        t["wt"][i][:, :], start=True, stop=True)),
                                    wait=[("sv", C[f"_p2we{ck}_{vh}"])],
                                    inc=[("st", 1)])
                            sc.emit("tensor",
                                    (lambda e, kh=kh, ck=ck, r0=r0, cb=cb, bk=bk:
                                     e.matmul(
                                        bk[:, 192:256], qslice(kh, ck),
                                        self.S_b[r0:r0 + 64, cb:cb + 64],
                                        start=True, stop=True)),
                                    inc=[("st", 1)])
                            C[f"_p2o{ck}_{vh}"] = sc.cnt("st")
                        for i, vh in enumerate(vhs):
                            bk = banks[i]
                            sc.emit("vector",
                                    (lambda e, ck=ck, vh=vh, i=i, bk=bk:
                                     e.tensor_scalar(
                                        t["ot"][i][:, :], bk[:, 192:256],
                                        self.ptcol(ck, "g8", vh), None,
                                        ALU.mult)),
                                    wait=[("st", C[f"_p2o{ck}_{vh}"])],
                                    inc=[("sv", 1)])
                            sc.emit("vector",
                                    (lambda e, ck=ck, vh=vh, i=i, bk=bk:
                                     e.scalar_tensor_tensor(
                                        t["ot"][i][:, :], bk[:, 128:192],
                                        self.ptcol(ck, "r8", vh),
                                        t["ot"][i][:, :], ALU.mult, ALU.add)),
                                    wait=[("sv", sc.cnt("sv"))], inc=[("sv", 1)])
                            sc.emit("vector",
                                    (lambda e, ck=ck, vh=vh, i=i, rnd=rnd:
                                     e.scalar_tensor_tensor(
                                        self.ob[ck][:, vh * 64:(vh + 1) * 64],
                                        t["ot"][i][:, :],
                                        self.mc[:, rnd:rnd + 1],
                                        self.ob[ck][:, vh * 64:(vh + 1) * 64],
                                        ALU.mult, ALU.add)),
                                    wait=[("sv", sc.cnt("sv"))], inc=[("sv", 1)])
                            C[f"_p2oe{ck}_{vh}"] = sc.cnt("sv")
                        for i, vh in enumerate(vhs):
                            r0, cb = svh(vh)
                            bk = banks[i]
                            sc.emit("tensor",
                                    (lambda e, ck=ck, vh=vh, i=i, r0=r0, bk=bk:
                                     e.matmul(
                                        bk[r0:r0 + 64, 256:320],
                                        wslot(ck, vh, t["ktk"], 64),
                                        t["wt"][i][:, :], start=True, stop=True,
                                        tile_position=(0, r0))),
                                    wait=[("sv", C[f"_p2oe{ck}_{vh}"])],
                                    inc=[("st", 1)])
                            C[f"_p2s1{ck}_{vh}"] = sc.cnt("st")
                        for i, vh in enumerate(vhs):
                            r0, cb = svh(vh)
                            bk = banks[i]
                            sc.emit("vector",
                                    (lambda e, ck=ck, vh=vh, r0=r0, cb=cb, bk=bk:
                                     e.scalar_tensor_tensor(
                                        self.S_f[r0:r0 + 64, cb:cb + 64],
                                        self.S_f[r0:r0 + 64, cb:cb + 64],
                                        self.gcrep[r0:r0 + 64,
                                                   16 * ck + vh:16 * ck + vh + 1],
                                        bk[r0:r0 + 64, 256:320], ALU.mult,
                                        ALU.add)),
                                    wait=[("st", C[f"_p2s1{ck}_{vh}"])],
                                    inc=[("sv", 1)])
                            sc.emit("scalar",
                                    (lambda e, r0=r0, cb=cb: e.copy(
                                        self.S_b[r0:r0 + 64, cb:cb + 64],
                                        self.S_f[r0:r0 + 64, cb:cb + 64])),
                                    wait=[("sv", sc.cnt("sv"))], inc=[("ss", 1)])
                            C[f"_p2bs{ck + 1}_{vh}"] = sc.cnt("ss")
                            C[f"_p2f{ck}_{vh}"] = sc.cnt("sv")
            sc.emit("vector",
                    (lambda e, rnd=rnd: e.tensor_scalar(
                        self.agbuf[:, :], self.S_f[:, :],
                        self.mc[:, rnd:rnd + 1], None, ALU.mult)),
                    wait=[("sv", sc.cnt("sv"))], inc=[("sv", 1)])
            sc.emit("vector",
                    (lambda e, rnd=rnd: e.scalar_tensor_tensor(
                        self.S_send[:, :], self.S_send[:, :],
                        self.mcn[:, rnd:rnd + 1], self.agbuf[:, :],
                        ALU.mult, ALU.add)),
                    wait=[("sv", sc.cnt("sv"))], inc=[("sv", 1)])

        sc.dma("sync", self.d_send.ap(), self.S_send[:, :],
               wait=[("sv", sc.cnt("sv"))])
        if "sf" in self.taps:
            sc.dma("sync", self.taps["sf"].ap(), self.S_f[:, :],
                   wait=[("sv", sc.cnt("sv"))])
        sc.realize(self.nc, self.sems, "p2")

    def phase3_post(self, p):
        sc, C = Sched(self.C), self.C
        pb = self.pb
        wout, wffn, stage2 = p["wout"], p["wffn"], p["stage2"]
        self.wout_dbg = wout
        x1, htmp, hsq = p["x1"], p["htmp"], p["hsq"]
        hnb, hnF, hsr, sm, rr = p["hnb"], p["hnF"], p["hsr"], p["sm"], p["rr"]

        for k in range(8):
            w = [("sv", sc.cnt("sv"))] if k > 0 else ()
            sc.dma("sync", stage2[:, :],
                   self.d_wout.ap()[k * 128:(k + 1) * 128, :], wait=w)
            sc.emit("vector",
                    (lambda e, k=k: e.tensor_scalar(
                        wout[:, k * H:(k + 1) * H], stage2[:, :],
                        self.nwrep[:, 0:1], None, ALU.mult)),
                    wait=[("sd", sc.cnt("sd"))], inc=[("sv", 1)])
        wo_v = sc.cnt("sv")
        for k in range(8):
            w = [("sv", wo_v)] if k == 0 else [("ss", sc.cnt("ss"))]
            sc.dma("sync", stage2[:, :],
                   self.d_wffn.ap()[k * 128:(k + 1) * 128, :], wait=w)
            sc.emit("scalar",
                    (lambda e, k=k: e.sign(wffn[:, k * H:(k + 1) * H],
                                           stage2[:, :])),
                    wait=[("sd", sc.cnt("sd"))], inc=[("ss", 1)])
        wf_s = sc.cnt("ss")

        htmp2, hsq2, hnF2, sm2 = p["htmp2"], p["hsq2"], p["hnF2"], p["sm2"]

        for it in range(NTT + 1):
            at, bt = it, it - 1
            do_a, do_b = at < NTT, bt >= 0
            xa = x1[at % 4] if do_a else None
            xb = x1[bt % 4] if do_b else None
            if do_a:
                sc.emit("vector",
                        (lambda e, at=at, xa=xa: e.tensor_mul(
                            htmp[:, :], self.ob[at][:, :], self.zs[at][:, :])),
                        wait=[("sv", wo_v), ("ss", C.get(f"_lns{at - 1}", 0))],
                        inc=[("sv", 1)])
                h_v = sc.cnt("sv")
                sc.emit("scalar",
                        (lambda e: e.activation(hsq[:, :], htmp[:, :],
                                                AF.Square)),
                        wait=[("sv", h_v)], inc=[("ss", 1)])
                C[f"_sqs{at}"] = sc.cnt("ss")
                sc.dma("sync", hsr[at % 2][:, :],
                       self.d_hs.ap()[HLO + at * 128: HLO + (at + 1) * 128, :],
                       wait=[("sv", C.get(f"_x1v{at - 2}", 0))])
                hs_sd = sc.cnt("sd")
            if do_b:
                for f4 in range(4):
                    sc.emit("tensor",
                            (lambda e, xb=xb, f4=f4: e.transpose(
                                self.tbk[1][:, f4 * 128:(f4 + 1) * 128],
                                xb[:, f4 * 128:(f4 + 1) * 128],
                                self.identF[:, :])),
                            wait=([("sv", max(C[f"_x1v{bt}"],
                                              C.get("_x1fe", 0)))]
                                  if f4 == 0 else ()),
                            inc=([("st", 1)] if f4 == 3 else ()))
                sc.emit("vector",
                        (lambda e: e.tensor_copy(hnF2[:, 0:512],
                                                 self.tbk[1][:, 0:512])),
                        wait=[("st", sc.cnt("st"))], inc=[("sv", 1)])
                C["_x1fe"] = sc.cnt("sv")
            if do_a:
                sc.emit("vector",
                        (lambda e: e.tensor_reduce(
                            rr[:, :],
                            hsq[:, :].rearrange("p (h e) -> p h e", h=16),
                            AX.X, ALU.add)),
                        wait=[("ss", C[f"_sqs{at}"])], inc=[("sv", 1)])
                sc.emit("vector",
                        (lambda e: e.tensor_scalar(rr[:, :], rr[:, :], 1.0 / DV,
                                                   RMS_EPS, ALU.mult, ALU.add)),
                        wait=[("sv", sc.cnt("sv"))], inc=[("sv", 1)])
                sc.emit("vector", lambda e: e.reciprocal(rr[:, :], rr[:, :]),
                        wait=[("sv", sc.cnt("sv"))], inc=[("sv", 1)])
                sc.emit("scalar",
                        (lambda e: e.activation(rr[:, :], rr[:, :], AF.Sqrt)),
                        wait=[("sv", sc.cnt("sv"))], inc=[("ss", 1)])
                sqrt_s = sc.cnt("ss")
            if do_b:
                for f4 in range(4):
                    sc.emit("tensor",
                            (lambda e, xb=xb, f4=f4: e.transpose(
                                self.tbk[1][:, f4 * 128:(f4 + 1) * 128],
                                xb[:, 512 + f4 * 128:512 + (f4 + 1) * 128],
                                self.identF[:, :])),
                            wait=([("sv", C["_x1fe"])] if f4 == 0 else ()),
                            inc=([("st", 1)] if f4 == 3 else ()))
                sc.emit("vector",
                        (lambda e: e.tensor_copy(hnF2[:, 512:1024],
                                                 self.tbk[1][:, 0:512])),
                        wait=[("st", sc.cnt("st"))], inc=[("sv", 1)])
                C["_x1fe"] = sc.cnt("sv")
                x1f_v = sc.cnt("sv")
            if do_a:
                sc.emit("vector",
                        (lambda e: e.tensor_tensor(
                            hnb[:, :].rearrange("p (h e) -> p h e", h=16),
                            htmp[:, :].rearrange("p (h e) -> p h e", h=16),
                            rr[:, :].rearrange("p (h o) -> p h o", o=1)
                            .broadcast_to([128, 16, DV]), ALU.mult)),
                        wait=[("ss", sqrt_s)], inc=[("sv", 1)])
                hn_v = sc.cnt("sv")
            if do_b:
                for hf in range(2):
                    bank = pb[2 + hf]
                    for k in range(8):
                        w = [("sv", x1f_v), ("ss", wf_s)] if k == 0 else ()
                        sc.emit("tensor",
                                (lambda e, k=k, hf=hf, bank=bank: e.matmul(
                                    bank[:, 0:512],
                                    hnF2[:, k * 128:(k + 1) * 128],
                                    wffn[:, k * H + hf * 512:
                                          k * H + hf * 512 + 512],
                                    start=(k == 0), stop=(k == 7))),
                                wait=w, inc=([("st", 1)] if k == 7 else ()))
                    sc.emit("vector",
                            (lambda e, hf=hf, bank=bank, xb=xb: e.tensor_add(
                                htmp2[:, hf * 512:hf * 512 + 512],
                                bank[:, 0:512],
                                xb[:, hf * 512:hf * 512 + 512])),
                            wait=[("st", sc.cnt("st")),
                                  ("sd", C.get("_o2sd", 0)),
                                  ("ss", C.get("_lns2", 0))],
                            inc=[("sv", 1)])
            if do_a:
                for f in range(8):
                    sc.emit("tensor",
                            (lambda e, f=f: e.transpose(
                                self.tbk[0].bitcast(BF16)[:,
                                                          f * 128:(f + 1) * 128],
                                hnb[:, f * 128:(f + 1) * 128],
                                self.ident[:, :])),
                            wait=([("sv", hn_v)] if f == 0 else ()),
                            inc=([("st", 1)] if f == 7 else ()))
                sc.emit("vector",
                        (lambda e: e.tensor_copy(
                            hnF[:, :], self.tbk[0].bitcast(BF16)[:, 0:1024])),
                        wait=[("st", sc.cnt("st"))], inc=[("sv", 1)])
                hnf_v = sc.cnt("sv")
            if do_b:
                self._layernorm(sc, htmp2, hsq2, sm2)
                C["_lns2"] = sc.cnt("ss")
                sc.dma("sync", self.d_out.ap()[bt * 128:(bt + 1) * 128, :],
                       htmp2[:, :], wait=[("sv", sc.cnt("sv"))])
                C["_o2sd"] = sc.cnt("sd")
            if do_a:
                for hf in range(2):
                    bank = pb[hf]
                    for k in range(8):
                        w = [("sv", hnf_v)] if k == 0 else ()
                        sc.emit("tensor",
                                (lambda e, k=k, hf=hf, bank=bank: e.matmul(
                                    bank[:, 0:512],
                                    hnF[:, k * 128:(k + 1) * 128],
                                    wout[:, k * H + hf * 512:
                                          k * H + hf * 512 + 512],
                                    start=(k == 0), stop=(k == 7))),
                                wait=w, inc=([("st", 1)] if k == 7 else ()))
                    sc.emit("vector",
                            (lambda e, at=at, hf=hf, bank=bank, xa=xa:
                             e.tensor_add(
                                 xa[:, hf * 512:hf * 512 + 512],
                                 bank[:, 0:512],
                                 hsr[at % 2][:, hf * 512:hf * 512 + 512])),
                            wait=[("st", sc.cnt("st")), ("sd", hs_sd)],
                            inc=[("sv", 1)])
                self._layernorm(sc, xa, hsq, sm)
                C[f"_x1v{at}"] = sc.cnt("sv")
                C[f"_lns{at}"] = sc.cnt("ss")
        if "zraw2" in self.taps:
            sc.dma("sync", self.taps["zraw2"].ap(),
                   self.zs[0].bitcast(F32)[:, 0:512],
                   wait=[("sv", sc.cnt("sv"))])
        sc.emit("sync", lambda e: e.wait_ge(self.sems["sd"], sc.cnt("sd")))
        sc.emit("sync", lambda e: e.wait_ge(self.sems["sd2"], sc.cnt("sd2")))
        sc.realize(self.nc, self.sems, "p3")

    def _layernorm(self, sc, xt, scratch, sm):
        xt_v = sc.cnt("sv")
        sc.emit("vector",
                (lambda e: e.tensor_reduce(sm[0][:, :], xt[:, :], AX.X, ALU.add)),
                inc=[("sv", 1)])
        sc.emit("vector",
                (lambda e: e.tensor_scalar_mul(sm[0][:, :], sm[0][:, :], 1.0 / H)),
                wait=[("sv", sc.cnt("sv"))], inc=[("sv", 1)])
        sc.emit("scalar",
                (lambda e: e.activation(scratch[:, :], xt[:, :], AF.Square)),
                wait=[("sv", xt_v)], inc=[("ss", 1)])
        sc.emit("vector",
                (lambda e: e.tensor_reduce(sm[1][:, :], scratch[:, :], AX.X,
                                           ALU.add)),
                wait=[("ss", sc.cnt("ss"))], inc=[("sv", 1)])
        sc.emit("vector",
                (lambda e: e.tensor_scalar_mul(sm[1][:, :], sm[1][:, :], 1.0 / H)),
                wait=[("sv", sc.cnt("sv"))], inc=[("sv", 1)])
        sc.emit("vector",
                (lambda e: e.tensor_mul(sm[2][:, :], sm[0][:, :], sm[0][:, :])),
                wait=[("sv", sc.cnt("sv"))], inc=[("sv", 1)])
        sc.emit("vector",
                (lambda e: e.tensor_sub(sm[1][:, :], sm[1][:, :], sm[2][:, :])),
                wait=[("sv", sc.cnt("sv"))], inc=[("sv", 1)])
        sc.emit("vector",
                (lambda e: e.tensor_scalar_add(sm[1][:, :], sm[1][:, :], LN_EPS)),
                wait=[("sv", sc.cnt("sv"))], inc=[("sv", 1)])
        sc.emit("vector", (lambda e: e.reciprocal(sm[2][:, :], sm[1][:, :])),
                wait=[("sv", sc.cnt("sv"))], inc=[("sv", 1)])
        sc.emit("scalar", (lambda e: e.activation(sm[3][:, :], sm[2][:, :],
                                                  AF.Sqrt)),
                wait=[("sv", sc.cnt("sv"))], inc=[("ss", 1)])
        sc.emit("vector",
                (lambda e: e.tensor_scalar(xt[:, :], xt[:, :], sm[0][:, 0:1],
                                           sm[3][:, 0:1], ALU.subtract,
                                           ALU.mult)),
                wait=[("ss", sc.cnt("ss"))], inc=[("sv", 1)])



_NC_CACHE = {}


def _get_nc():
    if "main" not in _NC_CACHE:
        _NC_CACHE["main"] = Builder().build()
    return _NC_CACHE["main"]


def _consts_maps():
    i0, i1 = np.indices((128, 128))
    ident = np.eye(128, dtype=np.float32).astype(ml_dtypes.bfloat16)
    identB = (np.eye(128) + np.eye(128, k=-64)).astype(np.float32) \
        .astype(ml_dtypes.bfloat16)
    maskL = (i0 > i1).astype(np.float32).astype(ml_dtypes.bfloat16)
    maskU = (i0 < i1).astype(np.float32).astype(ml_dtypes.bfloat16)
    maskD = np.where(i0 <= i1, 0.0, NEG_BIG).astype(np.float32)
    maskDW = np.ascontiguousarray(np.tile(maskD, (1, WIN)))
    ones16 = np.ones((16, CKL), dtype=np.float32)
    identF = np.eye(128, dtype=np.float32)
    return {"ident_bf": ident, "identB_bf": identB, "identF": identF,
            "maskL": maskL, "maskU": maskU, "maskDW": maskDW,
            "ones16": ones16}


def kernel(L0_attn_hidden_states, L0_attn_W_qkv, L0_attn_conv_w, L0_attn_W_z,
           L0_attn_W_b, L0_attn_W_a, L0_attn_dt_bias, L0_attn_A_log,
           L0_attn_norm_weight, L0_attn_W_out, L0_ffn_weight):
    hs = np.ascontiguousarray(L0_attn_hidden_states, dtype=np.float32)
    wqkvT = np.ascontiguousarray(np.asarray(L0_attn_W_qkv, np.float32).T)
    convw = np.asarray(L0_attn_conv_w, np.float32)[:, 0, :]
    convw_p = np.zeros((128, 64), np.float32)
    for oc in range(16):
        convw_p[:, 4 * oc:4 * oc + 4] = convw[oc * 128:(oc + 1) * 128, :]
    wzT = np.ascontiguousarray(np.asarray(L0_attn_W_z, np.float32).T)
    wab = np.concatenate([np.asarray(L0_attn_W_a, np.float32),
                          np.asarray(L0_attn_W_b, np.float32)], axis=0)
    wabT = np.ascontiguousarray(wab.T)
    woutT = np.ascontiguousarray(np.asarray(L0_attn_W_out, np.float32).T)
    wffnT = np.ascontiguousarray(np.asarray(L0_ffn_weight, np.float32).T)
    dtb = np.ascontiguousarray(L0_attn_dt_bias, np.float32)
    alog = np.ascontiguousarray(L0_attn_A_log, np.float32)
    nw_rep = np.ascontiguousarray(
        np.tile(np.asarray(L0_attn_norm_weight, np.float32), 2))
    consts = _consts_maps()

    def shard_hs(core):
        b, q = core // 4, core % 4
        st = q * SEG
        pad = np.zeros((SEGH, H), np.float32)
        lo = max(0, st - HLO)
        pad[HLO - (st - lo):] = hs[b, lo:st + SEG]
        return pad

    base = [dict(hs_pad=shard_hs(c), wqkvT=wqkvT, convw_p=convw_p, wzT=wzT,
                 wabT=wabT, woutT=woutT, wffnT=wffnT, dtb=dtb, alog=alog,
                 nw_rep=nw_rep, **consts) for c in range(8)]

    nc = _get_nc()

    def masks(c):
        msel = np.zeros((128, 32), np.float32)
        r = c % 4
        if r > 0:
            msel[:, r * 8 + (c - 1)] = 1.0
        mc = np.zeros((128, 4), np.float32)
        mc[:, r] = 1.0
        mcn = 1.0 - mc
        return {"msel": msel, "mc": mc, "mcn": mcn}

    in_maps = [dict(base[c], **masks(c)) for c in range(8)]
    res = run_bass_kernel_spmd(nc, in_maps, core_ids=list(range(8)))
    out = np.empty((B, S, H), np.float32)
    for c in range(8):
        b, q = c // 4, c % 4
        out[b, q * SEG:(q + 1) * SEG] = res.results[c]["out_x2"]
    return out
